# version 01
# baseline (speedup 1.0000x reference)
import os
import numpy as np
from contextlib import ExitStack

os.environ.setdefault("NEURON_SCRATCHPAD_PAGE_SIZE", "512")

import concourse.bass as bass
import concourse.mybir as mybir
from concourse.bass_utils import run_bass_kernel_spmd
from concourse.aot_env import allow_ici_all_to_all

F32 = mybir.dt.float32
BF16 = mybir.dt.bfloat16
U8 = mybir.dt.uint8
AF = mybir.ActivationFunctionType
ALU = mybir.AluOpType
AX = mybir.AxisListType

NCORE = 8
TOK = 8192
D = 2048
TPC = 1024
FC = 16
TCH = 8

HD0 = 256
NE, EPC, DI, MCI = 32, 4, 768, 6
H1, D1, BLK, NB = 16, 128, 256, 32
HID1, HC1 = 5632, 44

LAMBDA_INIT = 0.8
RMS_EPS = 1.1920929e-07
L1_EPS = 1e-6
LN_EPS = 1e-5
INV_SQRT_HALF0 = float(1.0 / np.sqrt(128.0))

GW_TILE = 128 * 2048
DW_TILE = 128 * 768
EXP_ELEMS = MCI * GW_TILE
GW_BASE = 0
UW_BASE = EPC * EXP_ELEMS
DW_BASE = 2 * EPC * EXP_ELEMS
RANK_ELEMS = 3 * EPC * EXP_ELEMS
AGW_ROWS = RANK_ELEMS // 1024



def _pmaj(a):
    K_, M = a.shape
    return np.ascontiguousarray(a.reshape(K_ // 128, 128, M).transpose(1, 0, 2).reshape(128, -1))


def _colblocks(a, bs=128):
    K_, M = a.shape
    kc = K_ // 128
    nb = M // bs
    return np.ascontiguousarray(a.reshape(kc, 128, nb, bs).transpose(2, 1, 0, 3).reshape(nb, 128, kc * bs))


def _u8view(w_int32):
    w = np.ascontiguousarray(np.asarray(w_int32))
    assert w.dtype == np.int32
    return np.ascontiguousarray(w.view(np.uint8).reshape(*w.shape, 4)[..., 0])


def make_host_inputs(kw):
    f = np.float32
    x = np.ascontiguousarray(np.asarray(kw["L0_attn_x"], f).reshape(TOK, D))
    wq = np.asarray(kw["L0_attn_wq"], f)
    wk = np.asarray(kw["L0_attn_wk"], f)
    wv = np.asarray(kw["L0_attn_wv"], f)
    woT_cb = _colblocks(np.ascontiguousarray(np.asarray(kw["L0_attn_wo"], f).T))
    lam = np.concatenate([np.asarray(kw[n], f).reshape(2, 128).T for n in
                          ("L0_attn_lq1", "L0_attn_lk1", "L0_attn_lq2", "L0_attn_lk2")], axis=1)
    lam = np.ascontiguousarray(lam)
    router_pm = _pmaj(np.ascontiguousarray(np.asarray(kw["L0_ffn_router_w"], f).T))

    gw = np.asarray(kw["L0_ffn_gate_w"], f)
    uw = np.asarray(kw["L0_ffn_up_w"], f)
    dw = np.asarray(kw["L0_ffn_down_w"], f)

    wqkvu = np.asarray(kw["L1_attn_W_qkvu"], f)
    slopes = np.asarray(kw["L1_attn_slope_rate"], f).reshape(H1)
    woutT_cb = _colblocks(np.ascontiguousarray(np.asarray(kw["L1_attn_W_out"], f).T))

    qg_cb = _colblocks(np.ascontiguousarray(_u8view(kw["L1_ffn_wq_gate"]).T))
    qu_cb = _colblocks(np.ascontiguousarray(_u8view(kw["L1_ffn_wq_up"]).T))
    qd_cb = _colblocks(np.ascontiguousarray(_u8view(kw["L1_ffn_wq_down"]).T))
    sg = np.ascontiguousarray(np.asarray(kw["L1_ffn_s_gate"], f).reshape(HC1, 128).T)
    su = np.ascontiguousarray(np.asarray(kw["L1_ffn_s_up"], f).reshape(HC1, 128).T)
    sd = np.ascontiguousarray(np.asarray(kw["L1_ffn_s_down"], f).reshape(FC, 128).T)

    ident = np.eye(128, dtype=f)
    masks = np.zeros((128, 4 * 512), f)
    for j in range(4):
        si = np.arange(128)[:, None]
        m = np.arange(512)[None, :]
        masks[:, j * 512:(j + 1) * 512] = (si + 128 * j <= m).astype(f)
    arr_b = np.broadcast_to(np.arange(1, BLK + 1, dtype=f)[None, :], (128, BLK)).copy()
    arr2t = np.stack([255.0 - (np.arange(128) + 128 * j) for j in range(2)], axis=1).astype(f)
    idx2 = np.zeros((128, 2 * BLK), f)
    dmask = np.zeros((128, 2 * BLK), f)
    for j in range(2):
        si = np.arange(128)[:, None] + 128 * j
        m = np.arange(BLK)[None, :]
        v = (m - si).astype(f)
        dmask[:, j * BLK:(j + 1) * BLK] = (v >= 0).astype(f)
        v[v < 0] = 0.0
        idx2[:, j * BLK:(j + 1) * BLK] = v
    c256 = np.full((128, 1), 256.0, f)

    in_maps = []
    for c in range(NCORE):
        mm = {}
        mm["x_slice"] = np.ascontiguousarray(x[c * TPC:(c + 1) * TPC, :])
        mm["l0_wq"] = _pmaj(np.ascontiguousarray(wq[c * HD0:(c + 1) * HD0, :].T))
        mm["l0_wk"] = _pmaj(np.ascontiguousarray(wk[c * HD0:(c + 1) * HD0, :].T))
        mm["l0_wv"] = _pmaj(np.ascontiguousarray(wv[c * HD0:(c + 1) * HD0, :].T))
        mm["l0_wo"] = woT_cb
        mm["l0_lam"] = lam
        mm["l0_router"] = router_pm
        parts = []
        for arr3 in (gw, uw, dw):
            for e in range(EPC * c, EPC * (c + 1)):
                parts.append(_colblocks(np.ascontiguousarray(arr3[e].T)).reshape(-1))
        mm["moe_w"] = np.ascontiguousarray(np.concatenate(parts).reshape(AGW_ROWS, 1024))
        rows = [wqkvu[g * D + HD0 * c: g * D + HD0 * (c + 1), :] for g in range(4)]
        mm["l1_wqkvu"] = _pmaj(np.ascontiguousarray(np.concatenate(rows, axis=0).T))
        mm["l1_slopes"] = np.ascontiguousarray(
            np.broadcast_to(slopes[2 * c:2 * c + 2][None, :], (128, 2)).astype(f))
        mm["l1_wout"] = woutT_cb
        mm["l1_qg"] = qg_cb
        mm["l1_qu"] = qu_cb
        mm["l1_qd"] = qd_cb
        mm["l1_sg"] = sg
        mm["l1_su"] = su
        mm["l1_sd"] = sd
        mm["ident"] = ident
        mm["l0_masks"] = masks
        mm["l1_arr"] = arr_b
        mm["l1_arr2t"] = arr2t
        mm["l1_idx"] = idx2
        mm["l1_dmask"] = dmask
        mm["l1_c256"] = c256
        in_maps.append(mm)
    return in_maps



class SemC:
    def __init__(self, es, nc, name):
        self.h = es.enter_context(nc.semaphore(name))
        self.v = 0
        self.emitted = 0
        self.name = name

    def reserve(self, n=1, step=1):
        vals = [self.v + step * (i + 1) for i in range(n)]
        self.v += step * n
        return vals

    def emit(self, inst, step=1):
        self.emitted += step
        inst.then_inc(self.h, step)
        return inst


class K:
    def __init__(self, debug=False):
        nc = bass.Bass("TRN2", target_bir_lowering=False, debug=False)
        self.nc = nc
        self.debug = debug
        es = ExitStack()
        self.es = es
        dt = nc.dram_tensor
        self.in_x = dt("x_slice", [TPC, D], F32, kind="ExternalInput")
        self.in_wq = dt("l0_wq", [128, FC * HD0], F32, kind="ExternalInput")
        self.in_wk = dt("l0_wk", [128, FC * HD0], F32, kind="ExternalInput")
        self.in_wv = dt("l0_wv", [128, FC * HD0], F32, kind="ExternalInput")
        self.in_wo = dt("l0_wo", [FC, 128, D], F32, kind="ExternalInput")
        self.in_lam = dt("l0_lam", [128, 8], F32, kind="ExternalInput")
        self.in_router = dt("l0_router", [128, FC * NE], F32, kind="ExternalInput")
        self.in_moew = dt("moe_w", [AGW_ROWS, 1024], F32, kind="ExternalInput")
        self.in_l1w = dt("l1_wqkvu", [128, FC * 1024], F32, kind="ExternalInput")
        self.in_slopes = dt("l1_slopes", [128, 2], F32, kind="ExternalInput")
        self.in_wout = dt("l1_wout", [FC, 128, D], F32, kind="ExternalInput")
        self.in_qg = dt("l1_qg", [HC1, 128, FC * 128], U8, kind="ExternalInput")
        self.in_qu = dt("l1_qu", [HC1, 128, FC * 128], U8, kind="ExternalInput")
        self.in_qd = dt("l1_qd", [FC, 128, HC1 * 128], U8, kind="ExternalInput")
        self.in_sg = dt("l1_sg", [128, HC1], F32, kind="ExternalInput")
        self.in_su = dt("l1_su", [128, HC1], F32, kind="ExternalInput")
        self.in_sd = dt("l1_sd", [128, FC], F32, kind="ExternalInput")
        self.in_ident = dt("ident", [128, 128], F32, kind="ExternalInput")
        self.in_masks = dt("l0_masks", [128, 4 * 512], F32, kind="ExternalInput")
        self.in_arr = dt("l1_arr", [128, BLK], F32, kind="ExternalInput")
        self.in_arr2t = dt("l1_arr2t", [128, 2], F32, kind="ExternalInput")
        self.in_idx = dt("l1_idx", [128, 2 * BLK], F32, kind="ExternalInput")
        self.in_dmask = dt("l1_dmask", [128, 2 * BLK], F32, kind="ExternalInput")
        self.in_c256 = dt("l1_c256", [128, 1], F32, kind="ExternalInput")

        self.out = dt("out", [TPC, D], F32, kind="ExternalOutput")
        if debug:
            self.dbg = dt("dbg", [D, TPC], F32, kind="ExternalOutput")

        self.agw_in = dt("agw_in", [AGW_ROWS, 1024], BF16)
        self.agw_out = dt("agw_out", [NCORE * AGW_ROWS, 1024], BF16)
        self.ag0_in = dt("ag0_in", [D, TPC], BF16)
        self.ag0_out = dt("ag0_out", [NCORE * D, TPC], BF16)
        self.a2y_in = dt("a2y_in", [NCORE * HD0, TPC], BF16)
        self.a2y_out = dt("a2y_out", [NCORE * HD0, TPC], BF16)
        self.ag2_in = dt("ag2_in", [D, TPC], BF16)
        self.ag2_out = dt("ag2_out", [NCORE * D, TPC], BF16)
        self.a2ou_in = dt("a2ou_in", [NCORE * 512, TPC], BF16)
        self.a2ou_out = dt("a2ou_out", [NCORE * 512, TPC], BF16)
        self.res_park = dt("res_park", [D, TPC], F32)

        self._ch = {}
        self.bar = {e: SemC(es, nc, f"bar_{e}") for e in
                    ("sync", "tensor", "scalar", "vector", "gpsimd")}
        self.cc = {n: SemC(es, nc, f"cc_{n}") for n in
                   ("agw", "ag0", "a2y", "ag2", "a2ou")}

        sb = lambda name, shape, dtype: es.enter_context(nc.sbuf_tensor(name, shape, dtype))
        self.ones_bf = sb("ones_bf", [128, 128], BF16)
        self.ones_f = sb("ones_f", [128, 128], F32)
        self.ident_f = sb("ident_f", [128, 128], F32)
        self.ident_b = sb("ident_b", [128, 128], BF16)
        self.masks_b = sb("masks_b", [128, 4 * 512], BF16)
        self.neglam = sb("neglam", [128, 1], F32)
        self.scrcol = sb("scrcol", [128, 8], F32)
        self.lamt = sb("lamt", [128, 8], F32)
        self.cst = sb("cst", [128, 6], F32)

        self.res = None
        self.xln = None

    def ch(self, name):
        if name not in self._ch:
            self._ch[name] = SemC(self.es, self.nc, f"ch_{name}")
        return self._ch[name]

    def rch(self, fc, sl=slice(None)):
        return self.res[:, fc * TPC:(fc + 1) * TPC][:, sl]

    def xch(self, fc, sl=slice(None)):
        return self.xln[:, fc * TPC:(fc + 1) * TPC][:, sl]

    def vst(self, v, inst):
        c = self.ch("dvx")
        c.v += 1
        c.emitted += 1
        inst.then_inc(c.h, 1)
        v.wait_ge(c.h, c.v)

    def barrier(self):
        nc = self.nc
        tot = {e: self.bar[e].reserve(1)[0] for e in self.bar}
        qd = [(self._ch[n].h, self._ch[n].v) for n in ("qsy", "qsc", "qgp") if n in self._ch]

        def body(engname):
            def f(e):
                for h, v in qd:
                    e.wait_ge(h, v)
                if engname in ("vector", "scalar", "gpsimd"):
                    self.bar[engname].emit(e.drain())
                else:
                    self.bar[engname].emit(e.nop())
                for en, t in tot.items():
                    if en != engname:
                        e.wait_ge(self.bar[en].h, t)
            return f
        with nc.Block() as b:
            b.sync(body("sync"))
            b.tensor(body("tensor"))
            b.scalar(body("scalar"))
            b.vector(body("vector"))
            b.gpsimd(body("gpsimd"))

    def check_channels(self):
        for c in list(self._ch.values()) + list(self.bar.values()) + list(self.cc.values()):
            if c.v != c.emitted:
                raise RuntimeError(f"channel {c.name}: reserved {c.v} != emitted {c.emitted}")



def emit_init(k: K):
    nc = k.nc
    qgp, qsy = k.ch("qgp"), k.ch("qsy")
    c_dv, c_pe, c_ac = k.ch("dv"), k.ch("pe"), k.ch("ac")
    with ExitStack() as ph:
        psl = ph.enter_context(nc.psum_tensor("ps_init", [128, 512], F32))
        e_w = qgp.reserve(3, 16)
        e_sy = qsy.reserve(2, 16)
        e_prod = c_dv.reserve(1)[0]
        e_mm = c_pe.reserve(1)[0]
        e_sum = c_dv.reserve(1)[0]
        e_exp = c_ac.reserve(1)[0]
        e_neg = c_dv.reserve(1)[0]
        with nc.Block() as b:
            @b.gpsimd
            def _(g):
                qgp.emit(g.dma_start(out=k.agw_in.ap(), in_=k.in_moew.ap()), 16)
                qgp.emit(g.dma_start(out=k.ident_b[:, :], in_=k.in_ident.ap()), 16)
                qgp.emit(g.dma_start(out=k.masks_b[:, :], in_=k.in_masks.ap()), 16)

            @b.sync
            def _(sy):
                qsy.emit(sy.dma_start(out=k.ident_f[:, :], in_=k.in_ident.ap()), 16)
                qsy.emit(sy.dma_start(out=k.lamt[:, :], in_=k.in_lam.ap()), 16)

            @b.vector
            def _(v):
                v.memset(k.ones_bf[:, :], 1.0)
                v.memset(k.ones_f[:, :], 1.0)
                v.memset(k.cst[:, 0:1], 1.0 / D)
                v.memset(k.cst[:, 1:2], 1.0 / 128.0)
                v.memset(k.cst[:, 2:3], LN_EPS)
                v.memset(k.cst[:, 3:4], RMS_EPS)
                v.memset(k.cst[:, 4:5], L1_EPS)
                v.memset(k.cst[:, 5:6], INV_SQRT_HALF0)
                v.wait_ge(qsy.h, e_sy[1])
                v.tensor_mul(k.scrcol[:, 0:1], k.lamt[:, 0:1], k.lamt[:, 2:3])
                v.tensor_mul(k.scrcol[:, 1:2], k.lamt[:, 1:2], k.lamt[:, 3:4])
                v.tensor_mul(k.scrcol[:, 2:3], k.lamt[:, 4:5], k.lamt[:, 6:7])
                c_dv.emit(v.tensor_mul(k.scrcol[:, 3:4], k.lamt[:, 5:6], k.lamt[:, 7:8]))

            @b.tensor
            def _(t):
                t.wait_ge(c_dv.h, e_prod)
                c_pe.emit(t.matmul(psl[:, 0:4], k.ones_f[:, :], k.scrcol[:, 0:4],
                                   start=True, stop=True))

            @b.vector
            def _(v):
                v.wait_ge(c_pe.h, e_mm)
                k.vst(v, v.tensor_copy(k.scrcol[:, 0:4], psl[:, 0:4]))
                v.tensor_add(k.scrcol[:, 4:5], k.scrcol[:, 0:1], k.scrcol[:, 1:2])
                c_dv.emit(v.tensor_add(k.scrcol[:, 5:6], k.scrcol[:, 2:3], k.scrcol[:, 3:4]))

            @b.scalar
            def _(s):
                s.wait_ge(c_dv.h, e_sum)
                s.activation(k.scrcol[:, 6:7], k.scrcol[:, 4:5], AF.Exp)
                c_ac.emit(s.activation(k.scrcol[:, 7:8], k.scrcol[:, 5:6], AF.Exp))

            @b.vector
            def _(v):
                v.wait_ge(c_ac.h, e_exp)
                k.vst(v, v.tensor_scalar(k.scrcol[:, 6:7], k.scrcol[:, 6:7], -1.0, None, ALU.mult))
                k.vst(v, v.tensor_add(k.neglam[:, :], k.scrcol[:, 7:8], k.scrcol[:, 6:7]))
                c_dv.emit(v.tensor_scalar(k.neglam[:, :], k.neglam[:, :], -LAMBDA_INIT, None, ALU.add))


def emit_load_x(k: K):
    nc = k.nc
    qsy, c_pe, c_dv = k.ch("qsy"), k.ch("pe"), k.ch("dv")
    with ExitStack() as ph:
        xin = ph.enter_context(nc.sbuf_tensor("xin", [128, 2 * D], F32))
        psT = [ph.enter_context(nc.psum_tensor(f"psT{i}", [128, 512], F32)) for i in range(4)]
        NTOT = TCH * FC
        e_ld = qsy.reserve(TCH, 16)
        e_tr = c_pe.reserve(NTOT)
        e_cp = c_dv.reserve(NTOT)
        with nc.Block() as b:
            @b.sync
            def _(sy):
                for tc in range(TCH):
                    if tc >= 2:
                        sy.wait_ge(c_pe.h, e_tr[(tc - 1) * FC - 1])
                    qsy.emit(sy.dma_start(out=xin[:, (tc % 2) * D:(tc % 2 + 1) * D],
                                          in_=k.in_x[tc * 128:(tc + 1) * 128, :]), 16)

            @b.tensor
            def _(t):
                n = 0
                for tc in range(TCH):
                    t.wait_ge(qsy.h, e_ld[tc])
                    for fc in range(FC):
                        if n >= 4:
                            t.wait_ge(c_dv.h, e_cp[n - 4])
                        c_pe.emit(t.transpose(
                            psT[n % 4][:, 0:128],
                            xin[:, (tc % 2) * D + fc * 128:(tc % 2) * D + (fc + 1) * 128],
                            k.ident_f[:, :]))
                        n += 1

            @b.vector
            def _(v):
                n = 0
                for tc in range(TCH):
                    for fc in range(FC):
                        v.wait_ge(c_pe.h, e_tr[n])
                        c_dv.emit(v.tensor_copy(k.rch(fc, slice(tc * 128, (tc + 1) * 128)),
                                                psT[n % 4][:, 0:128]))
                        n += 1


def emit_ln(k: K, tag, ag=None):
    nc = k.nc
    qsy = k.ch("qsy")
    c_pe, c_ac, c_dv = k.ch("pe"), k.ch("ac"), k.ch("dv")
    with ExitStack() as ph:
        sq = ph.enter_context(nc.sbuf_tensor(f"ln_sq_{tag}", [128, 2 * TPC], BF16))
        mean = ph.enter_context(nc.sbuf_tensor(f"ln_mean_{tag}", [128, TPC], F32))
        ex2 = ph.enter_context(nc.sbuf_tensor(f"ln_ex2_{tag}", [128, TPC], F32))
        tsc = ph.enter_context(nc.sbuf_tensor(f"ln_tsc_{tag}", [128, TPC], F32))
        rstd = ph.enter_context(nc.sbuf_tensor(f"ln_rstd_{tag}", [128, TPC], F32))
        mrn = ph.enter_context(nc.sbuf_tensor(f"ln_mrn_{tag}", [128, TPC], F32))
        ps_s = ph.enter_context(nc.psum_tensor(f"ln_pss_{tag}", [128, TPC], F32))
        ps_q = ph.enter_context(nc.psum_tensor(f"ln_psq_{tag}", [128, TPC], F32))

        e_sq = c_ac.reserve(FC)
        e_stats = c_ac.reserve(1)[0]
        e_sqrt = c_ac.reserve(1)[0]
        e_xb = c_dv.reserve(FC)
        e_var = c_dv.reserve(1)[0]
        e_norm = c_dv.reserve(FC)
        e_mm = c_pe.reserve(FC)
        if ag is not None:
            e_ag = qsy.reserve(FC, 16)

        with nc.Block() as b:
            @b.scalar
            def _(s):
                for fc in range(FC):
                    if fc >= 2:
                        s.wait_ge(c_pe.h, e_mm[fc - 2])
                    c_ac.emit(s.activation(sq[:, (fc % 2) * TPC:(fc % 2 + 1) * TPC],
                                           k.rch(fc), AF.Square))
                s.wait_ge(c_pe.h, e_mm[-1])
                s.activation(mean[:, :], ps_s[:, :], AF.Copy, scale=k.cst[:, 0:1])
                c_ac.emit(s.activation(ex2[:, :], ps_q[:, :], AF.Copy, scale=k.cst[:, 0:1]))
                s.wait_ge(c_dv.h, e_var)
                c_ac.emit(s.activation(tsc[:, :], tsc[:, :], AF.Sqrt, bias=k.cst[:, 2:3]))

            @b.vector
            def _(v):
                for fc in range(FC):
                    c_dv.emit(v.tensor_copy(k.xch(fc), k.rch(fc)))
                v.wait_ge(c_ac.h, e_stats)
                v.tensor_mul(tsc[:, :], mean[:, :], mean[:, :])
                c_dv.emit(v.scalar_tensor_tensor(tsc[:, :], tsc[:, :], -1.0, ex2[:, :],
                                                 ALU.mult, ALU.add))
                v.wait_ge(c_ac.h, e_sqrt)
                k.vst(v, v.reciprocal(rstd[:, :], tsc[:, :]))
                v.scalar_tensor_tensor(mrn[:, :], mean[:, :], -1.0, rstd[:, :],
                                       ALU.mult, ALU.mult)
                for fc in range(FC):
                    v.tensor_mul(tsc[:, :], k.rch(fc), rstd[:, :])
                    c_dv.emit(v.tensor_add(k.xch(fc), tsc[:, :], mrn[:, :]))

            @b.tensor
            def _(t):
                for fc in range(FC):
                    t.wait_ge(c_ac.h, e_sq[fc])
                    t.wait_ge(c_dv.h, e_xb[fc])
                    st, sp = (fc == 0), (fc == FC - 1)
                    for nh in range(2):
                        sl = slice(nh * 512, (nh + 1) * 512)
                        t.matmul(ps_s[:, sl], k.ones_bf[:, :], k.xch(fc, sl), start=st, stop=sp)
                        _g = t.matmul(ps_q[:, sl], k.ones_bf[:, :],
                                 sq[:, (fc % 2) * TPC:(fc % 2 + 1) * TPC][:, sl], start=st, stop=sp)
                    c_pe.emit(_g)

            if ag is not None:
                ag_in, ccname, ag_outt = ag

                @b.sync
                def _(sy):
                    for fc in range(FC):
                        sy.wait_ge(c_dv.h, e_norm[fc])
                        qsy.emit(sy.dma_start(out=ag_in.ap()[fc * 128:(fc + 1) * 128, :],
                                              in_=k.xch(fc)), 16)


def emit_park_res(k: K):
    nc = k.nc
    qsy = k.ch("qsy")
    ev = qsy.reserve(FC, 16)
    with nc.Block() as b:
        @b.sync
        def _(sy):
            for fc in range(FC):
                qsy.emit(sy.dma_start(out=k.res_park.ap()[fc * 128:(fc + 1) * 128, :],
                                      in_=k.rch(fc)), 16)
            sy.wait_ge(qsy.h, ev[-1])


def emit_restore_res(k: K):
    nc = k.nc
    qsy = k.ch("qsy")
    ev = qsy.reserve(FC, 16)
    with nc.Block() as b:
        @b.sync
        def _(sy):
            for fc in range(FC):
                qsy.emit(sy.dma_start(out=k.rch(fc),
                                      in_=k.res_park.ap()[fc * 128:(fc + 1) * 128, :]), 16)
    return ev[-1]


def emit_dump(k: K, getter, cast=False):
    nc = k.nc
    with nc.Block() as b:
        if cast:
            qgp = k.ch("qgp")
            ev = qgp.reserve(FC, 16)

            @b.gpsimd
            def _(g):
                for fc in range(FC):
                    qgp.emit(g.dma_start(out=k.dbg[fc * 128:(fc + 1) * 128, :],
                                         in_=getter(fc)), 16)
                g.wait_ge(qgp.h, ev[-1])
        else:
            qsy = k.ch("qsy")
            ev = qsy.reserve(FC, 16)

            @b.sync
            def _(sy):
                for fc in range(FC):
                    qsy.emit(sy.dma_start(out=k.dbg[fc * 128:(fc + 1) * 128, :],
                                          in_=getter(fc)), 16)
                sy.wait_ge(qsy.h, ev[-1])


def emit_collective(k: K, kind, in_ap, out_ap, ccname):
    nc = k.nc
    tgt = k.cc[ccname].reserve(1)[0]
    with nc.Block() as b:
        @b.gpsimd
        def _(g):
            with allow_ici_all_to_all():
                k.cc[ccname].emit(g.collective_compute(
                    kind, ALU.bypass,
                    replica_groups=[list(range(NCORE))],
                    ins=[in_ap], outs=[out_ap]))
            g.wait_ge(k.cc[ccname].h, tgt)



def emit_attn0(k: K):
    nc = k.nc
    qsy, qgp = k.ch("qsy"), k.ch("qgp")
    c_pe, c_ac, c_dv = k.ch("pe"), k.ch("ac"), k.ch("dv")
    NTT = 16
    with ExitStack() as ph:
        sbf = lambda n, s, d: ph.enter_context(nc.sbuf_tensor(n, s, d))
        wq_b = sbf("a0_wq", [128, FC * HD0], BF16)
        wk_b = sbf("a0_wk", [128, FC * HD0], BF16)
        wv_b = sbf("a0_wv", [128, FC * HD0], BF16)
        xt = sbf("a0_xt", [128, 24 * 512], BF16)
        q1n = sbf("a0_q1n", [128, TOK], BF16)
        q2n = sbf("a0_q2n", [128, TOK], BF16)
        k1n = sbf("a0_k1n", [128, TOK], BF16)
        k2n = sbf("a0_k2n", [128, TOK], BF16)
        v_tm = sbf("a0_vtm", [128, 64 * HD0], BF16)
        sqs = sbf("a0_sqs", [128, 2 * 512], BF16)
        rst = sbf("a0_rst", [128, 2 * 512], F32)
        pexp = sbf("a0_pexp", [128, 4 * 512], BF16)
        y1b = sbf("a0_y1", [128, 2 * 1024], F32)
        y2b = sbf("a0_y2", [128, 2 * 1024], F32)
        rden = sbf("a0_rden", [128, 2 * 512], F32)
        ytmp = sbf("a0_ytmp", [128, 2 * 1024], BF16)
        ps = [ph.enter_context(nc.psum_tensor(f"a0_ps{i}", [128, 512], F32)) for i in range(8)]

        e_w = qgp.reserve(3, 16)
        e_xt = qsy.reserve(NTT * FC, 16)
        e_proj, e_p2, e_stat = [], [], []
        for nt in range(NTT):
            e_proj.append(c_pe.reserve(FC))
            e_p2.append(c_pe.reserve(FC))
            e_stat.append(c_pe.reserve(4))
        e_sqr = [c_ac.reserve(8) for _ in range(NTT)]
        e_evac = [c_dv.reserve(12) for _ in range(NTT)]

        with nc.Block() as b:
            @b.gpsimd
            def _(g):
                qgp.emit(g.dma_start(out=wq_b[:, :], in_=k.in_wq.ap()), 16)
                qgp.emit(g.dma_start(out=wk_b[:, :], in_=k.in_wk.ap()), 16)
                qgp.emit(g.dma_start(out=wv_b[:, :], in_=k.in_wv.ap()), 16)

            @b.sync
            def _(sy):
                sy.wait_ge(k.cc["ag0"].h, 1)
                n = 0
                for nt in range(NTT):
                    r, co = nt // 2, (nt % 2) * 512
                    for fc in range(FC):
                        if n >= 24:
                            pt = n - 24
                            sy.wait_ge(c_pe.h, e_p2[pt // FC][pt % FC])
                        qsy.emit(sy.dma_start(
                            out=xt[:, (n % 24) * 512:(n % 24 + 1) * 512],
                            in_=k.ag0_out.ap()[r * D + fc * 128: r * D + (fc + 1) * 128,
                                               co:co + 512]), 16)
                        n += 1

            @b.tensor
            def _(t):
                t.wait_ge(qgp.h, e_w[2])
                for nt in range(NTT):
                    if nt >= 1:
                        t.wait_ge(c_dv.h, e_evac[nt - 1][-1])
                    for fc in range(FC):
                        n = nt * FC + fc
                        t.wait_ge(qsy.h, e_xt[n])
                        xs = xt[:, (n % 24) * 512:(n % 24 + 1) * 512]
                        st, sp = (fc == 0), (fc == FC - 1)
                        t.matmul(ps[0][:, :], wq_b[:, fc * HD0:fc * HD0 + 128], xs, start=st, stop=sp)
                        t.matmul(ps[1][:, :], wq_b[:, fc * HD0 + 128:fc * HD0 + 256], xs, start=st, stop=sp)
                        t.matmul(ps[2][:, :], wk_b[:, fc * HD0:fc * HD0 + 128], xs, start=st, stop=sp)
                        t.matmul(ps[3][:, :], wk_b[:, fc * HD0 + 128:fc * HD0 + 256], xs, start=st, stop=sp)
                        for tc in range(2):
                            _g = t.matmul(ps[4 + tc][:, 0:256],
                                          xs[:, tc * 128:(tc + 1) * 128],
                                          wv_b[:, fc * HD0:(fc + 1) * HD0], start=st, stop=sp)
                        c_pe.emit(_g)
                    t.wait_ge(c_dv.h, e_evac[nt][1])
                    for fc in range(FC):
                        n = nt * FC + fc
                        xs = xt[:, (n % 24) * 512:(n % 24 + 1) * 512]
                        st, sp = (fc == 0), (fc == FC - 1)
                        for tc in range(2, 4):
                            _g = t.matmul(ps[4 + tc - 2][:, 0:256],
                                          xs[:, tc * 128:(tc + 1) * 128],
                                          wv_b[:, fc * HD0:(fc + 1) * HD0], start=st, stop=sp)
                        c_pe.emit(_g)
                    for part in range(4):
                        t.wait_ge(c_ac.h, e_sqr[nt][2 * part])
                        _g = t.matmul(ps[6 + part % 2][:, :], k.ones_bf[:, :],
                                           sqs[:, (part % 2) * 512:(part % 2 + 1) * 512],
                                           start=True, stop=True)
                        c_pe.emit(_g)

            @b.scalar
            def _(s):
                for nt in range(NTT):
                    s.wait_ge(c_pe.h, e_proj[nt][FC - 1])
                    for part in range(4):
                        if part >= 2:
                            s.wait_ge(c_pe.h, e_stat[nt][part - 2])
                            s.wait_ge(c_dv.h, e_evac[nt][2 + 2 * (part - 2) + 1])
                        c_ac.emit(s.activation(sqs[:, (part % 2) * 512:(part % 2 + 1) * 512],
                                               ps[part][:, :], AF.Square))
                        s.wait_ge(c_pe.h, e_stat[nt][part])
                        c_ac.emit(s.activation(rst[:, (part % 2) * 512:(part % 2 + 1) * 512],
                                               ps[6 + part % 2][:, :], AF.Sqrt,
                                               scale=k.cst[:, 1:2], bias=k.cst[:, 3:4]))

            @b.vector
            def _(v):
                dsts = [q1n, q2n, k1n, k2n]
                for nt in range(NTT):
                    for tc in range(2):
                        v.wait_ge(c_pe.h, e_proj[nt][FC - 1])
                        c_dv.emit(v.tensor_copy(
                            v_tm[:, (nt * 4 + tc) * 256:(nt * 4 + tc + 1) * 256],
                            ps[4 + tc][:, 0:256]))
                    for part in range(4):
                        v.wait_ge(c_ac.h, e_sqr[nt][2 * part + 1])
                        sl = slice((part % 2) * 512, (part % 2 + 1) * 512)
                        c_dv.emit(v.reciprocal(rst[:, sl], rst[:, sl]))
                        v.wait_ge(c_dv.h, e_evac[nt][2 + 2 * part])
                        c_dv.emit(v.tensor_mul(dsts[part][:, nt * 512:(nt + 1) * 512],
                                               ps[part][:, :], rst[:, sl]))
                    for tc in range(2, 4):
                        v.wait_ge(c_pe.h, e_p2[nt][FC - 1])
                        c_dv.emit(v.tensor_copy(
                            v_tm[:, (nt * 4 + tc) * 256:(nt * 4 + tc + 1) * 256],
                            ps[4 + tc - 2][:, 0:256]))

        plan = []
        for mt in range(NTT):
            for a in range(2):
                plan.append((mt, a, 4 * (mt + 1)))

        e_sc, e_av, e_exp, e_msk, e_fin, e_cmb, e_ydma = {}, {}, {}, {}, {}, {}, {}
        for (mt, a, smax) in plan:
            scl_, avl = [None] * smax, [None] * smax
            scl_[0] = c_pe.reserve(1)[0]
            for sc in range(1, smax):
                scl_[sc] = c_pe.reserve(1)[0]
                avl[sc - 1] = c_pe.reserve(1)[0]
            avl[smax - 1] = c_pe.reserve(1)[0]
            e_sc[(mt, a)] = scl_
            e_av[(mt, a)] = avl
            e_exp[(mt, a)] = c_ac.reserve(smax)
            e_msk[(mt, a)] = c_dv.reserve(4)
            e_fin[(mt, a)] = c_dv.reserve(3)
            if a == 1:
                e_cmb[mt] = c_dv.reserve(2)
        for mt in range(NTT):
            e_ydma[mt] = qsy.reserve(2, 16)

        with nc.Block() as b:
            @b.tensor
            def _(t):
                t.wait_ge(c_dv.h, e_evac[NTT - 1][-1])
                for (mt, a, smax) in plan:
                    qn = q1n if a == 0 else q2n
                    kn = k1n if a == 0 else k2n
                    if (mt, a) != (0, 0):
                        pm, pa = (mt, 0) if a == 1 else (mt - 1, 1)
                        t.wait_ge(c_dv.h, e_fin[(pm, pa)][2])

                    def av(sc):
                        j = sc - 4 * mt
                        if j >= 0:
                            t.wait_ge(c_dv.h, e_msk[(mt, a)][j])
                        else:
                            t.wait_ge(c_ac.h, e_exp[(mt, a)][sc])
                        pslot = pexp[:, (sc % 4) * 512:(sc % 4 + 1) * 512]
                        st, sp = (sc == 0), (sc == smax - 1)
                        t.matmul(ps[2][:, :], v_tm[:, sc * HD0:sc * HD0 + 128], pslot,
                                 start=st, stop=sp)
                        t.matmul(ps[3][:, :], v_tm[:, sc * HD0 + 128:sc * HD0 + 256], pslot,
                                 start=st, stop=sp)
                        _g = t.matmul(ps[4][:, :], k.ones_bf[:, :], pslot, start=st, stop=sp)
                        c_pe.emit(_g)

                    for sc in range(smax):
                        if sc >= 2:
                            t.wait_ge(c_ac.h, e_exp[(mt, a)][sc - 2])
                        _g = t.matmul(ps[sc % 2][:, :], kn[:, sc * 128:(sc + 1) * 128],
                                           qn[:, mt * 512:(mt + 1) * 512], start=True, stop=True)
                        c_pe.emit(_g)
                        if sc >= 1:
                            av(sc - 1)
                    av(smax - 1)

            @b.scalar
            def _(s):
                for (mt, a, smax) in plan:
                    for sc in range(smax):
                        s.wait_ge(c_pe.h, e_sc[(mt, a)][sc])
                        if sc >= 4:
                            s.wait_ge(c_pe.h, e_av[(mt, a)][sc - 4])
                        c_ac.emit(s.activation(pexp[:, (sc % 4) * 512:(sc % 4 + 1) * 512],
                                               ps[sc % 2][:, :], AF.Exp, scale=k.cst[:, 5:6]))

            @b.vector
            def _(v):
                for (mt, a, smax) in plan:
                    for j in range(4):
                        sc = 4 * mt + j
                        v.wait_ge(c_ac.h, e_exp[(mt, a)][sc])
                        sl = slice((sc % 4) * 512, (sc % 4 + 1) * 512)
                        c_dv.emit(v.tensor_mul(pexp[:, sl], pexp[:, sl],
                                               k.masks_b[:, j * 512:(j + 1) * 512]))
                    v.wait_ge(c_pe.h, e_av[(mt, a)][smax - 1])
                    ybuf = y1b if a == 0 else y2b
                    rsl = slice(a * 512, (a + 1) * 512)
                    c_dv.emit(v.reciprocal(rden[:, rsl], ps[4][:, :]))
                    mo = (mt % 2) * 1024
                    c_dv.emit(v.tensor_mul(ybuf[:, mo:mo + 512], ps[2][:, :], rden[:, rsl]))
                    c_dv.emit(v.tensor_mul(ybuf[:, mo + 512:mo + 1024], ps[3][:, :], rden[:, rsl]))
                    if a == 1:
                        if mt >= 2:
                            v.wait_ge(qsy.h, e_ydma[mt - 2][1])
                        for e in range(2):
                            c_dv.emit(v.scalar_tensor_tensor(
                                ytmp[:, mo + e * 512:mo + (e + 1) * 512],
                                y2b[:, mo + e * 512:mo + (e + 1) * 512],
                                k.neglam[:, :],
                                y1b[:, mo + e * 512:mo + (e + 1) * 512],
                                ALU.mult, ALU.add))

            @b.sync
            def _(sy):
                for mt in range(NTT):
                    sy.wait_ge(c_dv.h, e_cmb[mt][1])
                    mo = (mt % 2) * 1024
                    r, co = mt // 2, (mt % 2) * 512
                    for e in range(2):
                        qsy.emit(sy.dma_start(
                            out=k.a2y_in.ap()[r * HD0 + e * 128: r * HD0 + (e + 1) * 128,
                                              co:co + 512],
                            in_=ytmp[:, mo + e * 512:mo + (e + 1) * 512]), 16)



def emit_attn0_out(k: K):
    nc = k.nc
    qsy, qgp = k.ch("qsy"), k.ch("qgp")
    c_pe, c_dv = k.ch("pe"), k.ch("dv")
    e_restore = emit_restore_res(k)
    with ExitStack() as ph:
        ys = ph.enter_context(nc.sbuf_tensor("a0o_ys", [128, FC * TPC], BF16))
        wo_b = ph.enter_context(nc.sbuf_tensor("a0o_wo", [128, 2 * D], BF16))
        psA = [ph.enter_context(nc.psum_tensor(f"a0o_ps{i}", [128, 512], F32)) for i in range(2)]
        e_ys = qsy.reserve(FC, 16)
        e_wo = qgp.reserve(FC, 16)
        e_mm = c_pe.reserve(FC * 2)
        e_add = c_dv.reserve(FC * 2)
        with nc.Block() as b:
            @b.sync
            def _(sy):
                sy.wait_ge(k.cc["a2y"].h, 1)
                for fc in range(FC):
                    qsy.emit(sy.dma_start(out=ys[:, fc * TPC:(fc + 1) * TPC],
                                          in_=k.a2y_out.ap()[fc * 128:(fc + 1) * 128, :]), 16)

            @b.gpsimd
            def _(g):
                for oc in range(FC):
                    if oc >= 2:
                        g.wait_ge(c_pe.h, e_mm[2 * (oc - 1) - 1])
                    qgp.emit(g.dma_start(out=wo_b[:, (oc % 2) * D:(oc % 2 + 1) * D],
                                         in_=k.in_wo[oc, :, :]), 16)

            @b.tensor
            def _(t):
                t.wait_ge(qsy.h, e_ys[-1])
                n = 0
                for oc in range(FC):
                    t.wait_ge(qgp.h, e_wo[oc])
                    for th in range(2):
                        if n >= 2:
                            t.wait_ge(c_dv.h, e_add[n - 2])
                        for kc in range(FC):
                            _g = t.matmul(psA[n % 2][:, :],
                                     wo_b[:, (oc % 2) * D + kc * 128:(oc % 2) * D + (kc + 1) * 128],
                                     ys[:, kc * TPC + th * 512: kc * TPC + (th + 1) * 512],
                                     start=(kc == 0), stop=(kc == FC - 1))
                        c_pe.emit(_g)
                        n += 1

            @b.vector
            def _(v):
                v.wait_ge(qsy.h, e_restore)
                n = 0
                for oc in range(FC):
                    for th in range(2):
                        v.wait_ge(c_pe.h, e_mm[n])
                        sl = slice(th * 512, (th + 1) * 512)
                        c_dv.emit(v.scalar_tensor_tensor(
                            k.rch(oc, sl), psA[n % 2][:, :], 1.0 - LAMBDA_INIT,
                            k.rch(oc, sl), ALU.mult, ALU.add))
                        n += 1


def emit_router(k: K, wt32):
    nc = k.nc
    qgp = k.ch("qgp")
    c_pe, c_ac, c_dv = k.ch("pe"), k.ch("ac"), k.ch("dv")
    with ExitStack() as ph:
        rt_b = ph.enter_context(nc.sbuf_tensor("rt_b", [128, FC * NE], BF16))
        lgt = ph.enter_context(nc.sbuf_tensor("rt_lgt", [128, NE], F32))
        prb = ph.enter_context(nc.sbuf_tensor("rt_prb", [128, NE], F32))
        tpr = ph.enter_context(nc.sbuf_tensor("rt_tpr", [128, NE], F32))
        wtt = ph.enter_context(nc.sbuf_tensor("rt_wtt", [128, NE], F32))
        msk = ph.enter_context(nc.sbuf_tensor("rt_msk", [128, NE], F32))
        mcol = ph.enter_context(nc.sbuf_tensor("rt_mcol", [128, 1], F32))
        psL = ph.enter_context(nc.psum_tensor("rt_psL", [128, 512], F32))
        psT = ph.enter_context(nc.psum_tensor("rt_psT", [128, 512], F32))
        e_w = qgp.reserve(1, 16)
        e_lg, e_tr = [], []
        for tc in range(TCH):
            e_lg.append(c_pe.reserve(1)[0])
            e_tr.append(c_pe.reserve(1)[0])
        e_lgfree, e_shift, e_wt, e_cp = [], [], [], []
        for tc in range(TCH):
            e_lgfree.append(c_dv.reserve(1)[0])
            e_shift.append(c_dv.reserve(1)[0])
            e_wt.append(c_dv.reserve(1)[0])
            e_cp.append(c_dv.reserve(1)[0])
        e_ex = [c_ac.reserve(1)[0] for _ in range(TCH)]
        with nc.Block() as b:
            @b.gpsimd
            def _(g):
                qgp.emit(g.dma_start(out=rt_b[:, :], in_=k.in_router.ap()), 16)

            @b.tensor
            def _(t):
                t.wait_ge(qgp.h, e_w[0])
                for tc in range(TCH):
                    if tc >= 1:
                        t.wait_ge(c_dv.h, e_lgfree[tc - 1])
                        t.wait_ge(c_dv.h, e_cp[tc - 1])
                    for fc in range(FC):
                        _g = t.matmul(psL[:, 0:NE],
                                 k.xln[:, fc * TPC + tc * 128: fc * TPC + (tc + 1) * 128],
                                 rt_b[:, fc * NE:(fc + 1) * NE],
                                 start=(fc == 0), stop=(fc == FC - 1))
                    c_pe.emit(_g)
                    t.wait_ge(c_dv.h, e_wt[tc])
                    _g = t.transpose(psT[0:NE, 0:128], wtt[:, :], k.ident_f[:, :])
                    c_pe.emit(_g)

            @b.scalar
            def _(s):
                for tc in range(TCH):
                    s.wait_ge(c_dv.h, e_shift[tc])
                    c_ac.emit(s.activation(prb[:, :], lgt[:, :], AF.Exp))

            @b.vector
            def _(v):
                for tc in range(TCH):
                    v.wait_ge(c_pe.h, e_lg[tc])
                    c_dv.emit(v.tensor_copy(lgt[:, :], psL[:, 0:NE]))
                    k.vst(v, v.reduce_max(mcol[:, :], lgt[:, :], AX.X, negate=True))
                    c_dv.emit(v.tensor_single_scalar(lgt[:, :], lgt[:, :], mcol[:, :], ALU.add))
                    v.wait_ge(c_ac.h, e_ex[tc])
                    k.vst(v, v.reduce_sum(mcol[:, :], prb[:, :], AX.X))
                    k.vst(v, v.reciprocal(mcol[:, :], mcol[:, :]))
                    k.vst(v, v.tensor_single_scalar(prb[:, :], prb[:, :], mcol[:, :], ALU.mult))
                    k.vst(v, v.tensor_copy(tpr[:, :], prb[:, :]))
                    for _ in range(3):
                        k.vst(v, v.reduce_max(mcol[:, :], tpr[:, :], AX.X))
                        k.vst(v, v.tensor_single_scalar(msk[:, :], tpr[:, :], mcol[:, :], ALU.is_ge))
                        k.vst(v, v.scalar_tensor_tensor(tpr[:, :], msk[:, :], -1e30, tpr[:, :],
                                                        ALU.mult, ALU.add))
                    k.vst(v, v.reduce_max(mcol[:, :], tpr[:, :], AX.X))
                    k.vst(v, v.tensor_single_scalar(msk[:, :], prb[:, :], mcol[:, :], ALU.is_ge))
                    k.vst(v, v.tensor_mul(wtt[:, :], prb[:, :], msk[:, :]))
                    k.vst(v, v.reduce_sum(mcol[:, :], wtt[:, :], AX.X))
                    k.vst(v, v.reciprocal(mcol[:, :], mcol[:, :]))
                    c_dv.emit(v.tensor_single_scalar(wtt[:, :], wtt[:, :], mcol[:, :], ALU.mult))
                    v.wait_ge(c_pe.h, e_tr[tc])
                    c_dv.emit(v.tensor_copy(wt32[0:NE, tc * 128:(tc + 1) * 128],
                                            psT[0:NE, 0:128]))


def emit_moe(k: K, wt32):
    nc = k.nc
    qsy = k.ch("qsy")
    c_pe, c_ac, c_dv = k.ch("pe"), k.ch("ac"), k.ch("dv")
    agw = k.agw_out.ap().rearrange("r c -> (r c)")
    with ExitStack() as ph:
        sbf = lambda n, s, d: ph.enter_context(nc.sbuf_tensor(n, s, d))
        gw_t = sbf("moe_gw", [128, 2 * 2048], BF16)
        uw_t = sbf("moe_uw", [128, 2 * 2048], BF16)
        dw_t = sbf("moe_dw", [128, 2 * 768], BF16)
        wrow = sbf("moe_wrow", [1, 2 * TPC], F32)
        h_sb = sbf("moe_h", [128, MCI * TPC], BF16)
        gs = sbf("moe_gs", [128, TPC], BF16)
        tts = sbf("moe_tts", [128, TPC], F32)
        wtb = sbf("moe_wtb", [128, TPC], F32)
        psG = ph.enter_context(nc.psum_tensor("moe_psG", [128, TPC], F32))
        psU = ph.enter_context(nc.psum_tensor("moe_psU", [128, TPC], F32))
        psW = ph.enter_context(nc.psum_tensor("moe_psW", [128, TPC], F32))
        psD = [ph.enter_context(nc.psum_tensor(f"moe_psD{i}", [128, 512], F32)) for i in range(2)]

        def gw_off(e, mc):
            return (e // EPC) * RANK_ELEMS + GW_BASE + (e % EPC) * EXP_ELEMS + mc * GW_TILE

        def uw_off(e, mc):
            return (e // EPC) * RANK_ELEMS + UW_BASE + (e % EPC) * EXP_ELEMS + mc * GW_TILE

        def dw_off(e, oc):
            return (e // EPC) * RANK_ELEMS + DW_BASE + (e % EPC) * EXP_ELEMS + oc * DW_TILE

        qgp = k.ch("qgp")
        e_gu, e_dw = [], []
        e_wtb, e_gmm, e_umm, e_dmm = [], [], [], []
        e_sil = []
        e_wtbc, e_h, e_dadd = [], [], []
        e_wrow = [qgp.reserve(1, 16)[0] for _ in range(NE)]
        for e in range(NE):
            e_gu.append([qsy.reserve(2, 16) for _ in range(MCI)])
            e_dw.append([qsy.reserve(1, 16)[0] for _ in range(FC)])
            e_wtb.append(c_pe.reserve(1)[0])
            gm, um = [], []
            for mc in range(MCI):
                gm.append(c_pe.reserve(2))
                um.append(c_pe.reserve(2))
            e_gmm.append(gm)
            e_umm.append(um)
            e_dmm.append([c_pe.reserve(2) for _ in range(FC)])
            e_sil.append([c_ac.reserve(2) for _ in range(MCI)])
            e_wtbc.append(c_dv.reserve(1)[0])
            e_h.append([c_dv.reserve(2) for _ in range(MCI)])
            e_dadd.append([c_dv.reserve(2) for _ in range(FC)])

        with nc.Block() as b:
            @b.gpsimd
            def _(g):
                for e in range(NE):
                    if e >= 2:
                        g.wait_ge(c_pe.h, e_wtb[e - 2])
                    qgp.emit(g.dma_start(out=wrow[0:1, (e % 2) * TPC:(e % 2 + 1) * TPC],
                                         in_=wt32[e:e + 1, :]), 16)

            @b.sync
            def _(sy):
                sy.wait_ge(k.cc["agw"].h, 1)
                for e in range(NE):
                    for mc in range(MCI):
                        g = e * MCI + mc
                        if g >= 2:
                            pe_, pmc = (g - 2) // MCI, (g - 2) % MCI
                            sy.wait_ge(c_pe.h, e_umm[pe_][pmc][1])
                        o = gw_off(e, mc)
                        qsy.emit(sy.dma_start(
                            out=gw_t[:, (g % 2) * 2048:(g % 2 + 1) * 2048],
                            in_=agw[o:o + GW_TILE].rearrange("(p f) -> p f", p=128)), 16)
                        o = uw_off(e, mc)
                        qsy.emit(sy.dma_start(
                            out=uw_t[:, (g % 2) * 2048:(g % 2 + 1) * 2048],
                            in_=agw[o:o + GW_TILE].rearrange("(p f) -> p f", p=128)), 16)
                    for oc in range(FC):
                        g = e * FC + oc
                        if g >= 2:
                            pe_, poc = (g - 2) // FC, (g - 2) % FC
                            sy.wait_ge(c_pe.h, e_dmm[pe_][poc][1])
                        o = dw_off(e, oc)
                        qsy.emit(sy.dma_start(
                            out=dw_t[:, (g % 2) * 768:(g % 2 + 1) * 768],
                            in_=agw[o:o + DW_TILE].rearrange("(p f) -> p f", p=128)), 16)

            @b.scalar
            def _(s):
                for e in range(NE):
                    for mc in range(MCI):
                        for th in range(2):
                            s.wait_ge(c_pe.h, e_gmm[e][mc][th])
                            if mc >= 1:
                                s.wait_ge(c_dv.h, e_h[e][mc - 1][th])
                            elif e >= 1:
                                s.wait_ge(c_dv.h, e_h[e - 1][MCI - 1][th])
                            c_ac.emit(s.activation(gs[:, th * 512:(th + 1) * 512],
                                                   psG[:, th * 512:(th + 1) * 512], AF.Silu))

            @b.tensor
            def _(t):
                for e in range(NE):
                    if e >= 1:
                        t.wait_ge(c_dv.h, e_wtbc[e - 1])
                    t.wait_ge(qgp.h, e_wrow[e])
                    wsl = wrow[0:1, (e % 2) * TPC:(e % 2 + 1) * TPC]
                    t.matmul(psW[:, 0:512], k.ones_f[0:1, :], wsl[0:1, 0:512],
                             start=True, stop=True)
                    _g = t.matmul(psW[:, 512:1024], k.ones_f[0:1, :], wsl[0:1, 512:1024],
                                       start=True, stop=True)
                    c_pe.emit(_g)
                    for mc in range(MCI):
                        g = e * MCI + mc
                        t.wait_ge(qsy.h, e_gu[e][mc][1])
                        if g >= 1:
                            pe_, pmc = (g - 1) // MCI, (g - 1) % MCI
                            t.wait_ge(c_ac.h, e_sil[pe_][pmc][1])
                            t.wait_ge(c_dv.h, e_h[pe_][pmc][1])
                        for th in range(2):
                            for kc in range(FC):
                                _g = t.matmul(psG[:, th * 512:(th + 1) * 512],
                                         gw_t[:, (g % 2) * 2048 + kc * 128:(g % 2) * 2048 + (kc + 1) * 128],
                                         k.xln[:, kc * TPC + th * 512: kc * TPC + (th + 1) * 512],
                                         start=(kc == 0), stop=(kc == FC - 1))
                            c_pe.emit(_g)
                        for th in range(2):
                            for kc in range(FC):
                                _g = t.matmul(psU[:, th * 512:(th + 1) * 512],
                                         uw_t[:, (g % 2) * 2048 + kc * 128:(g % 2) * 2048 + (kc + 1) * 128],
                                         k.xln[:, kc * TPC + th * 512: kc * TPC + (th + 1) * 512],
                                         start=(kc == 0), stop=(kc == FC - 1))
                            c_pe.emit(_g)
                    t.wait_ge(c_dv.h, e_h[e][MCI - 1][1])
                    for oc in range(FC):
                        g = e * FC + oc
                        t.wait_ge(qsy.h, e_dw[e][oc])
                        for th in range(2):
                            if g >= 1:
                                pe_, poc = (g - 1) // FC, (g - 1) % FC
                                t.wait_ge(c_dv.h, e_dadd[pe_][poc][th])
                            for kc in range(MCI):
                                _g = t.matmul(psD[th][:, :],
                                         dw_t[:, (g % 2) * 768 + kc * 128:(g % 2) * 768 + (kc + 1) * 128],
                                         h_sb[:, kc * TPC + th * 512: kc * TPC + (th + 1) * 512],
                                         start=(kc == 0), stop=(kc == MCI - 1))
                            c_pe.emit(_g)

            @b.vector
            def _(v):
                for e in range(NE):
                    v.wait_ge(c_pe.h, e_wtb[e])
                    c_dv.emit(v.tensor_copy(wtb[:, :], psW[:, :]))
                    for mc in range(MCI):
                        for th in range(2):
                            sl = slice(th * 512, (th + 1) * 512)
                            v.wait_ge(c_pe.h, e_umm[e][mc][th])
                            v.tensor_mul(tts[:, sl], psU[:, sl], wtb[:, sl])
                            v.wait_ge(c_ac.h, e_sil[e][mc][th])
                            c_dv.emit(v.tensor_mul(
                                h_sb[:, mc * TPC + th * 512: mc * TPC + (th + 1) * 512],
                                tts[:, sl], gs[:, sl]))
                    for oc in range(FC):
                        for th in range(2):
                            v.wait_ge(c_pe.h, e_dmm[e][oc][th])
                            sl = slice(th * 512, (th + 1) * 512)
                            c_dv.emit(v.tensor_tensor(k.rch(oc, sl), psD[th][:, :],
                                                      k.rch(oc, sl), ALU.add))


def emit_attn1(k: K):
    nc = k.nc
    qsy, qgp, qsc = k.ch("qsy"), k.ch("qgp"), k.ch("qsc")
    c_pe, c_ac, c_dv = k.ch("pe"), k.ch("ac"), k.ch("dv")
    NTT = 16
    with ExitStack() as ph:
        sbf = lambda n, s, d: ph.enter_context(nc.sbuf_tensor(n, s, d))
        w_b = sbf("a1_w", [128, FC * 1024], BF16)
        xt = sbf("a1_xt", [128, 24 * 512], BF16)
        qsb = [sbf(f"a1_q{h}", [128, TOK], BF16) for h in range(2)]
        ksb = [sbf(f"a1_k{h}", [128, TOK], BF16) for h in range(2)]
        v_tm = sbf("a1_vtm", [128, 64 * 256], BF16)
        ubt = sbf("a1_ubt", [128, 2 * 1024], BF16)
        slp = sbf("a1_slp", [128, 2], F32)
        nsc = sbf("a1_nsc", [128, 2], F32)
        arrs = sbf("a1_arrs", [128, BLK], F32)
        arr2 = sbf("a1_arr2", [128, 2], F32)
        idxs = sbf("a1_idxs", [128, 2 * BLK], F32)
        dmsk = sbf("a1_dmsk", [128, 2 * BLK], F32)
        cc256 = sbf("a1_c256", [128, 1], F32)
        qdec = [sbf(f"a1_qdec{h}", [128, BLK], F32) for h in range(2)]
        kdec = [sbf(f"a1_kdec{h}", [128, 2], F32) for h in range(2)]
        ddt = [sbf(f"a1_ddt{h}", [128, 2 * BLK], F32) for h in range(2)]
        bdc = [sbf(f"a1_bdc{h}", [128, 1], F32) for h in range(2)]
        kv = [sbf(f"a1_kv{i}", [128, 128], F32) for i in range(2)]
        kv_b = sbf("a1_kvb", [128, 128], BF16)
        qd_t = sbf("a1_qdt", [128, BLK], BF16)
        tnd = sbf("a1_tnd", [128, BLK], F32)
        dumm = sbf("a1_dumm", [1, 8], F32)
        pdd = sbf("a1_pdd", [128, 2 * BLK], BF16)
        ktm = sbf("a1_ktm", [128, 2 * 128], BF16)
        ot = sbf("a1_ot", [128, 4 * BLK], BF16)

        e_w = qgp.reserve(4, 16)
        e_tab = qsy.reserve(6, 16)
        e_xt = qsy.reserve(NTT * FC + 4, 16)
        e_ns = c_dv.reserve(1)[0]
        e_ddm = c_dv.reserve(2)
        e_tbl = c_ac.reserve(8)
        e_proj, e_p2 = [], []
        for nt in range(NTT):
            e_proj.append(c_pe.reserve(FC))
            e_p2.append(c_pe.reserve(FC))
        e_sil = [c_ac.reserve(4) for _ in range(NTT)]
        e_evac = [c_dv.reserve(6) for _ in range(NTT)]
        e_udma = [qsc.reserve(2, 16) for _ in range(NTT)]

        with ExitStack() as pp:
            ps = [pp.enter_context(nc.psum_tensor(f"a1p_ps{i}", [128, 512], F32)) for i in range(8)]
            with nc.Block() as b:
                @b.gpsimd
                def _(g):
                    for i in range(4):
                        qgp.emit(g.dma_start(out=w_b[:, i * 4096:(i + 1) * 4096],
                                             in_=k.in_l1w[:, i * 4096:(i + 1) * 4096]), 16)

                @b.sync
                def _(sy):
                    qsy.emit(sy.dma_start(out=slp[:, :], in_=k.in_slopes.ap()), 16)
                    qsy.emit(sy.dma_start(out=arrs[:, :], in_=k.in_arr.ap()), 16)
                    qsy.emit(sy.dma_start(out=arr2[:, :], in_=k.in_arr2t.ap()), 16)
                    qsy.emit(sy.dma_start(out=idxs[:, :], in_=k.in_idx.ap()), 16)
                    qsy.emit(sy.dma_start(out=cc256[:, :], in_=k.in_c256.ap()), 16)
                    qsy.emit(sy.dma_start(out=dmsk[:, :], in_=k.in_dmask.ap()), 16)
                    sy.wait_ge(k.cc["ag2"].h, 1)
                    n = 0
                    for nt in range(NTT):
                        r, co = nt // 2, (nt % 2) * 512
                        for fc in range(FC):
                            if n >= 24:
                                pt = n - 24
                                sy.wait_ge(c_pe.h, e_p2[pt // FC][pt % FC])
                            qsy.emit(sy.dma_start(
                                out=xt[:, (n % 24) * 512:(n % 24 + 1) * 512],
                                in_=k.ag2_out.ap()[r * D + fc * 128: r * D + (fc + 1) * 128,
                                                   co:co + 512]), 16)
                            n += 1
                    for _ in range(4):
                        qsy.emit(sy.dma_start(
                            out=slp[:, 0:1],
                            in_=k.in_slopes.ap()[:, 0:1],
                            allow_slow_non_contiguous=True), 16)

                @b.vector
                def _(v):
                    v.wait_ge(qsy.h, e_tab[0])
                    c_dv.emit(v.tensor_scalar(nsc[:, :], slp[:, :], -1.0, None, ALU.mult))
                    for h in range(2):
                        v.wait_ge(c_ac.h, e_tbl[4 * h + 2])
                        c_dv.emit(v.tensor_mul(ddt[h][:, :], ddt[h][:, :], dmsk[:, :]))
                    for nt in range(NTT):
                        for h in range(2):
                            v.wait_ge(c_pe.h, e_proj[nt][FC - 1])
                            if nt >= 2:
                                v.wait_ge(qsc.h, e_udma[nt - 2][1])
                            c_dv.emit(v.tensor_copy(
                                ubt[:, (nt % 2) * 1024 + h * 512:(nt % 2) * 1024 + (h + 1) * 512],
                                ps[4 + h][:, :]))
                        for tc in range(2):
                            c_dv.emit(v.tensor_copy(
                                v_tm[:, (nt * 4 + tc) * 256:(nt * 4 + tc + 1) * 256],
                                ps[6 + tc][:, 0:256]))
                        for tc in range(2, 4):
                            v.wait_ge(c_pe.h, e_p2[nt][FC - 1])
                            c_dv.emit(v.tensor_copy(
                                v_tm[:, (nt * 4 + tc) * 256:(nt * 4 + tc + 1) * 256],
                                ps[6 + tc - 2][:, 0:256]))

                @b.scalar
                def _(s):
                    s.wait_ge(qsy.h, e_tab[5])
                    s.wait_ge(c_dv.h, e_ns)
                    for h in range(2):
                        c_ac.emit(s.activation(qdec[h][:, :], arrs[:, :], AF.Exp,
                                               scale=nsc[:, h:h + 1]))
                        c_ac.emit(s.activation(kdec[h][:, :], arr2[:, :], AF.Exp,
                                               scale=nsc[:, h:h + 1]))
                        c_ac.emit(s.activation(ddt[h][:, :], idxs[:, :], AF.Exp,
                                               scale=nsc[:, h:h + 1]))
                        c_ac.emit(s.activation(bdc[h][:, :], cc256[:, :], AF.Exp,
                                               scale=nsc[:, h:h + 1]))
                    for nt in range(NTT):
                        s.wait_ge(c_pe.h, e_proj[nt][FC - 1])
                        for i, (buf, bank) in enumerate([(qsb[0], 0), (qsb[1], 1),
                                                         (ksb[0], 2), (ksb[1], 3)]):
                            c_ac.emit(s.activation(buf[:, nt * 512:(nt + 1) * 512],
                                                   ps[bank][:, :], AF.Silu))
                        for h in range(2):
                            s.wait_ge(c_dv.h, e_evac[nt][h])
                            qsc.emit(s.dma_start(
                                out=k.a2ou_in.ap()[(nt // 2) * 512 + 256 + h * 128:
                                                   (nt // 2) * 512 + 256 + (h + 1) * 128,
                                                   (nt % 2) * 512:(nt % 2 + 1) * 512],
                                in_=ubt[:, (nt % 2) * 1024 + h * 512:(nt % 2) * 1024 + (h + 1) * 512]), 16)

                @b.tensor
                def _(t):
                    t.wait_ge(qgp.h, e_w[3])
                    for nt in range(NTT):
                        if nt >= 1:
                            t.wait_ge(c_ac.h, e_sil[nt - 1][3])
                            t.wait_ge(c_dv.h, e_evac[nt - 1][-1])
                        for fc in range(FC):
                            n = nt * FC + fc
                            t.wait_ge(qsy.h, e_xt[n + 4])
                            xs = xt[:, (n % 24) * 512:(n % 24 + 1) * 512]
                            st, sp = (fc == 0), (fc == FC - 1)
                            base = fc * 1024
                            t.matmul(ps[0][:, :], w_b[:, base:base + 128], xs, start=st, stop=sp)
                            t.matmul(ps[1][:, :], w_b[:, base + 128:base + 256], xs, start=st, stop=sp)
                            t.matmul(ps[2][:, :], w_b[:, base + 256:base + 384], xs, start=st, stop=sp)
                            t.matmul(ps[3][:, :], w_b[:, base + 384:base + 512], xs, start=st, stop=sp)
                            t.matmul(ps[4][:, :], w_b[:, base + 768:base + 896], xs, start=st, stop=sp)
                            t.matmul(ps[5][:, :], w_b[:, base + 896:base + 1024], xs, start=st, stop=sp)
                            for tc in range(2):
                                _g = t.matmul(ps[6 + tc][:, 0:256],
                                              xs[:, tc * 128:(tc + 1) * 128],
                                              w_b[:, base + 512:base + 768], start=st, stop=sp)
                            c_pe.emit(_g)
                        t.wait_ge(c_dv.h, e_evac[nt][3])
                        for fc in range(FC):
                            n = nt * FC + fc
                            xs = xt[:, (n % 24) * 512:(n % 24 + 1) * 512]
                            st, sp = (fc == 0), (fc == FC - 1)
                            base = fc * 1024
                            for tc in range(2, 4):
                                _g = t.matmul(ps[6 + tc - 2][:, 0:256],
                                              xs[:, tc * 128:(tc + 1) * 128],
                                              w_b[:, base + 512:base + 768], start=st, stop=sp)
                            c_pe.emit(_g)


        if getattr(k, "dump_a1tb", False):
            qgp2 = k.ch("qgp")
            ev = qgp2.reserve(12, 16)
            with nc.Block() as b2:
                @b2.gpsimd
                def _(g):
                    g.wait_ge(k.ch("ac").h, e_tbl[-1])
                    g.wait_ge(k.ch("dv").h, e_evac[NTT - 1][-1])
                    g.wait_ge(k.ch("ac").h, e_sil[NTT - 1][3])
                    dd = k.dbg
                    qgp2.emit(g.dma_start(out=dd[0:128, 0:BLK], in_=qdec[0][:, :]), 16)
                    qgp2.emit(g.dma_start(out=dd[128:256, 0:BLK], in_=qdec[1][:, :]), 16)
                    qgp2.emit(g.dma_start(out=dd[256:384, 0:2 * BLK], in_=ddt[0][:, :]), 16)
                    qgp2.emit(g.dma_start(out=dd[384:512, 0:2 * BLK], in_=ddt[1][:, :]), 16)
                    qgp2.emit(g.dma_start(out=dd[512:640, 0:2], in_=kdec[0][:, :],
                                           allow_slow_non_contiguous=True), 16)
                    qgp2.emit(g.dma_start(out=dd[640:768, 0:2], in_=kdec[1][:, :],
                                          allow_slow_non_contiguous=True), 16)
                    qgp2.emit(g.dma_start(out=dd[768:896, 0:1], in_=bdc[0][:, :],
                                          allow_slow_non_contiguous=True), 16)
                    qgp2.emit(g.dma_start(out=dd[896:1024, 0:2], in_=nsc[:, :],
                                          allow_slow_non_contiguous=True), 16)
                    qgp2.emit(g.dma_start(out=dd[1024:1152, 0:512], in_=qsb[1][:, 0:512]), 16)
                    qgp2.emit(g.dma_start(out=dd[1152:1280, 0:512], in_=ksb[1][:, 0:512]), 16)
                    qgp2.emit(g.dma_start(out=dd[1280:1408, 0:1024], in_=v_tm[:, 0:1024]), 16)
                    qgp2.emit(g.dma_start(out=dd[1408:1536, 0:1024], in_=ubt[:, 0:1024]), 16)
                    g.wait_ge(qgp2.h, ev[-1])
            return
        e_scan_pe = {}
        e_scan_dv = {}
        e_odma = {}
        for h in range(2):
            for bb in range(NB):
                e_scan_pe[(h, bb)] = c_pe.reserve(3)
                e_scan_dv[(h, bb)] = c_dv.reserve(5)
        e_scan_tail = c_dv.reserve(1)[0]

        if getattr(k, "dump_qk_pre", False):
            k.barrier()
            with nc.Block() as b:
                @b.gpsimd
                def _(g):
                    qgp2 = k.ch("qgp")
                    ev = qgp2.reserve(6, 16)
                    qgp2.emit(g.dma_start(out=k.dbg[0:128, :], in_=qsb[0][:, 0:1024]), 16)
                    qgp2.emit(g.dma_start(out=k.dbg[128:256, :], in_=qsb[1][:, 0:1024]), 16)
                    qgp2.emit(g.dma_start(out=k.dbg[256:384, :], in_=ksb[0][:, 0:1024]), 16)
                    qgp2.emit(g.dma_start(out=k.dbg[384:512, :], in_=ksb[1][:, 0:1024]), 16)
                    qgp2.emit(g.dma_start(out=k.dbg[512:640, :], in_=v_tm[:, 0:1024]), 16)
                    qgp2.emit(g.dma_start(out=k.dbg[640:768, :], in_=qsb[0][:, 7168:8192]), 16)
                    g.wait_ge(qgp2.h, ev[-1])
            return

        with ExitStack() as pp:
            psO = pp.enter_context(nc.psum_tensor("a1s_O", [128, 512], F32))
            psS = [pp.enter_context(nc.psum_tensor(f"a1s_S{j}", [128, 512], F32)) for j in range(2)]
            psKV = pp.enter_context(nc.psum_tensor("a1s_KV", [128, 512], F32))
            psB = pp.enter_context(nc.psum_tensor("a1s_B", [128, 1024], BF16))
            psF = pp.enter_context(nc.psum_tensor("a1s_F", [128, 512], F32))
            psN = pp.enter_context(nc.psum_tensor("a1s_N", [128, 512], F32))

            first = True
            for h in range(2):
                for bb in range(NB):
                    E = e_scan_dv[(h, bb)]
                    EP = e_scan_pe[(h, bb)]
                    g_ = h * NB + bb
                    e_odma[(h, bb)] = qsy.reserve(1, 16)[0]
                    with nc.Block() as b:
                        @b.vector
                        def _(v, h=h, bb=bb, E=E, EP=EP, g_=g_, first=first):
                            cur, nxt = kv[bb % 2], kv[(bb + 1) % 2]
                            if first:
                                v.wait_ge(c_ac.h, e_sil[NTT - 1][3])
                                v.wait_ge(c_dv.h, e_evac[NTT - 1][-1])
                                v.wait_ge(c_ac.h, e_tbl[-1])
                            if bb == 0:
                                k.vst(v, v.memset(kv[0][:, :], 0.0))
                            v.tensor_mul(qd_t[:, :], qsb[h][:, bb * BLK:(bb + 1) * BLK],
                                         qdec[h][:, :])
                            k.vst(v, v.tensor_copy(kv_b[:, :], cur[:, :]))
                            c_dv.emit(v.tensor_copy(dumm[0:1, 0:4], dumm[0:1, 4:8]))
                            v.wait_ge(c_pe.h, EP[0])
                            v.tensor_mul(pdd[:, 0:BLK], psS[0][:, 0:BLK], ddt[h][:, 0:BLK])
                            v.tensor_mul(pdd[:, BLK:2 * BLK], psS[1][:, 0:BLK],
                                         ddt[h][:, BLK:2 * BLK])
                            import os as _os
                            if _os.environ.get("SCAN_ND", "1") == "1":
                                k.vst(v, v.tensor_copy(tnd[:, :], psN[:, 0:BLK]))
                            else:
                                k.vst(v, v.tensor_scalar(tnd[:, :], k.ones_bf[:, 0:128] if False else psN[:, 0:BLK], 0.0, None, ALU.mult))
                            c_dv.emit(v.tensor_copy(dumm[0:1, 0:4], dumm[0:1, 4:8]))
                            v.wait_ge(c_pe.h, EP[1])
                            v.tensor_single_scalar(ktm[:, 0:128], psB[:, 0:128],
                                                   kdec[h][:, 0:1], ALU.mult)
                            k.vst(v, v.tensor_single_scalar(ktm[:, 128:256], psB[:, 128:256],
                                                            kdec[h][:, 1:2], ALU.mult))
                            c_dv.emit(v.tensor_copy(dumm[0:1, 0:4], dumm[0:1, 4:8]))
                            v.wait_ge(c_pe.h, EP[2])
                            c_dv.emit(v.scalar_tensor_tensor(nxt[:, :], cur[:, :],
                                                             bdc[h][:, :], psKV[:, 0:128],
                                                             ALU.mult, ALU.add))
                            if g_ >= 4:
                                pv = g_ - 4
                                v.wait_ge(qsy.h, e_odma[(pv // NB, pv % NB)])
                            k.vst(v, v.scalar_tensor_tensor(
                                ot[:, (g_ % 4) * BLK:(g_ % 4 + 1) * BLK],
                                psO[:, 0:BLK], 1.0, tnd[:, :], ALU.mult, ALU.add))
                            c_dv.emit(v.tensor_copy(dumm[0:1, 0:4], dumm[0:1, 4:8]))

                        @b.tensor
                        def _(t, h=h, bb=bb, E=E, EP=EP, first=first):
                            if first:
                                t.wait_ge(c_ac.h, e_sil[NTT - 1][3])
                                t.wait_ge(c_dv.h, e_evac[NTT - 1][-1])

                            def flush(ev):
                                mm = t.matmul(psF[0:16, 0:128], k.ones_bf[:, 0:16],
                                              k.ones_bf[:, :], start=True, stop=True)
                                if ev:
                                    c_pe.emit(mm)
                            t.wait_ge(c_dv.h, E[0])
                            t.matmul(psS[0][:, 0:BLK], ksb[h][:, (2 * bb) * 128:(2 * bb + 1) * 128],
                                     qsb[h][:, bb * BLK:(bb + 1) * BLK], start=True, stop=True)
                            t.matmul(psS[1][:, 0:BLK],
                                     ksb[h][:, (2 * bb + 1) * 128:(2 * bb + 2) * 128],
                                     qsb[h][:, bb * BLK:(bb + 1) * BLK],
                                     start=True, stop=True)
                            import os as _os
                            if _os.environ.get("SCAN_ND", "1") == "1":
                                t.matmul(psN[:, 0:BLK], kv_b[:, :], qd_t[:, :],
                                         start=True, stop=True)
                            else:
                                t.matmul(psN[:, 0:BLK], k.ones_bf[:, :], qd_t[:, :],
                                         start=True, stop=True)
                            flush(True)
                            t.wait_ge(c_dv.h, E[1])
                            t.matmul(psO[:, 0:BLK], v_tm[:, (2 * bb) * 256 + h * 128:
                                                         (2 * bb) * 256 + (h + 1) * 128],
                                     pdd[:, 0:BLK], start=True, stop=False)
                            t.matmul(psO[:, 0:BLK],
                                     v_tm[:, (2 * bb + 1) * 256 + h * 128:
                                          (2 * bb + 1) * 256 + (h + 1) * 128],
                                     pdd[:, BLK:2 * BLK], start=False, stop=True)
                            t.transpose(psB[:, 0:128], ksb[h][:, (2 * bb) * 128:(2 * bb + 1) * 128],
                                        k.ident_b[:, :])
                            t.transpose(psB[:, 128:256],
                                        ksb[h][:, (2 * bb + 1) * 128:(2 * bb + 2) * 128],
                                        k.ident_b[:, :])
                            flush(True)
                            t.wait_ge(c_dv.h, E[2])
                            t.matmul(psKV[:, 0:128], ktm[:, 0:128],
                                     v_tm[:, (2 * bb) * 256 + h * 128:(2 * bb) * 256 + (h + 1) * 128],
                                     start=True, stop=False)
                            t.matmul(psKV[:, 0:128], ktm[:, 128:256],
                                     v_tm[:, (2 * bb + 1) * 256 + h * 128:
                                          (2 * bb + 1) * 256 + (h + 1) * 128],
                                     start=False, stop=True)
                            flush(True)

                        @b.sync
                        def _(sy, h=h, bb=bb, E=E, g_=g_):
                            sy.wait_ge(c_dv.h, E[4])
                            qsy.emit(sy.dma_start(
                                out=k.a2ou_in.ap()[(bb // 4) * 512 + h * 128:
                                                   (bb // 4) * 512 + (h + 1) * 128,
                                                   (bb % 4) * 256:(bb % 4 + 1) * 256],
                                in_=ot[:, (g_ % 4) * BLK:(g_ % 4 + 1) * BLK]), 16)
                    k.barrier()
                    first = False
            with nc.Block() as b:
                @b.vector
                def _(v):
                    c_dv.emit(v.tensor_copy(dumm[0:1, 0:4], dumm[0:1, 4:8]))
            if getattr(k, "dump_a1qk", False):
                k.barrier()
                with nc.Block() as b:
                    @b.gpsimd
                    def _(g):
                        qgp2 = k.ch("qgp")
                        ev = qgp2.reserve(6, 16)
                        qgp2.emit(g.dma_start(out=k.dbg[0:128, :], in_=qsb[0][:, 0:1024]), 16)
                        qgp2.emit(g.dma_start(out=k.dbg[128:256, :], in_=qsb[1][:, 0:1024]), 16)
                        qgp2.emit(g.dma_start(out=k.dbg[256:384, :], in_=ksb[0][:, 0:1024]), 16)
                        qgp2.emit(g.dma_start(out=k.dbg[384:512, :], in_=ksb[1][:, 0:1024]), 16)
                        qgp2.emit(g.dma_start(out=k.dbg[512:640, :], in_=v_tm[:, 0:1024]), 16)
                        qgp2.emit(g.dma_start(out=k.dbg[640:768, 0:512], in_=pdd[:, 0:512]), 16)
                        g.wait_ge(qgp2.h, ev[-1])


def emit_attn1_out(k: K):
    nc = k.nc
    qsy, qgp = k.ch("qsy"), k.ch("qgp")
    c_pe, c_ac, c_dv = k.ch("pe"), k.ch("ac"), k.ch("dv")
    e_restore = emit_restore_res(k)
    with ExitStack() as ph:
        os_ = ph.enter_context(nc.sbuf_tensor("a1o_os", [128, FC * TPC], BF16))
        us_ = ph.enter_context(nc.sbuf_tensor("a1o_us", [128, FC * TPC], BF16))
        sqb = ph.enter_context(nc.sbuf_tensor("a1o_sqb", [128, 2 * TPC], BF16))
        rstd = ph.enter_context(nc.sbuf_tensor("a1o_rstd", [128, TPC], F32))
        tmp = ph.enter_context(nc.sbuf_tensor("a1o_tmp", [128, TPC], F32))
        wo_b = ph.enter_context(nc.sbuf_tensor("a1o_wo", [128, 2 * D], BF16))
        psS = ph.enter_context(nc.psum_tensor("a1o_psS", [128, TPC], F32))
        psA = [ph.enter_context(nc.psum_tensor(f"a1o_ps{i}", [128, 512], F32)) for i in range(2)]

        e_ou = qsy.reserve(2 * FC, 16)
        e_sq = c_ac.reserve(FC)
        e_rst = c_ac.reserve(1)[0]
        e_smm = c_pe.reserve(FC)
        e_xg = c_dv.reserve(FC)
        e_wo = qgp.reserve(FC, 16)
        e_mm = c_pe.reserve(FC * 2)
        e_add = c_dv.reserve(FC * 2)
        with nc.Block() as b:
            @b.sync
            def _(sy):
                sy.wait_ge(k.cc["a2ou"].h, 1)
                for fc in range(FC):
                    qsy.emit(sy.dma_start(
                        out=os_[:, fc * TPC:(fc + 1) * TPC],
                        in_=k.a2ou_out.ap()[(fc // 2) * 512 + (fc % 2) * 128:
                                            (fc // 2) * 512 + (fc % 2) * 128 + 128, :]), 16)
                for fc in range(FC):
                    qsy.emit(sy.dma_start(
                        out=us_[:, fc * TPC:(fc + 1) * TPC],
                        in_=k.a2ou_out.ap()[(fc // 2) * 512 + 256 + (fc % 2) * 128:
                                            (fc // 2) * 512 + 256 + (fc % 2) * 128 + 128, :]), 16)

            @b.scalar
            def _(s):
                for fc in range(FC):
                    s.wait_ge(qsy.h, e_ou[fc])
                    if fc >= 2:
                        s.wait_ge(c_pe.h, e_smm[fc - 2])
                    c_ac.emit(s.activation(sqb[:, (fc % 2) * TPC:(fc % 2 + 1) * TPC],
                                           os_[:, fc * TPC:(fc + 1) * TPC], AF.Square))
                s.wait_ge(c_pe.h, e_smm[-1])
                c_ac.emit(s.activation(rstd[:, :], psS[:, :], AF.Sqrt,
                                       scale=k.cst[:, 0:1], bias=k.cst[:, 4:5]))

            @b.tensor
            def _(t):
                for fc in range(FC):
                    t.wait_ge(c_ac.h, e_sq[fc])
                    st, sp = (fc == 0), (fc == FC - 1)
                    for nh in range(2):
                        sl = slice(nh * 512, (nh + 1) * 512)
                        _g = t.matmul(psS[:, sl], k.ones_bf[:, :],
                                 sqb[:, (fc % 2) * TPC:(fc % 2 + 1) * TPC][:, sl],
                                 start=st, stop=sp)
                    c_pe.emit(_g)
                t.wait_ge(c_dv.h, e_xg[-1])
                n = 0
                for oc in range(FC):
                    t.wait_ge(qgp.h, e_wo[oc])
                    for th in range(2):
                        if n >= 2:
                            t.wait_ge(c_dv.h, e_add[n - 2])
                        for kc in range(FC):
                            _g = t.matmul(psA[n % 2][:, :],
                                     wo_b[:, (oc % 2) * D + kc * 128:(oc % 2) * D + (kc + 1) * 128],
                                     os_[:, kc * TPC + th * 512: kc * TPC + (th + 1) * 512],
                                     start=(kc == 0), stop=(kc == FC - 1))
                        c_pe.emit(_g)
                        n += 1

            @b.vector
            def _(v):
                v.wait_ge(c_ac.h, e_rst)
                k.vst(v, v.reciprocal(rstd[:, :], rstd[:, :]))
                for fc in range(FC):
                    v.wait_ge(qsy.h, e_ou[FC + fc])
                    v.tensor_mul(tmp[:, :], os_[:, fc * TPC:(fc + 1) * TPC], rstd[:, :])
                    c_dv.emit(v.tensor_mul(os_[:, fc * TPC:(fc + 1) * TPC], tmp[:, :],
                                           us_[:, fc * TPC:(fc + 1) * TPC]))
                v.wait_ge(qsy.h, e_restore)
                n = 0
                for oc in range(FC):
                    for th in range(2):
                        v.wait_ge(c_pe.h, e_mm[n])
                        sl = slice(th * 512, (th + 1) * 512)
                        c_dv.emit(v.tensor_tensor(k.rch(oc, sl), psA[n % 2][:, :],
                                                  k.rch(oc, sl), ALU.add))
                        n += 1

            @b.gpsimd
            def _(g):
                for oc in range(FC):
                    if oc >= 2:
                        g.wait_ge(c_pe.h, e_mm[2 * (oc - 1) - 1])
                    qgp.emit(g.dma_start(out=wo_b[:, (oc % 2) * D:(oc % 2 + 1) * D],
                                         in_=k.in_wout[oc, :, :]), 16)


def emit_ffn1(k: K):
    nc = k.nc
    qsy, qgp = k.ch("qsy"), k.ch("qgp")
    c_pe, c_ac, c_dv = k.ch("pe"), k.ch("ac"), k.ch("dv")
    with ExitStack() as ph:
        sbf = lambda n, s, d: ph.enter_context(nc.sbuf_tensor(n, s, d))
        qg_b = sbf("f1_qg", [128, 2 * 2048], BF16)
        qu_b = sbf("f1_qu", [128, 2 * 2048], BF16)
        qd_b = sbf("f1_qd", [128, 2 * HC1 * 128], BF16)
        h_sb = sbf("f1_h", [128, HC1 * 512], BF16)
        sg_s = sbf("f1_sg", [128, HC1], F32)
        su_s = sbf("f1_su", [128, HC1], F32)
        sd_s = sbf("f1_sd", [128, FC], F32)
        xc = sbf("f1_xc", [128, 512], F32)
        hc = sbf("f1_hc", [128, 512], F32)
        gl = sbf("f1_gl", [128, 2 * 512], F32)
        ul = sbf("f1_ul", [128, 2 * 512], F32)
        gsb = sbf("f1_gs", [128, 2 * 512], BF16)
        dl = sbf("f1_dl", [128, 2 * 512], F32)
        psX = ph.enter_context(nc.psum_tensor("f1_psX", [128, 512], F32))
        psG = [ph.enter_context(nc.psum_tensor(f"f1_psG{i}", [128, 512], F32)) for i in range(2)]
        psU = [ph.enter_context(nc.psum_tensor(f"f1_psU{i}", [128, 512], F32)) for i in range(2)]
        psH = ph.enter_context(nc.psum_tensor("f1_psH", [128, 512], F32))
        psD = [ph.enter_context(nc.psum_tensor(f"f1_psD{i}", [128, 512], F32)) for i in range(2)]

        e_s = qsy.reserve(3, 16)
        e_gu, e_qd = {}, {}
        e_xmm, e_gmm, e_umm, e_hmm, e_dmm = {}, {}, {}, {}, {}
        e_xc, e_gli, e_uli, e_hh, e_hc, e_dli = {}, {}, {}, {}, {}, {}
        e_gs = {}
        for th in range(2):
            for mc in range(HC1):
                e_gu[(th, mc)] = qgp.reserve(2, 16)
            for oc in range(FC):
                e_qd[(th, oc)] = qgp.reserve(1, 16)[0]
            e_xmm[th] = c_pe.reserve(1)[0]
            for mc in range(HC1):
                e_gmm[(th, mc)] = c_pe.reserve(1)[0]
                e_umm[(th, mc)] = c_pe.reserve(1)[0]
            e_hmm[th] = c_pe.reserve(1)[0]
            for oc in range(FC):
                e_dmm[(th, oc)] = c_pe.reserve(1)[0]
            e_xc[th] = c_dv.reserve(1)[0]
            for mc in range(HC1):
                e_gli[(th, mc)] = c_dv.reserve(1)[0]
                e_uli[(th, mc)] = c_dv.reserve(1)[0]
                e_hh[(th, mc)] = c_dv.reserve(1)[0]
            e_hc[th] = c_dv.reserve(1)[0]
            for oc in range(FC):
                e_dli[(th, oc)] = c_dv.reserve(2)
            for mc in range(HC1):
                e_gs[(th, mc)] = c_ac.reserve(1)[0]

        with nc.Block() as b:
            @b.sync
            def _(sy):
                qsy.emit(sy.dma_start(out=sg_s[:, :], in_=k.in_sg.ap()), 16)
                qsy.emit(sy.dma_start(out=su_s[:, :], in_=k.in_su.ap()), 16)
                qsy.emit(sy.dma_start(out=sd_s[:, :], in_=k.in_sd.ap()), 16)

            @b.gpsimd
            def _(g):
                for th in range(2):
                    for mc in range(HC1):
                        gidx = th * HC1 + mc
                        if gidx >= 2:
                            pth, pmc = (gidx - 2) // HC1, (gidx - 2) % HC1
                            g.wait_ge(c_pe.h, e_umm[(pth, pmc)])
                        qgp.emit(g.dma_start(out=qg_b[:, (mc % 2) * 2048:(mc % 2 + 1) * 2048],
                                             in_=k.in_qg[mc, :, :]), 16)
                        qgp.emit(g.dma_start(out=qu_b[:, (mc % 2) * 2048:(mc % 2 + 1) * 2048],
                                             in_=k.in_qu[mc, :, :]), 16)
                    for oc in range(FC):
                        gidx = th * FC + oc
                        if gidx >= 2:
                            pth, poc = (gidx - 2) // FC, (gidx - 2) % FC
                            g.wait_ge(c_pe.h, e_dmm[(pth, poc)])
                        qgp.emit(g.dma_start(out=qd_b[:, (oc % 2) * HC1 * 128:(oc % 2 + 1) * HC1 * 128],
                                             in_=k.in_qd[oc, :, :]), 16)

            @b.tensor
            def _(t):
                for th in range(2):
                    tsl = slice(th * 512, (th + 1) * 512)
                    if th >= 1:
                        t.wait_ge(c_dv.h, e_xc[th - 1])
                    for fc in range(FC):
                        _g = t.matmul(psX[:, :], k.ones_bf[:, :],
                                 k.xln[:, fc * TPC + th * 512: fc * TPC + (th + 1) * 512],
                                 start=(fc == 0), stop=(fc == FC - 1))
                    c_pe.emit(_g)
                    for mc in range(HC1):
                        t.wait_ge(qgp.h, e_gu[(th, mc)][1])
                        gidx = th * HC1 + mc
                        if gidx >= 2:
                            pth, pmc = (gidx - 2) // HC1, (gidx - 2) % HC1
                            t.wait_ge(c_dv.h, e_gli[(pth, pmc)])
                        for kc in range(FC):
                            _g = t.matmul(psG[mc % 2][:, :],
                                     qg_b[:, (mc % 2) * 2048 + kc * 128:(mc % 2) * 2048 + (kc + 1) * 128],
                                     k.xln[:, kc * TPC + th * 512: kc * TPC + (th + 1) * 512],
                                     start=(kc == 0), stop=(kc == FC - 1))
                        c_pe.emit(_g)
                        if gidx >= 2:
                            pth, pmc = (gidx - 2) // HC1, (gidx - 2) % HC1
                            t.wait_ge(c_dv.h, e_uli[(pth, pmc)])
                        for kc in range(FC):
                            _g = t.matmul(psU[mc % 2][:, :],
                                     qu_b[:, (mc % 2) * 2048 + kc * 128:(mc % 2) * 2048 + (kc + 1) * 128],
                                     k.xln[:, kc * TPC + th * 512: kc * TPC + (th + 1) * 512],
                                     start=(kc == 0), stop=(kc == FC - 1))
                        c_pe.emit(_g)
                    t.wait_ge(c_dv.h, e_hh[(th, HC1 - 1)])
                    if th >= 1:
                        t.wait_ge(c_dv.h, e_hc[th - 1])
                    for mc in range(HC1):
                        _g = t.matmul(psH[:, :], k.ones_bf[:, :],
                                 h_sb[:, mc * 512:(mc + 1) * 512],
                                 start=(mc == 0), stop=(mc == HC1 - 1))
                    c_pe.emit(_g)
                    for oc in range(FC):
                        t.wait_ge(qgp.h, e_qd[(th, oc)])
                        gidx = th * FC + oc
                        if gidx >= 2:
                            pth, poc = (gidx - 2) // FC, (gidx - 2) % FC
                            t.wait_ge(c_dv.h, e_dli[(pth, poc)][0])
                        for kc in range(HC1):
                            _g = t.matmul(psD[oc % 2][:, :],
                                     qd_b[:, (oc % 2) * HC1 * 128 + kc * 128:
                                          (oc % 2) * HC1 * 128 + (kc + 1) * 128],
                                     h_sb[:, kc * 512:(kc + 1) * 512],
                                     start=(kc == 0), stop=(kc == HC1 - 1))
                        c_pe.emit(_g)

            @b.scalar
            def _(s):
                s.wait_ge(qsy.h, e_s[2])
                for th in range(2):
                    for mc in range(HC1):
                        s.wait_ge(c_dv.h, e_gli[(th, mc)])
                        gidx = th * HC1 + mc
                        if gidx >= 2:
                            pth, pmc = (gidx - 2) // HC1, (gidx - 2) % HC1
                            s.wait_ge(c_dv.h, e_hh[(pth, pmc)])
                        c_ac.emit(s.activation(gsb[:, (mc % 2) * 512:(mc % 2 + 1) * 512],
                                               gl[:, (mc % 2) * 512:(mc % 2 + 1) * 512],
                                               AF.Silu, scale=sg_s[:, mc:mc + 1]))

            @b.vector
            def _(v):
                for th in range(2):
                    v.wait_ge(c_pe.h, e_xmm[th])
                    c_dv.emit(v.tensor_scalar(xc[:, :], psX[:, :], -128.0, None, ALU.mult))
                    for mc in range(HC1):
                        msl = slice((mc % 2) * 512, (mc % 2 + 1) * 512)
                        gidx = th * HC1 + mc
                        if gidx >= 2:
                            pth, pmc = (gidx - 2) // HC1, (gidx - 2) % HC1
                            v.wait_ge(c_ac.h, e_gs[(pth, pmc)])
                        v.wait_ge(c_pe.h, e_gmm[(th, mc)])
                        c_dv.emit(v.tensor_tensor(gl[:, msl], psG[mc % 2][:, :], xc[:, :], ALU.add))
                        v.wait_ge(c_pe.h, e_umm[(th, mc)])
                        c_dv.emit(v.tensor_tensor(ul[:, msl], psU[mc % 2][:, :], xc[:, :], ALU.add))
                        v.wait_ge(c_ac.h, e_gs[(th, mc)])
                        c_dv.emit(v.scalar_tensor_tensor(h_sb[:, mc * 512:(mc + 1) * 512],
                                                         ul[:, msl], su_s[:, mc:mc + 1],
                                                         gsb[:, msl], ALU.mult, ALU.mult))
                    v.wait_ge(c_pe.h, e_hmm[th])
                    c_dv.emit(v.tensor_scalar(hc[:, :], psH[:, :], -128.0, None, ALU.mult))
                    for oc in range(FC):
                        osl = slice((oc % 2) * 512, (oc % 2 + 1) * 512)
                        v.wait_ge(c_pe.h, e_dmm[(th, oc)])
                        c_dv.emit(v.tensor_tensor(dl[:, osl], psD[oc % 2][:, :], hc[:, :], ALU.add))
                        c_dv.emit(v.scalar_tensor_tensor(
                            k.rch(oc, slice(th * 512, (th + 1) * 512)),
                            dl[:, osl], sd_s[:, oc:oc + 1],
                            k.rch(oc, slice(th * 512, (th + 1) * 512)),
                            ALU.mult, ALU.add))


def emit_out(k: K):
    nc = k.nc
    qsy = k.ch("qsy")
    c_pe, c_dv = k.ch("pe"), k.ch("dv")
    with ExitStack() as ph:
        orow = ph.enter_context(nc.sbuf_tensor("orow", [128, 2 * D], F32))
        psT = [ph.enter_context(nc.psum_tensor(f"o_psT{i}", [128, 512], F32)) for i in range(4)]
        NTOT = TCH * FC
        e_tr = c_pe.reserve(NTOT)
        e_cp = c_dv.reserve(NTOT)
        e_st = qsy.reserve(TCH, 16)
        with nc.Block() as b:
            @b.tensor
            def _(t):
                n = 0
                for tc in range(TCH):
                    for fc in range(FC):
                        if n >= 4:
                            t.wait_ge(c_dv.h, e_cp[n - 4])
                        _g = t.transpose(psT[n % 4][:, 0:128],
                                              k.rch(fc, slice(tc * 128, (tc + 1) * 128)),
                                              k.ident_f[:, :])
                        c_pe.emit(_g)
                        n += 1

            @b.vector
            def _(v):
                n = 0
                for tc in range(TCH):
                    if tc >= 2:
                        v.wait_ge(qsy.h, e_st[tc - 2])
                    for fc in range(FC):
                        v.wait_ge(c_pe.h, e_tr[n])
                        c_dv.emit(v.tensor_copy(
                            orow[:, (tc % 2) * D + fc * 128:(tc % 2) * D + (fc + 1) * 128],
                            psT[n % 4][:, 0:128]))
                        n += 1

            @b.sync
            def _(sy):
                for tc in range(TCH):
                    sy.wait_ge(c_dv.h, e_cp[(tc + 1) * FC - 1])
                    qsy.emit(sy.dma_start(out=k.out[tc * 128:(tc + 1) * 128, :],
                                          in_=orow[:, (tc % 2) * D:(tc % 2 + 1) * D]), 16)
                sy.wait_ge(qsy.h, e_st[-1])



def build_program(stop_after="out", debug=False):
    k = K(debug=debug)
    nc = k.nc
    dump = None

    def stop(name, dump_src=None, cast=False):
        nonlocal dump
        if debug and stop_after == name:
            k.barrier()
            if dump_src is not None:
                emit_dump(k, dump_src, cast=cast)
            return True
        return False

    emit_init(k)
    k.barrier()
    emit_collective(k, "AllGather", k.agw_in.ap(), k.agw_out.ap(), "agw")
    k.barrier()

    stk_a = ExitStack()
    k.res = stk_a.enter_context(nc.sbuf_tensor("res_a", [128, FC * TPC], F32))
    k.xln = stk_a.enter_context(nc.sbuf_tensor("xln_a", [128, FC * TPC], BF16))
    emit_load_x(k)
    k.barrier()
    if stop("load", lambda fc: k.rch(fc)):
        k.check_channels()
        return k
    emit_ln(k, "ln0", ag=None if stop_after in ("ln0noag", "ln0nodump") else (k.ag0_in, "ag0", k.ag0_out))
    k.barrier()
    if stop_after == "ln0noag":
        if debug:
            emit_dump(k, lambda fc: k.xch(fc), cast=True)
        k.check_channels()
        return k
    if stop_after == "ln0nodump":
        if debug:
            emit_dump(k, lambda fc: k.rch(fc), cast=False)
        k.check_channels()
        return k
    emit_collective(k, "AllGather", k.ag0_in.ap(), k.ag0_out.ap(), "ag0")
    k.barrier()
    if stop("ln0", lambda fc: k.xch(fc), cast=True):
        k.check_channels()
        return k
    emit_park_res(k)
    k.barrier()
    stk_a.close()
    k.res = k.xln = None

    emit_attn0(k)
    k.barrier()
    emit_collective(k, "AllToAll", k.a2y_in.ap(), k.a2y_out.ap(), "a2y")
    k.barrier()

    stk_b = ExitStack()
    k.res = stk_b.enter_context(nc.sbuf_tensor("res_b", [128, FC * TPC], F32))
    k.xln = stk_b.enter_context(nc.sbuf_tensor("xln_b", [128, FC * TPC], BF16))
    wt32 = stk_b.enter_context(nc.sbuf_tensor("wt32", [NE, TPC], F32))
    emit_attn0_out(k)
    k.barrier()
    if stop("res1", lambda fc: k.rch(fc)):
        k.check_channels()
        return k
    emit_ln(k, "ln1")
    k.barrier()
    emit_router(k, wt32)
    k.barrier()
    emit_moe(k, wt32)
    k.barrier()
    if stop("moe", lambda fc: k.rch(fc)):
        k.check_channels()
        return k
    emit_ln(k, "ln2", ag=(k.ag2_in, "ag2", k.ag2_out))
    k.barrier()
    emit_collective(k, "AllGather", k.ag2_in.ap(), k.ag2_out.ap(), "ag2")
    k.barrier()
    emit_park_res(k)
    k.barrier()
    stk_b.close()
    k.res = k.xln = None

    if debug and stop_after == "a1tb":
        k.dump_a1tb = True
        emit_attn1(k)
        return k
    if debug and stop_after == "a1qk2":
        k.dump_qk_pre = True
        emit_attn1(k)
        return k
    if debug and stop_after == "a1qk":
        k.dump_a1qk = True
        emit_attn1(k)
        k.check_channels()
        return k
    emit_attn1(k)
    k.barrier()
    if debug and stop_after == "a1pre":
        with k.nc.Block() as _b:
            @_b.gpsimd
            def _(g):
                qgp = k.ch("qgp")
                ev = qgp.reserve(4, 16)
                for j in range(4):
                    qgp.emit(g.dma_start(
                        out=k.dbg[j * 512:(j + 1) * 512, :],
                        in_=k.a2ou_in.ap()[j * 512:(j + 1) * 512, :]), 16)
                g.wait_ge(qgp.h, ev[-1])
        k.check_channels()
        return k
    emit_collective(k, "AllToAll", k.a2ou_in.ap(), k.a2ou_out.ap(), "a2ou")
    k.barrier()
    if debug and stop_after == "a1in":
        with k.nc.Block() as _b:
            @_b.gpsimd
            def _(g):
                qgp = k.ch("qgp")
                ev = qgp.reserve(4, 16)
                for j in range(4):
                    qgp.emit(g.dma_start(
                        out=k.dbg[j * 512:(j + 1) * 512, :],
                        in_=k.a2ou_in.ap()[j * 512:(j + 1) * 512, :]), 16)
                g.wait_ge(qgp.h, ev[-1])
        k.check_channels()
        return k
    if debug and stop_after in ("a1o", "a1u"):
        off = 0 if stop_after == "a1o" else 256
        with k.nc.Block() as _b:
            @_b.gpsimd
            def _(g):
                qgp = k.ch("qgp")
                ev = qgp.reserve(FC, 16)
                for fc in range(FC):
                    qgp.emit(g.dma_start(
                        out=k.dbg[fc * 128:(fc + 1) * 128, :],
                        in_=k.a2ou_out.ap()[(fc // 2) * 512 + off + (fc % 2) * 128:
                                            (fc // 2) * 512 + off + (fc % 2) * 128 + 128, :]), 16)
                g.wait_ge(qgp.h, ev[-1])
        k.check_channels()
        return k

    stk_c = ExitStack()
    k.res = stk_c.enter_context(nc.sbuf_tensor("res_c", [128, FC * TPC], F32))
    k.xln = stk_c.enter_context(nc.sbuf_tensor("xln_c", [128, FC * TPC], BF16))
    emit_attn1_out(k)
    k.barrier()
    if stop("res3", lambda fc: k.rch(fc)):
        k.check_channels()
        return k
    emit_ln(k, "ln3")
    k.barrier()
    emit_ffn1(k)
    k.barrier()
    if stop("ffn1", lambda fc: k.rch(fc)):
        k.check_channels()
        return k
    emit_out(k)
    stk_c.close()
    k.check_channels()
    return k


def kernel(**kw):
    in_maps = make_host_inputs(kw)
    k = build_program(stop_after="out", debug=False)
    res = run_bass_kernel_spmd(k.nc, in_maps, core_ids=list(range(NCORE)))
    out = np.concatenate([res.results[c]["out"] for c in range(NCORE)], axis=0)
    return out.reshape(1, TOK, D).astype(np.float32)


# version 02
# speedup vs baseline: 1.0108x; 1.0108x over previous
import os
import numpy as np
from contextlib import ExitStack

os.environ.setdefault("NEURON_SCRATCHPAD_PAGE_SIZE", "512")

import concourse.bass as bass
import concourse.mybir as mybir
from concourse.bass_utils import run_bass_kernel_spmd
from concourse.aot_env import allow_ici_all_to_all

F32 = mybir.dt.float32
BF16 = mybir.dt.bfloat16
U8 = mybir.dt.uint8
AF = mybir.ActivationFunctionType
ALU = mybir.AluOpType
AX = mybir.AxisListType

NCORE = 8
TOK = 8192
D = 2048
TPC = 1024
FC = 16
TCH = 8

HD0 = 256
NE, EPC, DI, MCI = 32, 4, 768, 6
H1, D1, BLK, NB = 16, 128, 256, 32
HID1, HC1 = 5632, 44

LAMBDA_INIT = 0.8
RMS_EPS = 1.1920929e-07
L1_EPS = 1e-6
LN_EPS = 1e-5
INV_SQRT_HALF0 = float(1.0 / np.sqrt(128.0))

GW_TILE = 128 * 2048
DW_TILE = 128 * 768
EXP_ELEMS = MCI * GW_TILE
GW_BASE = 0
UW_BASE = EPC * EXP_ELEMS
DW_BASE = 2 * EPC * EXP_ELEMS
RANK_ELEMS = 3 * EPC * EXP_ELEMS
AGW_ROWS = RANK_ELEMS // 1024



def _pmaj(a):
    K_, M = a.shape
    return np.ascontiguousarray(a.reshape(K_ // 128, 128, M).transpose(1, 0, 2).reshape(128, -1))


def _colblocks(a, bs=128):
    K_, M = a.shape
    kc = K_ // 128
    nb = M // bs
    return np.ascontiguousarray(a.reshape(kc, 128, nb, bs).transpose(2, 1, 0, 3).reshape(nb, 128, kc * bs))


def _u8view(w_int32):
    w = np.ascontiguousarray(np.asarray(w_int32))
    assert w.dtype == np.int32
    return np.ascontiguousarray(w.view(np.uint8).reshape(*w.shape, 4)[..., 0])


def make_host_inputs(kw):
    f = np.float32
    x = np.ascontiguousarray(np.asarray(kw["L0_attn_x"], f).reshape(TOK, D))
    wq = np.asarray(kw["L0_attn_wq"], f)
    wk = np.asarray(kw["L0_attn_wk"], f)
    wv = np.asarray(kw["L0_attn_wv"], f)
    woT_cb = _colblocks(np.ascontiguousarray(np.asarray(kw["L0_attn_wo"], f).T))
    lam = np.concatenate([np.asarray(kw[n], f).reshape(2, 128).T for n in
                          ("L0_attn_lq1", "L0_attn_lk1", "L0_attn_lq2", "L0_attn_lk2")], axis=1)
    lam = np.ascontiguousarray(lam)
    router_pm = _pmaj(np.ascontiguousarray(np.asarray(kw["L0_ffn_router_w"], f).T))

    gw = np.asarray(kw["L0_ffn_gate_w"], f)
    uw = np.asarray(kw["L0_ffn_up_w"], f)
    dw = np.asarray(kw["L0_ffn_down_w"], f)

    wqkvu = np.asarray(kw["L1_attn_W_qkvu"], f)
    slopes = np.asarray(kw["L1_attn_slope_rate"], f).reshape(H1)
    woutT_cb = _colblocks(np.ascontiguousarray(np.asarray(kw["L1_attn_W_out"], f).T))

    qg_cb = _colblocks(np.ascontiguousarray(_u8view(kw["L1_ffn_wq_gate"]).T))
    qu_cb = _colblocks(np.ascontiguousarray(_u8view(kw["L1_ffn_wq_up"]).T))
    qd_cb = _colblocks(np.ascontiguousarray(_u8view(kw["L1_ffn_wq_down"]).T))
    sg = np.ascontiguousarray(np.asarray(kw["L1_ffn_s_gate"], f).reshape(HC1, 128).T)
    su = np.ascontiguousarray(np.asarray(kw["L1_ffn_s_up"], f).reshape(HC1, 128).T)
    sd = np.ascontiguousarray(np.asarray(kw["L1_ffn_s_down"], f).reshape(FC, 128).T)

    ident = np.eye(128, dtype=f)
    masks = np.zeros((128, 4 * 512), f)
    for j in range(4):
        si = np.arange(128)[:, None]
        m = np.arange(512)[None, :]
        masks[:, j * 512:(j + 1) * 512] = (si + 128 * j <= m).astype(f)
    arr_b = np.broadcast_to(np.arange(1, BLK + 1, dtype=f)[None, :], (128, BLK)).copy()
    arr2t = np.stack([255.0 - (np.arange(128) + 128 * j) for j in range(2)], axis=1).astype(f)
    idx2 = np.zeros((128, 2 * BLK), f)
    dmask = np.zeros((128, 2 * BLK), f)
    for j in range(2):
        si = np.arange(128)[:, None] + 128 * j
        m = np.arange(BLK)[None, :]
        v = (m - si).astype(f)
        dmask[:, j * BLK:(j + 1) * BLK] = (v >= 0).astype(f)
        v[v < 0] = 0.0
        idx2[:, j * BLK:(j + 1) * BLK] = v
    c256 = np.full((128, 1), 256.0, f)

    in_maps = []
    for c in range(NCORE):
        mm = {}
        mm["x_slice"] = np.ascontiguousarray(x[c * TPC:(c + 1) * TPC, :])
        mm["l0_wq"] = _pmaj(np.ascontiguousarray(wq[c * HD0:(c + 1) * HD0, :].T))
        mm["l0_wk"] = _pmaj(np.ascontiguousarray(wk[c * HD0:(c + 1) * HD0, :].T))
        mm["l0_wv"] = _pmaj(np.ascontiguousarray(wv[c * HD0:(c + 1) * HD0, :].T))
        mm["l0_wo"] = woT_cb
        mm["l0_lam"] = lam
        mm["l0_router"] = router_pm
        parts = []
        for arr3 in (gw, uw, dw):
            for e in range(EPC * c, EPC * (c + 1)):
                parts.append(_colblocks(np.ascontiguousarray(arr3[e].T)).reshape(-1))
        mm["moe_w"] = np.ascontiguousarray(np.concatenate(parts).reshape(AGW_ROWS, 1024))
        rows = [wqkvu[g * D + HD0 * c: g * D + HD0 * (c + 1), :] for g in range(4)]
        mm["l1_wqkvu"] = _pmaj(np.ascontiguousarray(np.concatenate(rows, axis=0).T))
        mm["l1_slopes"] = np.ascontiguousarray(
            np.broadcast_to(slopes[2 * c:2 * c + 2][None, :], (128, 2)).astype(f))
        mm["l1_wout"] = woutT_cb
        mm["l1_qg"] = qg_cb
        mm["l1_qu"] = qu_cb
        mm["l1_qd"] = qd_cb
        mm["l1_sg"] = sg
        mm["l1_su"] = su
        mm["l1_sd"] = sd
        mm["ident"] = ident
        mm["l0_masks"] = masks
        mm["l1_arr"] = arr_b
        mm["l1_arr2t"] = arr2t
        mm["l1_idx"] = idx2
        mm["l1_dmask"] = dmask
        mm["l1_c256"] = c256
        in_maps.append(mm)
    return in_maps



class SemC:
    def __init__(self, es, nc, name):
        self.h = es.enter_context(nc.semaphore(name))
        self.v = 0
        self.emitted = 0
        self.name = name

    def reserve(self, n=1, step=1):
        vals = [self.v + step * (i + 1) for i in range(n)]
        self.v += step * n
        return vals

    def emit(self, inst, step=1):
        self.emitted += step
        inst.then_inc(self.h, step)
        return inst


class K:
    def __init__(self, debug=False):
        nc = bass.Bass("TRN2", target_bir_lowering=False, debug=False)
        self.nc = nc
        self.debug = debug
        es = ExitStack()
        self.es = es
        dt = nc.dram_tensor
        self.in_x = dt("x_slice", [TPC, D], F32, kind="ExternalInput")
        self.in_wq = dt("l0_wq", [128, FC * HD0], F32, kind="ExternalInput")
        self.in_wk = dt("l0_wk", [128, FC * HD0], F32, kind="ExternalInput")
        self.in_wv = dt("l0_wv", [128, FC * HD0], F32, kind="ExternalInput")
        self.in_wo = dt("l0_wo", [FC, 128, D], F32, kind="ExternalInput")
        self.in_lam = dt("l0_lam", [128, 8], F32, kind="ExternalInput")
        self.in_router = dt("l0_router", [128, FC * NE], F32, kind="ExternalInput")
        self.in_moew = dt("moe_w", [AGW_ROWS, 1024], F32, kind="ExternalInput")
        self.in_l1w = dt("l1_wqkvu", [128, FC * 1024], F32, kind="ExternalInput")
        self.in_slopes = dt("l1_slopes", [128, 2], F32, kind="ExternalInput")
        self.in_wout = dt("l1_wout", [FC, 128, D], F32, kind="ExternalInput")
        self.in_qg = dt("l1_qg", [HC1, 128, FC * 128], U8, kind="ExternalInput")
        self.in_qu = dt("l1_qu", [HC1, 128, FC * 128], U8, kind="ExternalInput")
        self.in_qd = dt("l1_qd", [FC, 128, HC1 * 128], U8, kind="ExternalInput")
        self.in_sg = dt("l1_sg", [128, HC1], F32, kind="ExternalInput")
        self.in_su = dt("l1_su", [128, HC1], F32, kind="ExternalInput")
        self.in_sd = dt("l1_sd", [128, FC], F32, kind="ExternalInput")
        self.in_ident = dt("ident", [128, 128], F32, kind="ExternalInput")
        self.in_masks = dt("l0_masks", [128, 4 * 512], F32, kind="ExternalInput")
        self.in_arr = dt("l1_arr", [128, BLK], F32, kind="ExternalInput")
        self.in_arr2t = dt("l1_arr2t", [128, 2], F32, kind="ExternalInput")
        self.in_idx = dt("l1_idx", [128, 2 * BLK], F32, kind="ExternalInput")
        self.in_dmask = dt("l1_dmask", [128, 2 * BLK], F32, kind="ExternalInput")
        self.in_c256 = dt("l1_c256", [128, 1], F32, kind="ExternalInput")

        self.out = dt("out", [TPC, D], F32, kind="ExternalOutput")
        if debug:
            self.dbg = dt("dbg", [D, TPC], F32, kind="ExternalOutput")

        self.agw_in = dt("agw_in", [AGW_ROWS, 1024], BF16)
        self.agw_out = dt("agw_out", [NCORE * AGW_ROWS, 1024], BF16)
        self.ag0_in = dt("ag0_in", [D, TPC], BF16)
        self.ag0_out = dt("ag0_out", [NCORE * D, TPC], BF16)
        self.a2y_in = dt("a2y_in", [NCORE * HD0, TPC], BF16)
        self.a2y_out = dt("a2y_out", [NCORE * HD0, TPC], BF16)
        self.ag2_in = dt("ag2_in", [D, TPC], BF16)
        self.ag2_out = dt("ag2_out", [NCORE * D, TPC], BF16)
        self.a2ou_in = dt("a2ou_in", [NCORE * 512, TPC], BF16)
        self.a2ou_out = dt("a2ou_out", [NCORE * 512, TPC], BF16)
        self.res_park = dt("res_park", [D, TPC], F32)

        self._ch = {}
        self.bar = {e: SemC(es, nc, f"bar_{e}") for e in
                    ("sync", "tensor", "scalar", "vector", "gpsimd")}
        self.cc = {n: SemC(es, nc, f"cc_{n}") for n in
                   ("agw", "ag0", "a2y", "ag2", "a2ou")}

        sb = lambda name, shape, dtype: es.enter_context(nc.sbuf_tensor(name, shape, dtype))
        self.ones_bf = sb("ones_bf", [128, 128], BF16)
        self.ones_f = sb("ones_f", [128, 128], F32)
        self.ident_f = sb("ident_f", [128, 128], F32)
        self.ident_b = sb("ident_b", [128, 128], BF16)
        self.masks_b = sb("masks_b", [128, 4 * 512], BF16)
        self.neglam = sb("neglam", [128, 1], F32)
        self.scrcol = sb("scrcol", [128, 8], F32)
        self.lamt = sb("lamt", [128, 8], F32)
        self.cst = sb("cst", [128, 6], F32)

        self.res = None
        self.xln = None

    def ch(self, name):
        if name not in self._ch:
            self._ch[name] = SemC(self.es, self.nc, f"ch_{name}")
        return self._ch[name]

    def rch(self, fc, sl=slice(None)):
        return self.res[:, fc * TPC:(fc + 1) * TPC][:, sl]

    def xch(self, fc, sl=slice(None)):
        return self.xln[:, fc * TPC:(fc + 1) * TPC][:, sl]

    def vst(self, v, inst):
        c = self.ch("dvx")
        c.v += 1
        c.emitted += 1
        inst.then_inc(c.h, 1)
        v.wait_ge(c.h, c.v)

    def barrier(self):
        nc = self.nc
        tot = {e: self.bar[e].reserve(1)[0] for e in self.bar}
        qd = [(self._ch[n].h, self._ch[n].v) for n in ("qsy", "qsc", "qgp") if n in self._ch]

        def body(engname):
            def f(e):
                for h, v in qd:
                    e.wait_ge(h, v)
                if engname in ("vector", "scalar", "gpsimd"):
                    self.bar[engname].emit(e.drain())
                else:
                    self.bar[engname].emit(e.nop())
                for en, t in tot.items():
                    if en != engname:
                        e.wait_ge(self.bar[en].h, t)
            return f
        with nc.Block() as b:
            b.sync(body("sync"))
            b.tensor(body("tensor"))
            b.scalar(body("scalar"))
            b.vector(body("vector"))
            b.gpsimd(body("gpsimd"))

    def check_channels(self):
        for c in list(self._ch.values()) + list(self.bar.values()) + list(self.cc.values()):
            if c.v != c.emitted:
                raise RuntimeError(f"channel {c.name}: reserved {c.v} != emitted {c.emitted}")



def emit_init(k: K):
    nc = k.nc
    qgp, qsy = k.ch("qgp"), k.ch("qsy")
    c_dv, c_pe, c_ac = k.ch("dv"), k.ch("pe"), k.ch("ac")
    with ExitStack() as ph:
        psl = ph.enter_context(nc.psum_tensor("ps_init", [128, 512], F32))
        e_w = qgp.reserve(3, 16)
        e_sy = qsy.reserve(2, 16)
        e_prod = c_dv.reserve(1)[0]
        e_mm = c_pe.reserve(1)[0]
        e_sum = c_dv.reserve(1)[0]
        e_exp = c_ac.reserve(1)[0]
        e_neg = c_dv.reserve(1)[0]
        with nc.Block() as b:
            @b.gpsimd
            def _(g):
                qgp.emit(g.dma_start(out=k.agw_in.ap(), in_=k.in_moew.ap()), 16)
                qgp.emit(g.dma_start(out=k.ident_b[:, :], in_=k.in_ident.ap()), 16)
                qgp.emit(g.dma_start(out=k.masks_b[:, :], in_=k.in_masks.ap()), 16)

            @b.sync
            def _(sy):
                qsy.emit(sy.dma_start(out=k.ident_f[:, :], in_=k.in_ident.ap()), 16)
                qsy.emit(sy.dma_start(out=k.lamt[:, :], in_=k.in_lam.ap()), 16)

            @b.vector
            def _(v):
                v.memset(k.ones_bf[:, :], 1.0)
                v.memset(k.ones_f[:, :], 1.0)
                v.memset(k.cst[:, 0:1], 1.0 / D)
                v.memset(k.cst[:, 1:2], 1.0 / 128.0)
                v.memset(k.cst[:, 2:3], LN_EPS)
                v.memset(k.cst[:, 3:4], RMS_EPS)
                v.memset(k.cst[:, 4:5], L1_EPS)
                v.memset(k.cst[:, 5:6], INV_SQRT_HALF0)
                v.wait_ge(qsy.h, e_sy[1])
                v.tensor_mul(k.scrcol[:, 0:1], k.lamt[:, 0:1], k.lamt[:, 2:3])
                v.tensor_mul(k.scrcol[:, 1:2], k.lamt[:, 1:2], k.lamt[:, 3:4])
                v.tensor_mul(k.scrcol[:, 2:3], k.lamt[:, 4:5], k.lamt[:, 6:7])
                c_dv.emit(v.tensor_mul(k.scrcol[:, 3:4], k.lamt[:, 5:6], k.lamt[:, 7:8]))

            @b.tensor
            def _(t):
                t.wait_ge(c_dv.h, e_prod)
                c_pe.emit(t.matmul(psl[:, 0:4], k.ones_f[:, :], k.scrcol[:, 0:4],
                                   start=True, stop=True))

            @b.vector
            def _(v):
                v.wait_ge(c_pe.h, e_mm)
                k.vst(v, v.tensor_copy(k.scrcol[:, 0:4], psl[:, 0:4]))
                v.tensor_add(k.scrcol[:, 4:5], k.scrcol[:, 0:1], k.scrcol[:, 1:2])
                c_dv.emit(v.tensor_add(k.scrcol[:, 5:6], k.scrcol[:, 2:3], k.scrcol[:, 3:4]))

            @b.scalar
            def _(s):
                s.wait_ge(c_dv.h, e_sum)
                s.activation(k.scrcol[:, 6:7], k.scrcol[:, 4:5], AF.Exp)
                c_ac.emit(s.activation(k.scrcol[:, 7:8], k.scrcol[:, 5:6], AF.Exp))

            @b.vector
            def _(v):
                v.wait_ge(c_ac.h, e_exp)
                k.vst(v, v.tensor_scalar(k.scrcol[:, 6:7], k.scrcol[:, 6:7], -1.0, None, ALU.mult))
                k.vst(v, v.tensor_add(k.neglam[:, :], k.scrcol[:, 7:8], k.scrcol[:, 6:7]))
                c_dv.emit(v.tensor_scalar(k.neglam[:, :], k.neglam[:, :], -LAMBDA_INIT, None, ALU.add))


def emit_load_x(k: K):
    nc = k.nc
    qsy, c_pe, c_dv = k.ch("qsy"), k.ch("pe"), k.ch("dv")
    with ExitStack() as ph:
        xin = ph.enter_context(nc.sbuf_tensor("xin", [128, 2 * D], F32))
        psT = [ph.enter_context(nc.psum_tensor(f"psT{i}", [128, 512], F32)) for i in range(4)]
        NTOT = TCH * FC
        e_ld = qsy.reserve(TCH, 16)
        e_tr = c_pe.reserve(NTOT)
        e_cp = c_dv.reserve(NTOT)
        with nc.Block() as b:
            @b.sync
            def _(sy):
                for tc in range(TCH):
                    if tc >= 2:
                        sy.wait_ge(c_pe.h, e_tr[(tc - 1) * FC - 1])
                    qsy.emit(sy.dma_start(out=xin[:, (tc % 2) * D:(tc % 2 + 1) * D],
                                          in_=k.in_x[tc * 128:(tc + 1) * 128, :]), 16)

            @b.tensor
            def _(t):
                n = 0
                for tc in range(TCH):
                    t.wait_ge(qsy.h, e_ld[tc])
                    for fc in range(FC):
                        if n >= 4:
                            t.wait_ge(c_dv.h, e_cp[n - 4])
                        c_pe.emit(t.transpose(
                            psT[n % 4][:, 0:128],
                            xin[:, (tc % 2) * D + fc * 128:(tc % 2) * D + (fc + 1) * 128],
                            k.ident_f[:, :]))
                        n += 1

            @b.vector
            def _(v):
                n = 0
                for tc in range(TCH):
                    for fc in range(FC):
                        v.wait_ge(c_pe.h, e_tr[n])
                        c_dv.emit(v.tensor_copy(k.rch(fc, slice(tc * 128, (tc + 1) * 128)),
                                                psT[n % 4][:, 0:128]))
                        n += 1


def emit_ln(k: K, tag, ag=None):
    nc = k.nc
    qsy = k.ch("qsy")
    c_pe, c_ac, c_dv = k.ch("pe"), k.ch("ac"), k.ch("dv")
    with ExitStack() as ph:
        sq = ph.enter_context(nc.sbuf_tensor(f"ln_sq_{tag}", [128, 2 * TPC], BF16))
        mean = ph.enter_context(nc.sbuf_tensor(f"ln_mean_{tag}", [128, TPC], F32))
        ex2 = ph.enter_context(nc.sbuf_tensor(f"ln_ex2_{tag}", [128, TPC], F32))
        tsc = ph.enter_context(nc.sbuf_tensor(f"ln_tsc_{tag}", [128, TPC], F32))
        rstd = ph.enter_context(nc.sbuf_tensor(f"ln_rstd_{tag}", [128, TPC], F32))
        mrn = ph.enter_context(nc.sbuf_tensor(f"ln_mrn_{tag}", [128, TPC], F32))
        ps_s = ph.enter_context(nc.psum_tensor(f"ln_pss_{tag}", [128, TPC], F32))
        ps_q = ph.enter_context(nc.psum_tensor(f"ln_psq_{tag}", [128, TPC], F32))

        e_sq = c_ac.reserve(FC)
        e_stats = c_ac.reserve(1)[0]
        e_sqrt = c_ac.reserve(1)[0]
        e_xb = c_dv.reserve(FC)
        e_var = c_dv.reserve(1)[0]
        e_norm = c_dv.reserve(FC)
        e_mm = c_pe.reserve(FC)
        if ag is not None:
            e_ag = qsy.reserve(FC, 16)

        with nc.Block() as b:
            @b.scalar
            def _(s):
                for fc in range(FC):
                    if fc >= 2:
                        s.wait_ge(c_pe.h, e_mm[fc - 2])
                    c_ac.emit(s.activation(sq[:, (fc % 2) * TPC:(fc % 2 + 1) * TPC],
                                           k.rch(fc), AF.Square))
                s.wait_ge(c_pe.h, e_mm[-1])
                s.activation(mean[:, :], ps_s[:, :], AF.Copy, scale=k.cst[:, 0:1])
                c_ac.emit(s.activation(ex2[:, :], ps_q[:, :], AF.Copy, scale=k.cst[:, 0:1]))
                s.wait_ge(c_dv.h, e_var)
                c_ac.emit(s.activation(tsc[:, :], tsc[:, :], AF.Sqrt, bias=k.cst[:, 2:3]))

            @b.vector
            def _(v):
                for fc in range(FC):
                    c_dv.emit(v.tensor_copy(k.xch(fc), k.rch(fc)))
                v.wait_ge(c_ac.h, e_stats)
                v.tensor_mul(tsc[:, :], mean[:, :], mean[:, :])
                c_dv.emit(v.scalar_tensor_tensor(tsc[:, :], tsc[:, :], -1.0, ex2[:, :],
                                                 ALU.mult, ALU.add))
                v.wait_ge(c_ac.h, e_sqrt)
                k.vst(v, v.reciprocal(rstd[:, :], tsc[:, :]))
                v.scalar_tensor_tensor(mrn[:, :], mean[:, :], -1.0, rstd[:, :],
                                       ALU.mult, ALU.mult)
                for fc in range(FC):
                    v.tensor_mul(tsc[:, :], k.rch(fc), rstd[:, :])
                    c_dv.emit(v.tensor_add(k.xch(fc), tsc[:, :], mrn[:, :]))

            @b.tensor
            def _(t):
                for fc in range(FC):
                    t.wait_ge(c_ac.h, e_sq[fc])
                    t.wait_ge(c_dv.h, e_xb[fc])
                    st, sp = (fc == 0), (fc == FC - 1)
                    for nh in range(2):
                        sl = slice(nh * 512, (nh + 1) * 512)
                        t.matmul(ps_s[:, sl], k.ones_bf[:, :], k.xch(fc, sl), start=st, stop=sp)
                        _g = t.matmul(ps_q[:, sl], k.ones_bf[:, :],
                                 sq[:, (fc % 2) * TPC:(fc % 2 + 1) * TPC][:, sl], start=st, stop=sp)
                    c_pe.emit(_g)

            if ag is not None:
                ag_in, ccname, ag_outt = ag

                @b.sync
                def _(sy):
                    for fc in range(FC):
                        sy.wait_ge(c_dv.h, e_norm[fc])
                        qsy.emit(sy.dma_start(out=ag_in.ap()[fc * 128:(fc + 1) * 128, :],
                                              in_=k.xch(fc)), 16)


def emit_park_res(k: K):
    nc = k.nc
    qsy = k.ch("qsy")
    ev = qsy.reserve(FC, 16)
    with nc.Block() as b:
        @b.sync
        def _(sy):
            for fc in range(FC):
                qsy.emit(sy.dma_start(out=k.res_park.ap()[fc * 128:(fc + 1) * 128, :],
                                      in_=k.rch(fc)), 16)
            sy.wait_ge(qsy.h, ev[-1])


def emit_restore_res(k: K):
    nc = k.nc
    qsy = k.ch("qsy")
    ev = qsy.reserve(FC, 16)
    with nc.Block() as b:
        @b.sync
        def _(sy):
            for fc in range(FC):
                qsy.emit(sy.dma_start(out=k.rch(fc),
                                      in_=k.res_park.ap()[fc * 128:(fc + 1) * 128, :]), 16)
    return ev[-1]


def emit_dump(k: K, getter, cast=False):
    nc = k.nc
    with nc.Block() as b:
        if cast:
            qgp = k.ch("qgp")
            ev = qgp.reserve(FC, 16)

            @b.gpsimd
            def _(g):
                for fc in range(FC):
                    qgp.emit(g.dma_start(out=k.dbg[fc * 128:(fc + 1) * 128, :],
                                         in_=getter(fc)), 16)
                g.wait_ge(qgp.h, ev[-1])
        else:
            qsy = k.ch("qsy")
            ev = qsy.reserve(FC, 16)

            @b.sync
            def _(sy):
                for fc in range(FC):
                    qsy.emit(sy.dma_start(out=k.dbg[fc * 128:(fc + 1) * 128, :],
                                          in_=getter(fc)), 16)
                sy.wait_ge(qsy.h, ev[-1])


def emit_collective(k: K, kind, in_ap, out_ap, ccname):
    nc = k.nc
    tgt = k.cc[ccname].reserve(1)[0]
    with nc.Block() as b:
        @b.gpsimd
        def _(g):
            with allow_ici_all_to_all():
                k.cc[ccname].emit(g.collective_compute(
                    kind, ALU.bypass,
                    replica_groups=[list(range(NCORE))],
                    ins=[in_ap], outs=[out_ap]))
            g.wait_ge(k.cc[ccname].h, tgt)



def emit_attn0(k: K):
    nc = k.nc
    qsy, qgp = k.ch("qsy"), k.ch("qgp")
    c_pe, c_ac, c_dv = k.ch("pe"), k.ch("ac"), k.ch("dv")
    NTT = 16
    with ExitStack() as ph:
        sbf = lambda n, s, d: ph.enter_context(nc.sbuf_tensor(n, s, d))
        wq_b = sbf("a0_wq", [128, FC * HD0], BF16)
        wk_b = sbf("a0_wk", [128, FC * HD0], BF16)
        wv_b = sbf("a0_wv", [128, FC * HD0], BF16)
        xt = sbf("a0_xt", [128, 24 * 512], BF16)
        q1n = sbf("a0_q1n", [128, TOK], BF16)
        q2n = sbf("a0_q2n", [128, TOK], BF16)
        k1n = sbf("a0_k1n", [128, TOK], BF16)
        k2n = sbf("a0_k2n", [128, TOK], BF16)
        v_tm = sbf("a0_vtm", [128, 64 * HD0], BF16)
        sqs = sbf("a0_sqs", [128, 2 * 512], BF16)
        rst = sbf("a0_rst", [128, 2 * 512], F32)
        pexp = sbf("a0_pexp", [128, 4 * 512], BF16)
        y1b = sbf("a0_y1", [128, 2 * 1024], F32)
        y2b = sbf("a0_y2", [128, 2 * 1024], F32)
        rden = sbf("a0_rden", [128, 2 * 512], F32)
        ytmp = sbf("a0_ytmp", [128, 2 * 1024], BF16)
        ps = [ph.enter_context(nc.psum_tensor(f"a0_ps{i}", [128, 512], F32)) for i in range(8)]

        e_w = qgp.reserve(3, 16)
        e_xt = qsy.reserve(NTT * FC, 16)
        e_proj, e_p2, e_stat = [], [], []
        for nt in range(NTT):
            e_proj.append(c_pe.reserve(FC))
            e_p2.append(c_pe.reserve(FC))
            e_stat.append(c_pe.reserve(4))
        e_sqr = [c_ac.reserve(8) for _ in range(NTT)]
        e_evac = [c_dv.reserve(12) for _ in range(NTT)]

        with nc.Block() as b:
            @b.gpsimd
            def _(g):
                qgp.emit(g.dma_start(out=wq_b[:, :], in_=k.in_wq.ap()), 16)
                qgp.emit(g.dma_start(out=wk_b[:, :], in_=k.in_wk.ap()), 16)
                qgp.emit(g.dma_start(out=wv_b[:, :], in_=k.in_wv.ap()), 16)

            @b.sync
            def _(sy):
                sy.wait_ge(k.cc["ag0"].h, 1)
                n = 0
                for nt in range(NTT):
                    r, co = nt // 2, (nt % 2) * 512
                    for fc in range(FC):
                        if n >= 24:
                            pt = n - 24
                            sy.wait_ge(c_pe.h, e_p2[pt // FC][pt % FC])
                        qsy.emit(sy.dma_start(
                            out=xt[:, (n % 24) * 512:(n % 24 + 1) * 512],
                            in_=k.ag0_out.ap()[r * D + fc * 128: r * D + (fc + 1) * 128,
                                               co:co + 512]), 16)
                        n += 1

            @b.tensor
            def _(t):
                t.wait_ge(qgp.h, e_w[2])
                for nt in range(NTT):
                    if nt >= 1:
                        t.wait_ge(c_dv.h, e_evac[nt - 1][-1])
                    for fc in range(FC):
                        n = nt * FC + fc
                        t.wait_ge(qsy.h, e_xt[n])
                        xs = xt[:, (n % 24) * 512:(n % 24 + 1) * 512]
                        st, sp = (fc == 0), (fc == FC - 1)
                        t.matmul(ps[0][:, :], wq_b[:, fc * HD0:fc * HD0 + 128], xs, start=st, stop=sp)
                        t.matmul(ps[1][:, :], wq_b[:, fc * HD0 + 128:fc * HD0 + 256], xs, start=st, stop=sp)
                        t.matmul(ps[2][:, :], wk_b[:, fc * HD0:fc * HD0 + 128], xs, start=st, stop=sp)
                        t.matmul(ps[3][:, :], wk_b[:, fc * HD0 + 128:fc * HD0 + 256], xs, start=st, stop=sp)
                        for tc in range(2):
                            _g = t.matmul(ps[4 + tc][:, 0:256],
                                          xs[:, tc * 128:(tc + 1) * 128],
                                          wv_b[:, fc * HD0:(fc + 1) * HD0], start=st, stop=sp)
                        c_pe.emit(_g)
                    t.wait_ge(c_dv.h, e_evac[nt][1])
                    for fc in range(FC):
                        n = nt * FC + fc
                        xs = xt[:, (n % 24) * 512:(n % 24 + 1) * 512]
                        st, sp = (fc == 0), (fc == FC - 1)
                        for tc in range(2, 4):
                            _g = t.matmul(ps[4 + tc - 2][:, 0:256],
                                          xs[:, tc * 128:(tc + 1) * 128],
                                          wv_b[:, fc * HD0:(fc + 1) * HD0], start=st, stop=sp)
                        c_pe.emit(_g)
                    for part in range(4):
                        t.wait_ge(c_ac.h, e_sqr[nt][2 * part])
                        _g = t.matmul(ps[6 + part % 2][:, :], k.ones_bf[:, :],
                                           sqs[:, (part % 2) * 512:(part % 2 + 1) * 512],
                                           start=True, stop=True)
                        c_pe.emit(_g)

            @b.scalar
            def _(s):
                for nt in range(NTT):
                    s.wait_ge(c_pe.h, e_proj[nt][FC - 1])
                    for part in range(4):
                        if part >= 2:
                            s.wait_ge(c_pe.h, e_stat[nt][part - 2])
                            s.wait_ge(c_dv.h, e_evac[nt][2 + 2 * (part - 2) + 1])
                        c_ac.emit(s.activation(sqs[:, (part % 2) * 512:(part % 2 + 1) * 512],
                                               ps[part][:, :], AF.Square))
                        s.wait_ge(c_pe.h, e_stat[nt][part])
                        c_ac.emit(s.activation(rst[:, (part % 2) * 512:(part % 2 + 1) * 512],
                                               ps[6 + part % 2][:, :], AF.Sqrt,
                                               scale=k.cst[:, 1:2], bias=k.cst[:, 3:4]))

            @b.vector
            def _(v):
                dsts = [q1n, q2n, k1n, k2n]
                for nt in range(NTT):
                    for tc in range(2):
                        v.wait_ge(c_pe.h, e_proj[nt][FC - 1])
                        c_dv.emit(v.tensor_copy(
                            v_tm[:, (nt * 4 + tc) * 256:(nt * 4 + tc + 1) * 256],
                            ps[4 + tc][:, 0:256]))
                    for part in range(4):
                        v.wait_ge(c_ac.h, e_sqr[nt][2 * part + 1])
                        sl = slice((part % 2) * 512, (part % 2 + 1) * 512)
                        c_dv.emit(v.reciprocal(rst[:, sl], rst[:, sl]))
                        v.wait_ge(c_dv.h, e_evac[nt][2 + 2 * part])
                        c_dv.emit(v.tensor_mul(dsts[part][:, nt * 512:(nt + 1) * 512],
                                               ps[part][:, :], rst[:, sl]))
                    for tc in range(2, 4):
                        v.wait_ge(c_pe.h, e_p2[nt][FC - 1])
                        c_dv.emit(v.tensor_copy(
                            v_tm[:, (nt * 4 + tc) * 256:(nt * 4 + tc + 1) * 256],
                            ps[4 + tc - 2][:, 0:256]))

        plan = []
        for mt in range(NTT):
            for a in range(2):
                plan.append((mt, a, 4 * (mt + 1)))

        e_sc, e_av, e_exp, e_msk, e_fin, e_cmb, e_ydma = {}, {}, {}, {}, {}, {}, {}
        for (mt, a, smax) in plan:
            scl_, avl = [None] * smax, [None] * smax
            scl_[0] = c_pe.reserve(1)[0]
            for sc in range(1, smax):
                scl_[sc] = c_pe.reserve(1)[0]
                avl[sc - 1] = c_pe.reserve(1)[0]
            avl[smax - 1] = c_pe.reserve(1)[0]
            e_sc[(mt, a)] = scl_
            e_av[(mt, a)] = avl
            e_exp[(mt, a)] = c_ac.reserve(smax)
            e_msk[(mt, a)] = c_dv.reserve(4)
            e_fin[(mt, a)] = c_dv.reserve(3)
            if a == 1:
                e_cmb[mt] = c_dv.reserve(2)
        for mt in range(NTT):
            e_ydma[mt] = qsy.reserve(2, 16)

        with nc.Block() as b:
            @b.tensor
            def _(t):
                t.wait_ge(c_dv.h, e_evac[NTT - 1][-1])
                for (mt, a, smax) in plan:
                    qn = q1n if a == 0 else q2n
                    kn = k1n if a == 0 else k2n
                    if (mt, a) != (0, 0):
                        pm, pa = (mt, 0) if a == 1 else (mt - 1, 1)
                        t.wait_ge(c_dv.h, e_fin[(pm, pa)][2])

                    def av(sc):
                        j = sc - 4 * mt
                        if j >= 0:
                            t.wait_ge(c_dv.h, e_msk[(mt, a)][j])
                        else:
                            t.wait_ge(c_ac.h, e_exp[(mt, a)][sc])
                        pslot = pexp[:, (sc % 4) * 512:(sc % 4 + 1) * 512]
                        st, sp = (sc == 0), (sc == smax - 1)
                        t.matmul(ps[2][:, :], v_tm[:, sc * HD0:sc * HD0 + 128], pslot,
                                 start=st, stop=sp)
                        t.matmul(ps[3][:, :], v_tm[:, sc * HD0 + 128:sc * HD0 + 256], pslot,
                                 start=st, stop=sp)
                        _g = t.matmul(ps[4][:, :], k.ones_bf[:, :], pslot, start=st, stop=sp)
                        c_pe.emit(_g)

                    for sc in range(smax):
                        if sc >= 2:
                            t.wait_ge(c_ac.h, e_exp[(mt, a)][sc - 2])
                        _g = t.matmul(ps[sc % 2][:, :], kn[:, sc * 128:(sc + 1) * 128],
                                           qn[:, mt * 512:(mt + 1) * 512], start=True, stop=True)
                        c_pe.emit(_g)
                        if sc >= 1:
                            av(sc - 1)
                    av(smax - 1)

            @b.scalar
            def _(s):
                for (mt, a, smax) in plan:
                    for sc in range(smax):
                        s.wait_ge(c_pe.h, e_sc[(mt, a)][sc])
                        if sc >= 4:
                            s.wait_ge(c_pe.h, e_av[(mt, a)][sc - 4])
                        c_ac.emit(s.activation(pexp[:, (sc % 4) * 512:(sc % 4 + 1) * 512],
                                               ps[sc % 2][:, :], AF.Exp, scale=k.cst[:, 5:6]))

            @b.vector
            def _(v):
                for (mt, a, smax) in plan:
                    for j in range(4):
                        sc = 4 * mt + j
                        v.wait_ge(c_ac.h, e_exp[(mt, a)][sc])
                        sl = slice((sc % 4) * 512, (sc % 4 + 1) * 512)
                        c_dv.emit(v.tensor_mul(pexp[:, sl], pexp[:, sl],
                                               k.masks_b[:, j * 512:(j + 1) * 512]))
                    v.wait_ge(c_pe.h, e_av[(mt, a)][smax - 1])
                    ybuf = y1b if a == 0 else y2b
                    rsl = slice(a * 512, (a + 1) * 512)
                    c_dv.emit(v.reciprocal(rden[:, rsl], ps[4][:, :]))
                    mo = (mt % 2) * 1024
                    c_dv.emit(v.tensor_mul(ybuf[:, mo:mo + 512], ps[2][:, :], rden[:, rsl]))
                    c_dv.emit(v.tensor_mul(ybuf[:, mo + 512:mo + 1024], ps[3][:, :], rden[:, rsl]))
                    if a == 1:
                        if mt >= 2:
                            v.wait_ge(qsy.h, e_ydma[mt - 2][1])
                        for e in range(2):
                            c_dv.emit(v.scalar_tensor_tensor(
                                ytmp[:, mo + e * 512:mo + (e + 1) * 512],
                                y2b[:, mo + e * 512:mo + (e + 1) * 512],
                                k.neglam[:, :],
                                y1b[:, mo + e * 512:mo + (e + 1) * 512],
                                ALU.mult, ALU.add))

            @b.sync
            def _(sy):
                for mt in range(NTT):
                    sy.wait_ge(c_dv.h, e_cmb[mt][1])
                    mo = (mt % 2) * 1024
                    r, co = mt // 2, (mt % 2) * 512
                    for e in range(2):
                        qsy.emit(sy.dma_start(
                            out=k.a2y_in.ap()[r * HD0 + e * 128: r * HD0 + (e + 1) * 128,
                                              co:co + 512],
                            in_=ytmp[:, mo + e * 512:mo + (e + 1) * 512]), 16)



def emit_attn0_out(k: K):
    nc = k.nc
    qsy, qgp = k.ch("qsy"), k.ch("qgp")
    c_pe, c_dv = k.ch("pe"), k.ch("dv")
    e_restore = emit_restore_res(k)
    with ExitStack() as ph:
        ys = ph.enter_context(nc.sbuf_tensor("a0o_ys", [128, FC * TPC], BF16))
        wo_b = ph.enter_context(nc.sbuf_tensor("a0o_wo", [128, 2 * D], BF16))
        psA = [ph.enter_context(nc.psum_tensor(f"a0o_ps{i}", [128, 512], F32)) for i in range(2)]
        e_ys = qsy.reserve(FC, 16)
        e_wo = qgp.reserve(FC, 16)
        e_mm = c_pe.reserve(FC * 2)
        e_add = c_dv.reserve(FC * 2)
        with nc.Block() as b:
            @b.sync
            def _(sy):
                sy.wait_ge(k.cc["a2y"].h, 1)
                for fc in range(FC):
                    qsy.emit(sy.dma_start(out=ys[:, fc * TPC:(fc + 1) * TPC],
                                          in_=k.a2y_out.ap()[fc * 128:(fc + 1) * 128, :]), 16)

            @b.gpsimd
            def _(g):
                for oc in range(FC):
                    if oc >= 2:
                        g.wait_ge(c_pe.h, e_mm[2 * (oc - 1) - 1])
                    qgp.emit(g.dma_start(out=wo_b[:, (oc % 2) * D:(oc % 2 + 1) * D],
                                         in_=k.in_wo[oc, :, :]), 16)

            @b.tensor
            def _(t):
                t.wait_ge(qsy.h, e_ys[-1])
                n = 0
                for oc in range(FC):
                    t.wait_ge(qgp.h, e_wo[oc])
                    for th in range(2):
                        if n >= 2:
                            t.wait_ge(c_dv.h, e_add[n - 2])
                        for kc in range(FC):
                            _g = t.matmul(psA[n % 2][:, :],
                                     wo_b[:, (oc % 2) * D + kc * 128:(oc % 2) * D + (kc + 1) * 128],
                                     ys[:, kc * TPC + th * 512: kc * TPC + (th + 1) * 512],
                                     start=(kc == 0), stop=(kc == FC - 1))
                        c_pe.emit(_g)
                        n += 1

            @b.vector
            def _(v):
                v.wait_ge(qsy.h, e_restore)
                n = 0
                for oc in range(FC):
                    for th in range(2):
                        v.wait_ge(c_pe.h, e_mm[n])
                        sl = slice(th * 512, (th + 1) * 512)
                        c_dv.emit(v.scalar_tensor_tensor(
                            k.rch(oc, sl), psA[n % 2][:, :], 1.0 - LAMBDA_INIT,
                            k.rch(oc, sl), ALU.mult, ALU.add))
                        n += 1


def emit_router(k: K, wt32):
    nc = k.nc
    qgp = k.ch("qgp")
    c_pe, c_ac, c_dv = k.ch("pe"), k.ch("ac"), k.ch("dv")
    with ExitStack() as ph:
        rt_b = ph.enter_context(nc.sbuf_tensor("rt_b", [128, FC * NE], BF16))
        lgt = ph.enter_context(nc.sbuf_tensor("rt_lgt", [128, NE], F32))
        prb = ph.enter_context(nc.sbuf_tensor("rt_prb", [128, NE], F32))
        tpr = ph.enter_context(nc.sbuf_tensor("rt_tpr", [128, NE], F32))
        wtt = ph.enter_context(nc.sbuf_tensor("rt_wtt", [128, NE], F32))
        msk = ph.enter_context(nc.sbuf_tensor("rt_msk", [128, NE], F32))
        mcol = ph.enter_context(nc.sbuf_tensor("rt_mcol", [128, 1], F32))
        psL = ph.enter_context(nc.psum_tensor("rt_psL", [128, 512], F32))
        psT = ph.enter_context(nc.psum_tensor("rt_psT", [128, 512], F32))
        e_w = qgp.reserve(1, 16)
        e_lg, e_tr = [], []
        for tc in range(TCH):
            e_lg.append(c_pe.reserve(1)[0])
            e_tr.append(c_pe.reserve(1)[0])
        e_lgfree, e_shift, e_wt, e_cp = [], [], [], []
        for tc in range(TCH):
            e_lgfree.append(c_dv.reserve(1)[0])
            e_shift.append(c_dv.reserve(1)[0])
            e_wt.append(c_dv.reserve(1)[0])
            e_cp.append(c_dv.reserve(1)[0])
        e_ex = [c_ac.reserve(1)[0] for _ in range(TCH)]
        with nc.Block() as b:
            @b.gpsimd
            def _(g):
                qgp.emit(g.dma_start(out=rt_b[:, :], in_=k.in_router.ap()), 16)

            @b.tensor
            def _(t):
                t.wait_ge(qgp.h, e_w[0])
                for tc in range(TCH):
                    if tc >= 1:
                        t.wait_ge(c_dv.h, e_lgfree[tc - 1])
                        t.wait_ge(c_dv.h, e_cp[tc - 1])
                    for fc in range(FC):
                        _g = t.matmul(psL[:, 0:NE],
                                 k.xln[:, fc * TPC + tc * 128: fc * TPC + (tc + 1) * 128],
                                 rt_b[:, fc * NE:(fc + 1) * NE],
                                 start=(fc == 0), stop=(fc == FC - 1))
                    c_pe.emit(_g)
                    t.wait_ge(c_dv.h, e_wt[tc])
                    _g = t.transpose(psT[0:NE, 0:128], wtt[:, :], k.ident_f[:, :])
                    c_pe.emit(_g)

            @b.scalar
            def _(s):
                for tc in range(TCH):
                    s.wait_ge(c_dv.h, e_shift[tc])
                    c_ac.emit(s.activation(prb[:, :], lgt[:, :], AF.Exp))

            @b.vector
            def _(v):
                for tc in range(TCH):
                    v.wait_ge(c_pe.h, e_lg[tc])
                    c_dv.emit(v.tensor_copy(lgt[:, :], psL[:, 0:NE]))
                    k.vst(v, v.reduce_max(mcol[:, :], lgt[:, :], AX.X, negate=True))
                    c_dv.emit(v.tensor_single_scalar(lgt[:, :], lgt[:, :], mcol[:, :], ALU.add))
                    v.wait_ge(c_ac.h, e_ex[tc])
                    k.vst(v, v.reduce_sum(mcol[:, :], prb[:, :], AX.X))
                    k.vst(v, v.reciprocal(mcol[:, :], mcol[:, :]))
                    k.vst(v, v.tensor_single_scalar(prb[:, :], prb[:, :], mcol[:, :], ALU.mult))
                    k.vst(v, v.tensor_copy(tpr[:, :], prb[:, :]))
                    for _ in range(3):
                        k.vst(v, v.reduce_max(mcol[:, :], tpr[:, :], AX.X))
                        k.vst(v, v.tensor_single_scalar(msk[:, :], tpr[:, :], mcol[:, :], ALU.is_ge))
                        k.vst(v, v.scalar_tensor_tensor(tpr[:, :], msk[:, :], -1e30, tpr[:, :],
                                                        ALU.mult, ALU.add))
                    k.vst(v, v.reduce_max(mcol[:, :], tpr[:, :], AX.X))
                    k.vst(v, v.tensor_single_scalar(msk[:, :], prb[:, :], mcol[:, :], ALU.is_ge))
                    k.vst(v, v.tensor_mul(wtt[:, :], prb[:, :], msk[:, :]))
                    k.vst(v, v.reduce_sum(mcol[:, :], wtt[:, :], AX.X))
                    k.vst(v, v.reciprocal(mcol[:, :], mcol[:, :]))
                    c_dv.emit(v.tensor_single_scalar(wtt[:, :], wtt[:, :], mcol[:, :], ALU.mult))
                    v.wait_ge(c_pe.h, e_tr[tc])
                    c_dv.emit(v.tensor_copy(wt32[0:NE, tc * 128:(tc + 1) * 128],
                                            psT[0:NE, 0:128]))


def emit_moe(k: K, wt32):
    nc = k.nc
    qsy = k.ch("qsy")
    c_pe, c_ac, c_dv = k.ch("pe"), k.ch("ac"), k.ch("dv")
    agw = k.agw_out.ap().rearrange("r c -> (r c)")
    with ExitStack() as ph:
        sbf = lambda n, s, d: ph.enter_context(nc.sbuf_tensor(n, s, d))
        gw_t = sbf("moe_gw", [128, 2 * 2048], BF16)
        uw_t = sbf("moe_uw", [128, 2 * 2048], BF16)
        dw_t = sbf("moe_dw", [128, 2 * 768], BF16)
        wrow = sbf("moe_wrow", [1, 2 * TPC], F32)
        h_sb = sbf("moe_h", [128, MCI * TPC], BF16)
        gs = sbf("moe_gs", [128, TPC], BF16)
        tts = sbf("moe_tts", [128, TPC], F32)
        wtb = sbf("moe_wtb", [128, TPC], F32)
        psG = ph.enter_context(nc.psum_tensor("moe_psG", [128, TPC], F32))
        psU = ph.enter_context(nc.psum_tensor("moe_psU", [128, TPC], F32))
        psW = ph.enter_context(nc.psum_tensor("moe_psW", [128, TPC], F32))
        psD = [ph.enter_context(nc.psum_tensor(f"moe_psD{i}", [128, 512], F32)) for i in range(2)]

        def gw_off(e, mc):
            return (e // EPC) * RANK_ELEMS + GW_BASE + (e % EPC) * EXP_ELEMS + mc * GW_TILE

        def uw_off(e, mc):
            return (e // EPC) * RANK_ELEMS + UW_BASE + (e % EPC) * EXP_ELEMS + mc * GW_TILE

        def dw_off(e, oc):
            return (e // EPC) * RANK_ELEMS + DW_BASE + (e % EPC) * EXP_ELEMS + oc * DW_TILE

        qgp = k.ch("qgp")
        e_gu, e_dw = [], []
        e_wtb, e_gmm, e_umm, e_dmm = [], [], [], []
        e_sil = []
        e_wtbc, e_h, e_dadd = [], [], []
        e_wrow = [qgp.reserve(1, 16)[0] for _ in range(NE)]
        for e in range(NE):
            e_gu.append([qsy.reserve(2, 16) for _ in range(MCI)])
            e_dw.append([qsy.reserve(1, 16)[0] for _ in range(FC)])
            e_wtb.append(c_pe.reserve(1)[0])
            gm, um = [], []
            for mc in range(MCI):
                gm.append(c_pe.reserve(2))
                um.append(c_pe.reserve(2))
            e_gmm.append(gm)
            e_umm.append(um)
            e_dmm.append([c_pe.reserve(2) for _ in range(FC)])
            e_sil.append([c_ac.reserve(2) for _ in range(MCI)])
            e_wtbc.append(c_dv.reserve(1)[0])
            e_h.append([c_dv.reserve(2) for _ in range(MCI)])
            e_dadd.append([c_dv.reserve(2) for _ in range(FC)])

        with nc.Block() as b:
            @b.gpsimd
            def _(g):
                for e in range(NE):
                    if e >= 2:
                        g.wait_ge(c_pe.h, e_wtb[e - 2])
                    qgp.emit(g.dma_start(out=wrow[0:1, (e % 2) * TPC:(e % 2 + 1) * TPC],
                                         in_=wt32[e:e + 1, :]), 16)

            @b.sync
            def _(sy):
                sy.wait_ge(k.cc["agw"].h, 1)
                for e in range(NE):
                    for mc in range(MCI):
                        g = e * MCI + mc
                        if g >= 2:
                            pe_, pmc = (g - 2) // MCI, (g - 2) % MCI
                            sy.wait_ge(c_pe.h, e_umm[pe_][pmc][1])
                        o = gw_off(e, mc)
                        qsy.emit(sy.dma_start(
                            out=gw_t[:, (g % 2) * 2048:(g % 2 + 1) * 2048],
                            in_=agw[o:o + GW_TILE].rearrange("(p f) -> p f", p=128)), 16)
                        o = uw_off(e, mc)
                        qsy.emit(sy.dma_start(
                            out=uw_t[:, (g % 2) * 2048:(g % 2 + 1) * 2048],
                            in_=agw[o:o + GW_TILE].rearrange("(p f) -> p f", p=128)), 16)
                    for oc in range(FC):
                        g = e * FC + oc
                        if g >= 2:
                            pe_, poc = (g - 2) // FC, (g - 2) % FC
                            sy.wait_ge(c_pe.h, e_dmm[pe_][poc][1])
                        o = dw_off(e, oc)
                        qsy.emit(sy.dma_start(
                            out=dw_t[:, (g % 2) * 768:(g % 2 + 1) * 768],
                            in_=agw[o:o + DW_TILE].rearrange("(p f) -> p f", p=128)), 16)

            @b.scalar
            def _(s):
                for e in range(NE):
                    for mc in range(MCI):
                        for th in range(2):
                            s.wait_ge(c_pe.h, e_gmm[e][mc][th])
                            if mc >= 1:
                                s.wait_ge(c_dv.h, e_h[e][mc - 1][th])
                            elif e >= 1:
                                s.wait_ge(c_dv.h, e_h[e - 1][MCI - 1][th])
                            c_ac.emit(s.activation(gs[:, th * 512:(th + 1) * 512],
                                                   psG[:, th * 512:(th + 1) * 512], AF.Silu))

            @b.tensor
            def _(t):
                for e in range(NE):
                    if e >= 1:
                        t.wait_ge(c_dv.h, e_wtbc[e - 1])
                    t.wait_ge(qgp.h, e_wrow[e])
                    wsl = wrow[0:1, (e % 2) * TPC:(e % 2 + 1) * TPC]
                    t.matmul(psW[:, 0:512], k.ones_f[0:1, :], wsl[0:1, 0:512],
                             start=True, stop=True)
                    _g = t.matmul(psW[:, 512:1024], k.ones_f[0:1, :], wsl[0:1, 512:1024],
                                       start=True, stop=True)
                    c_pe.emit(_g)
                    for mc in range(MCI):
                        g = e * MCI + mc
                        t.wait_ge(qsy.h, e_gu[e][mc][1])
                        if g >= 1:
                            pe_, pmc = (g - 1) // MCI, (g - 1) % MCI
                            t.wait_ge(c_ac.h, e_sil[pe_][pmc][1])
                            t.wait_ge(c_dv.h, e_h[pe_][pmc][1])
                        for th in range(2):
                            for kc in range(FC):
                                _g = t.matmul(psG[:, th * 512:(th + 1) * 512],
                                         gw_t[:, (g % 2) * 2048 + kc * 128:(g % 2) * 2048 + (kc + 1) * 128],
                                         k.xln[:, kc * TPC + th * 512: kc * TPC + (th + 1) * 512],
                                         start=(kc == 0), stop=(kc == FC - 1))
                            c_pe.emit(_g)
                        for th in range(2):
                            for kc in range(FC):
                                _g = t.matmul(psU[:, th * 512:(th + 1) * 512],
                                         uw_t[:, (g % 2) * 2048 + kc * 128:(g % 2) * 2048 + (kc + 1) * 128],
                                         k.xln[:, kc * TPC + th * 512: kc * TPC + (th + 1) * 512],
                                         start=(kc == 0), stop=(kc == FC - 1))
                            c_pe.emit(_g)
                    t.wait_ge(c_dv.h, e_h[e][MCI - 1][1])
                    for oc in range(FC):
                        g = e * FC + oc
                        t.wait_ge(qsy.h, e_dw[e][oc])
                        for th in range(2):
                            if g >= 1:
                                pe_, poc = (g - 1) // FC, (g - 1) % FC
                                t.wait_ge(c_dv.h, e_dadd[pe_][poc][th])
                            for kc in range(MCI):
                                _g = t.matmul(psD[th][:, :],
                                         dw_t[:, (g % 2) * 768 + kc * 128:(g % 2) * 768 + (kc + 1) * 128],
                                         h_sb[:, kc * TPC + th * 512: kc * TPC + (th + 1) * 512],
                                         start=(kc == 0), stop=(kc == MCI - 1))
                            c_pe.emit(_g)

            @b.vector
            def _(v):
                for e in range(NE):
                    v.wait_ge(c_pe.h, e_wtb[e])
                    c_dv.emit(v.tensor_copy(wtb[:, :], psW[:, :]))
                    for mc in range(MCI):
                        for th in range(2):
                            sl = slice(th * 512, (th + 1) * 512)
                            v.wait_ge(c_pe.h, e_umm[e][mc][th])
                            v.tensor_mul(tts[:, sl], psU[:, sl], wtb[:, sl])
                            v.wait_ge(c_ac.h, e_sil[e][mc][th])
                            c_dv.emit(v.tensor_mul(
                                h_sb[:, mc * TPC + th * 512: mc * TPC + (th + 1) * 512],
                                tts[:, sl], gs[:, sl]))
                    for oc in range(FC):
                        for th in range(2):
                            v.wait_ge(c_pe.h, e_dmm[e][oc][th])
                            sl = slice(th * 512, (th + 1) * 512)
                            c_dv.emit(v.tensor_tensor(k.rch(oc, sl), psD[th][:, :],
                                                      k.rch(oc, sl), ALU.add))


def emit_attn1(k: K):
    nc = k.nc
    qsy, qgp, qsc = k.ch("qsy"), k.ch("qgp"), k.ch("qsc")
    c_pe, c_ac, c_dv = k.ch("pe"), k.ch("ac"), k.ch("dv")
    NTT = 16
    with ExitStack() as ph:
        sbf = lambda n, s, d: ph.enter_context(nc.sbuf_tensor(n, s, d))
        w_b = sbf("a1_w", [128, FC * 1024], BF16)
        xt = sbf("a1_xt", [128, 24 * 512], BF16)
        qsb = [sbf(f"a1_q{h}", [128, TOK], BF16) for h in range(2)]
        ksb = [sbf(f"a1_k{h}", [128, TOK], BF16) for h in range(2)]
        v_tm = sbf("a1_vtm", [128, 64 * 256], BF16)
        ubt = sbf("a1_ubt", [128, 2 * 1024], BF16)
        slp = sbf("a1_slp", [128, 2], F32)
        nsc = sbf("a1_nsc", [128, 2], F32)
        arrs = sbf("a1_arrs", [128, BLK], F32)
        arr2 = sbf("a1_arr2", [128, 2], F32)
        idxs = sbf("a1_idxs", [128, 2 * BLK], F32)
        dmsk = sbf("a1_dmsk", [128, 2 * BLK], F32)
        cc256 = sbf("a1_c256", [128, 1], F32)
        qdec = [sbf(f"a1_qdec{h}", [128, BLK], F32) for h in range(2)]
        kdec = [sbf(f"a1_kdec{h}", [128, 2], F32) for h in range(2)]
        ddt = [sbf(f"a1_ddt{h}", [128, 2 * BLK], F32) for h in range(2)]
        bdc = [sbf(f"a1_bdc{h}", [128, 1], F32) for h in range(2)]
        kv = [sbf(f"a1_kv{i}", [128, 128], F32) for i in range(2)]
        kv_b = sbf("a1_kvb", [128, 128], BF16)
        qd_t = sbf("a1_qdt", [128, BLK], BF16)
        tnd = sbf("a1_tnd", [128, BLK], F32)
        dumm = sbf("a1_dumm", [1, 8], F32)
        pdd = sbf("a1_pdd", [128, 2 * BLK], BF16)
        ktm = sbf("a1_ktm", [128, 2 * 128], BF16)
        ot = sbf("a1_ot", [128, 4 * BLK], BF16)

        e_w = qgp.reserve(4, 16)
        e_tab = qsy.reserve(6, 16)
        e_xt = qsy.reserve(NTT * FC + 4, 16)
        e_ns = c_dv.reserve(1)[0]
        e_ddm = c_dv.reserve(2)
        e_tbl = c_ac.reserve(8)
        e_proj, e_p2 = [], []
        for nt in range(NTT):
            e_proj.append(c_pe.reserve(FC))
            e_p2.append(c_pe.reserve(FC))
        e_sil = [c_ac.reserve(4) for _ in range(NTT)]
        e_evac = [c_dv.reserve(6) for _ in range(NTT)]
        e_udma = [qsc.reserve(2, 16) for _ in range(NTT)]

        with ExitStack() as pp:
            ps = [pp.enter_context(nc.psum_tensor(f"a1p_ps{i}", [128, 512], F32)) for i in range(8)]
            with nc.Block() as b:
                @b.gpsimd
                def _(g):
                    for i in range(4):
                        qgp.emit(g.dma_start(out=w_b[:, i * 4096:(i + 1) * 4096],
                                             in_=k.in_l1w[:, i * 4096:(i + 1) * 4096]), 16)

                @b.sync
                def _(sy):
                    qsy.emit(sy.dma_start(out=slp[:, :], in_=k.in_slopes.ap()), 16)
                    qsy.emit(sy.dma_start(out=arrs[:, :], in_=k.in_arr.ap()), 16)
                    qsy.emit(sy.dma_start(out=arr2[:, :], in_=k.in_arr2t.ap()), 16)
                    qsy.emit(sy.dma_start(out=idxs[:, :], in_=k.in_idx.ap()), 16)
                    qsy.emit(sy.dma_start(out=cc256[:, :], in_=k.in_c256.ap()), 16)
                    qsy.emit(sy.dma_start(out=dmsk[:, :], in_=k.in_dmask.ap()), 16)
                    sy.wait_ge(k.cc["ag2"].h, 1)
                    n = 0
                    for nt in range(NTT):
                        r, co = nt // 2, (nt % 2) * 512
                        for fc in range(FC):
                            if n >= 24:
                                pt = n - 24
                                sy.wait_ge(c_pe.h, e_p2[pt // FC][pt % FC])
                            qsy.emit(sy.dma_start(
                                out=xt[:, (n % 24) * 512:(n % 24 + 1) * 512],
                                in_=k.ag2_out.ap()[r * D + fc * 128: r * D + (fc + 1) * 128,
                                                   co:co + 512]), 16)
                            n += 1
                    for _ in range(4):
                        qsy.emit(sy.dma_start(
                            out=slp[:, 0:1],
                            in_=k.in_slopes.ap()[:, 0:1],
                            allow_slow_non_contiguous=True), 16)

                @b.vector
                def _(v):
                    v.wait_ge(qsy.h, e_tab[0])
                    c_dv.emit(v.tensor_scalar(nsc[:, :], slp[:, :], -1.0, None, ALU.mult))
                    for h in range(2):
                        v.wait_ge(c_ac.h, e_tbl[4 * h + 2])
                        c_dv.emit(v.tensor_mul(ddt[h][:, :], ddt[h][:, :], dmsk[:, :]))
                    for nt in range(NTT):
                        for h in range(2):
                            v.wait_ge(c_pe.h, e_proj[nt][FC - 1])
                            if nt >= 2:
                                v.wait_ge(qsc.h, e_udma[nt - 2][1])
                            c_dv.emit(v.tensor_copy(
                                ubt[:, (nt % 2) * 1024 + h * 512:(nt % 2) * 1024 + (h + 1) * 512],
                                ps[4 + h][:, :]))
                        for tc in range(2):
                            c_dv.emit(v.tensor_copy(
                                v_tm[:, (nt * 4 + tc) * 256:(nt * 4 + tc + 1) * 256],
                                ps[6 + tc][:, 0:256]))
                        for tc in range(2, 4):
                            v.wait_ge(c_pe.h, e_p2[nt][FC - 1])
                            c_dv.emit(v.tensor_copy(
                                v_tm[:, (nt * 4 + tc) * 256:(nt * 4 + tc + 1) * 256],
                                ps[6 + tc - 2][:, 0:256]))

                @b.scalar
                def _(s):
                    s.wait_ge(qsy.h, e_tab[5])
                    s.wait_ge(c_dv.h, e_ns)
                    for h in range(2):
                        c_ac.emit(s.activation(qdec[h][:, :], arrs[:, :], AF.Exp,
                                               scale=nsc[:, h:h + 1]))
                        c_ac.emit(s.activation(kdec[h][:, :], arr2[:, :], AF.Exp,
                                               scale=nsc[:, h:h + 1]))
                        c_ac.emit(s.activation(ddt[h][:, :], idxs[:, :], AF.Exp,
                                               scale=nsc[:, h:h + 1]))
                        c_ac.emit(s.activation(bdc[h][:, :], cc256[:, :], AF.Exp,
                                               scale=nsc[:, h:h + 1]))
                    for nt in range(NTT):
                        s.wait_ge(c_pe.h, e_proj[nt][FC - 1])
                        for i, (buf, bank) in enumerate([(qsb[0], 0), (qsb[1], 1),
                                                         (ksb[0], 2), (ksb[1], 3)]):
                            c_ac.emit(s.activation(buf[:, nt * 512:(nt + 1) * 512],
                                                   ps[bank][:, :], AF.Silu))
                        for h in range(2):
                            s.wait_ge(c_dv.h, e_evac[nt][h])
                            qsc.emit(s.dma_start(
                                out=k.a2ou_in.ap()[(nt // 2) * 512 + 256 + h * 128:
                                                   (nt // 2) * 512 + 256 + (h + 1) * 128,
                                                   (nt % 2) * 512:(nt % 2 + 1) * 512],
                                in_=ubt[:, (nt % 2) * 1024 + h * 512:(nt % 2) * 1024 + (h + 1) * 512]), 16)

                @b.tensor
                def _(t):
                    t.wait_ge(qgp.h, e_w[3])
                    for nt in range(NTT):
                        if nt >= 1:
                            t.wait_ge(c_ac.h, e_sil[nt - 1][3])
                            t.wait_ge(c_dv.h, e_evac[nt - 1][-1])
                        for fc in range(FC):
                            n = nt * FC + fc
                            t.wait_ge(qsy.h, e_xt[n + 4])
                            xs = xt[:, (n % 24) * 512:(n % 24 + 1) * 512]
                            st, sp = (fc == 0), (fc == FC - 1)
                            base = fc * 1024
                            t.matmul(ps[0][:, :], w_b[:, base:base + 128], xs, start=st, stop=sp)
                            t.matmul(ps[1][:, :], w_b[:, base + 128:base + 256], xs, start=st, stop=sp)
                            t.matmul(ps[2][:, :], w_b[:, base + 256:base + 384], xs, start=st, stop=sp)
                            t.matmul(ps[3][:, :], w_b[:, base + 384:base + 512], xs, start=st, stop=sp)
                            t.matmul(ps[4][:, :], w_b[:, base + 768:base + 896], xs, start=st, stop=sp)
                            t.matmul(ps[5][:, :], w_b[:, base + 896:base + 1024], xs, start=st, stop=sp)
                            for tc in range(2):
                                _g = t.matmul(ps[6 + tc][:, 0:256],
                                              xs[:, tc * 128:(tc + 1) * 128],
                                              w_b[:, base + 512:base + 768], start=st, stop=sp)
                            c_pe.emit(_g)
                        t.wait_ge(c_dv.h, e_evac[nt][3])
                        for fc in range(FC):
                            n = nt * FC + fc
                            xs = xt[:, (n % 24) * 512:(n % 24 + 1) * 512]
                            st, sp = (fc == 0), (fc == FC - 1)
                            base = fc * 1024
                            for tc in range(2, 4):
                                _g = t.matmul(ps[6 + tc - 2][:, 0:256],
                                              xs[:, tc * 128:(tc + 1) * 128],
                                              w_b[:, base + 512:base + 768], start=st, stop=sp)
                            c_pe.emit(_g)


        if getattr(k, "dump_a1tb", False):
            qgp2 = k.ch("qgp")
            ev = qgp2.reserve(12, 16)
            with nc.Block() as b2:
                @b2.gpsimd
                def _(g):
                    g.wait_ge(k.ch("ac").h, e_tbl[-1])
                    g.wait_ge(k.ch("dv").h, e_evac[NTT - 1][-1])
                    g.wait_ge(k.ch("ac").h, e_sil[NTT - 1][3])
                    dd = k.dbg
                    qgp2.emit(g.dma_start(out=dd[0:128, 0:BLK], in_=qdec[0][:, :]), 16)
                    qgp2.emit(g.dma_start(out=dd[128:256, 0:BLK], in_=qdec[1][:, :]), 16)
                    qgp2.emit(g.dma_start(out=dd[256:384, 0:2 * BLK], in_=ddt[0][:, :]), 16)
                    qgp2.emit(g.dma_start(out=dd[384:512, 0:2 * BLK], in_=ddt[1][:, :]), 16)
                    qgp2.emit(g.dma_start(out=dd[512:640, 0:2], in_=kdec[0][:, :],
                                           allow_slow_non_contiguous=True), 16)
                    qgp2.emit(g.dma_start(out=dd[640:768, 0:2], in_=kdec[1][:, :],
                                          allow_slow_non_contiguous=True), 16)
                    qgp2.emit(g.dma_start(out=dd[768:896, 0:1], in_=bdc[0][:, :],
                                          allow_slow_non_contiguous=True), 16)
                    qgp2.emit(g.dma_start(out=dd[896:1024, 0:2], in_=nsc[:, :],
                                          allow_slow_non_contiguous=True), 16)
                    qgp2.emit(g.dma_start(out=dd[1024:1152, 0:512], in_=qsb[1][:, 0:512]), 16)
                    qgp2.emit(g.dma_start(out=dd[1152:1280, 0:512], in_=ksb[1][:, 0:512]), 16)
                    qgp2.emit(g.dma_start(out=dd[1280:1408, 0:1024], in_=v_tm[:, 0:1024]), 16)
                    qgp2.emit(g.dma_start(out=dd[1408:1536, 0:1024], in_=ubt[:, 0:1024]), 16)
                    g.wait_ge(qgp2.h, ev[-1])
            return
        e_scan_pe = {}
        e_scan_dv = {}
        e_odma = {}
        for h in range(2):
            for bb in range(NB):
                e_scan_pe[(h, bb)] = c_pe.reserve(3)
                e_scan_dv[(h, bb)] = c_dv.reserve(5)
        e_scan_tail = c_dv.reserve(1)[0]

        if getattr(k, "dump_qk_pre", False):
            k.barrier()
            with nc.Block() as b:
                @b.gpsimd
                def _(g):
                    qgp2 = k.ch("qgp")
                    ev = qgp2.reserve(6, 16)
                    qgp2.emit(g.dma_start(out=k.dbg[0:128, :], in_=qsb[0][:, 0:1024]), 16)
                    qgp2.emit(g.dma_start(out=k.dbg[128:256, :], in_=qsb[1][:, 0:1024]), 16)
                    qgp2.emit(g.dma_start(out=k.dbg[256:384, :], in_=ksb[0][:, 0:1024]), 16)
                    qgp2.emit(g.dma_start(out=k.dbg[384:512, :], in_=ksb[1][:, 0:1024]), 16)
                    qgp2.emit(g.dma_start(out=k.dbg[512:640, :], in_=v_tm[:, 0:1024]), 16)
                    qgp2.emit(g.dma_start(out=k.dbg[640:768, :], in_=qsb[0][:, 7168:8192]), 16)
                    g.wait_ge(qgp2.h, ev[-1])
            return

        with ExitStack() as pp:
            psO = pp.enter_context(nc.psum_tensor("a1s_O", [128, 512], F32))
            psS = [pp.enter_context(nc.psum_tensor(f"a1s_S{j}", [128, 512], F32)) for j in range(2)]
            psKV = pp.enter_context(nc.psum_tensor("a1s_KV", [128, 512], F32))
            psB = pp.enter_context(nc.psum_tensor("a1s_B", [128, 1024], BF16))
            psF = pp.enter_context(nc.psum_tensor("a1s_F", [128, 512], F32))
            psN = pp.enter_context(nc.psum_tensor("a1s_N", [128, 512], F32))

            first = True
            for h in range(2):
                for bb in range(NB):
                    E = e_scan_dv[(h, bb)]
                    EP = e_scan_pe[(h, bb)]
                    g_ = h * NB + bb
                    e_odma[(h, bb)] = qsy.reserve(1, 16)[0]
                    with nc.Block() as b:
                        @b.vector
                        def _(v, h=h, bb=bb, E=E, EP=EP, g_=g_, first=first):
                            cur, nxt = kv[bb % 2], kv[(bb + 1) % 2]
                            if first:
                                v.wait_ge(c_ac.h, e_sil[NTT - 1][3])
                                v.wait_ge(c_dv.h, e_evac[NTT - 1][-1])
                                v.wait_ge(c_ac.h, e_tbl[-1])
                            if bb == 0:
                                k.vst(v, v.memset(kv[0][:, :], 0.0))
                            v.tensor_mul(qd_t[:, :], qsb[h][:, bb * BLK:(bb + 1) * BLK],
                                         qdec[h][:, :])
                            k.vst(v, v.tensor_copy(kv_b[:, :], cur[:, :]))
                            c_dv.emit(v.tensor_copy(dumm[0:1, 0:4], dumm[0:1, 4:8]))
                            v.wait_ge(c_pe.h, EP[0])
                            v.tensor_mul(pdd[:, 0:BLK], psS[0][:, 0:BLK], ddt[h][:, 0:BLK])
                            v.tensor_mul(pdd[:, BLK:2 * BLK], psS[1][:, 0:BLK],
                                         ddt[h][:, BLK:2 * BLK])
                            import os as _os
                            if _os.environ.get("SCAN_ND", "1") == "1":
                                k.vst(v, v.tensor_copy(tnd[:, :], psN[:, 0:BLK]))
                            else:
                                k.vst(v, v.tensor_scalar(tnd[:, :], k.ones_bf[:, 0:128] if False else psN[:, 0:BLK], 0.0, None, ALU.mult))
                            c_dv.emit(v.tensor_copy(dumm[0:1, 0:4], dumm[0:1, 4:8]))
                            v.wait_ge(c_pe.h, EP[1])
                            v.tensor_single_scalar(ktm[:, 0:128], psB[:, 0:128],
                                                   kdec[h][:, 0:1], ALU.mult)
                            k.vst(v, v.tensor_single_scalar(ktm[:, 128:256], psB[:, 128:256],
                                                            kdec[h][:, 1:2], ALU.mult))
                            c_dv.emit(v.tensor_copy(dumm[0:1, 0:4], dumm[0:1, 4:8]))
                            v.wait_ge(c_pe.h, EP[2])
                            c_dv.emit(v.scalar_tensor_tensor(nxt[:, :], cur[:, :],
                                                             bdc[h][:, :], psKV[:, 0:128],
                                                             ALU.mult, ALU.add))
                            if g_ >= 4:
                                pv = g_ - 4
                                v.wait_ge(qsy.h, e_odma[(pv // NB, pv % NB)])
                            k.vst(v, v.scalar_tensor_tensor(
                                ot[:, (g_ % 4) * BLK:(g_ % 4 + 1) * BLK],
                                psO[:, 0:BLK], 1.0, tnd[:, :], ALU.mult, ALU.add))
                            c_dv.emit(v.tensor_copy(dumm[0:1, 0:4], dumm[0:1, 4:8]))

                        @b.tensor
                        def _(t, h=h, bb=bb, E=E, EP=EP, first=first):
                            if first:
                                t.wait_ge(c_ac.h, e_sil[NTT - 1][3])
                                t.wait_ge(c_dv.h, e_evac[NTT - 1][-1])

                            def flush(ev):
                                mm = t.matmul(psF[0:16, 0:128], k.ones_bf[:, 0:16],
                                              k.ones_bf[:, :], start=True, stop=True)
                                if ev:
                                    c_pe.emit(mm)
                            t.wait_ge(c_dv.h, E[0])
                            t.matmul(psS[0][:, 0:BLK], ksb[h][:, (2 * bb) * 128:(2 * bb + 1) * 128],
                                     qsb[h][:, bb * BLK:(bb + 1) * BLK], start=True, stop=True)
                            t.matmul(psS[1][:, 0:BLK],
                                     ksb[h][:, (2 * bb + 1) * 128:(2 * bb + 2) * 128],
                                     qsb[h][:, bb * BLK:(bb + 1) * BLK],
                                     start=True, stop=True)
                            import os as _os
                            if _os.environ.get("SCAN_ND", "1") == "1":
                                t.matmul(psN[:, 0:BLK], kv_b[:, :], qd_t[:, :],
                                         start=True, stop=True)
                            else:
                                t.matmul(psN[:, 0:BLK], k.ones_bf[:, :], qd_t[:, :],
                                         start=True, stop=True)
                            flush(True)
                            t.wait_ge(c_dv.h, E[1])
                            t.matmul(psO[:, 0:BLK], v_tm[:, (2 * bb) * 256 + h * 128:
                                                         (2 * bb) * 256 + (h + 1) * 128],
                                     pdd[:, 0:BLK], start=True, stop=False)
                            t.matmul(psO[:, 0:BLK],
                                     v_tm[:, (2 * bb + 1) * 256 + h * 128:
                                          (2 * bb + 1) * 256 + (h + 1) * 128],
                                     pdd[:, BLK:2 * BLK], start=False, stop=True)
                            t.transpose(psB[:, 0:128], ksb[h][:, (2 * bb) * 128:(2 * bb + 1) * 128],
                                        k.ident_b[:, :])
                            t.transpose(psB[:, 128:256],
                                        ksb[h][:, (2 * bb + 1) * 128:(2 * bb + 2) * 128],
                                        k.ident_b[:, :])
                            flush(True)
                            t.wait_ge(c_dv.h, E[2])
                            t.matmul(psKV[:, 0:128], ktm[:, 0:128],
                                     v_tm[:, (2 * bb) * 256 + h * 128:(2 * bb) * 256 + (h + 1) * 128],
                                     start=True, stop=False)
                            t.matmul(psKV[:, 0:128], ktm[:, 128:256],
                                     v_tm[:, (2 * bb + 1) * 256 + h * 128:
                                          (2 * bb + 1) * 256 + (h + 1) * 128],
                                     start=False, stop=True)
                            flush(True)

                        @b.sync
                        def _(sy, h=h, bb=bb, E=E, g_=g_):
                            sy.wait_ge(c_dv.h, E[4])
                            qsy.emit(sy.dma_start(
                                out=k.a2ou_in.ap()[(bb // 4) * 512 + h * 128:
                                                   (bb // 4) * 512 + (h + 1) * 128,
                                                   (bb % 4) * 256:(bb % 4 + 1) * 256],
                                in_=ot[:, (g_ % 4) * BLK:(g_ % 4 + 1) * BLK]), 16)
                    first = False
            with nc.Block() as b:
                @b.vector
                def _(v):
                    c_dv.emit(v.tensor_copy(dumm[0:1, 0:4], dumm[0:1, 4:8]))
            if getattr(k, "dump_a1qk", False):
                k.barrier()
                with nc.Block() as b:
                    @b.gpsimd
                    def _(g):
                        qgp2 = k.ch("qgp")
                        ev = qgp2.reserve(6, 16)
                        qgp2.emit(g.dma_start(out=k.dbg[0:128, :], in_=qsb[0][:, 0:1024]), 16)
                        qgp2.emit(g.dma_start(out=k.dbg[128:256, :], in_=qsb[1][:, 0:1024]), 16)
                        qgp2.emit(g.dma_start(out=k.dbg[256:384, :], in_=ksb[0][:, 0:1024]), 16)
                        qgp2.emit(g.dma_start(out=k.dbg[384:512, :], in_=ksb[1][:, 0:1024]), 16)
                        qgp2.emit(g.dma_start(out=k.dbg[512:640, :], in_=v_tm[:, 0:1024]), 16)
                        qgp2.emit(g.dma_start(out=k.dbg[640:768, 0:512], in_=pdd[:, 0:512]), 16)
                        g.wait_ge(qgp2.h, ev[-1])


def emit_attn1_out(k: K):
    nc = k.nc
    qsy, qgp = k.ch("qsy"), k.ch("qgp")
    c_pe, c_ac, c_dv = k.ch("pe"), k.ch("ac"), k.ch("dv")
    e_restore = emit_restore_res(k)
    with ExitStack() as ph:
        os_ = ph.enter_context(nc.sbuf_tensor("a1o_os", [128, FC * TPC], BF16))
        us_ = ph.enter_context(nc.sbuf_tensor("a1o_us", [128, FC * TPC], BF16))
        sqb = ph.enter_context(nc.sbuf_tensor("a1o_sqb", [128, 2 * TPC], BF16))
        rstd = ph.enter_context(nc.sbuf_tensor("a1o_rstd", [128, TPC], F32))
        tmp = ph.enter_context(nc.sbuf_tensor("a1o_tmp", [128, TPC], F32))
        wo_b = ph.enter_context(nc.sbuf_tensor("a1o_wo", [128, 2 * D], BF16))
        psS = ph.enter_context(nc.psum_tensor("a1o_psS", [128, TPC], F32))
        psA = [ph.enter_context(nc.psum_tensor(f"a1o_ps{i}", [128, 512], F32)) for i in range(2)]

        e_ou = qsy.reserve(2 * FC, 16)
        e_sq = c_ac.reserve(FC)
        e_rst = c_ac.reserve(1)[0]
        e_smm = c_pe.reserve(FC)
        e_xg = c_dv.reserve(FC)
        e_wo = qgp.reserve(FC, 16)
        e_mm = c_pe.reserve(FC * 2)
        e_add = c_dv.reserve(FC * 2)
        with nc.Block() as b:
            @b.sync
            def _(sy):
                sy.wait_ge(k.cc["a2ou"].h, 1)
                for fc in range(FC):
                    qsy.emit(sy.dma_start(
                        out=os_[:, fc * TPC:(fc + 1) * TPC],
                        in_=k.a2ou_out.ap()[(fc // 2) * 512 + (fc % 2) * 128:
                                            (fc // 2) * 512 + (fc % 2) * 128 + 128, :]), 16)
                for fc in range(FC):
                    qsy.emit(sy.dma_start(
                        out=us_[:, fc * TPC:(fc + 1) * TPC],
                        in_=k.a2ou_out.ap()[(fc // 2) * 512 + 256 + (fc % 2) * 128:
                                            (fc // 2) * 512 + 256 + (fc % 2) * 128 + 128, :]), 16)

            @b.scalar
            def _(s):
                for fc in range(FC):
                    s.wait_ge(qsy.h, e_ou[fc])
                    if fc >= 2:
                        s.wait_ge(c_pe.h, e_smm[fc - 2])
                    c_ac.emit(s.activation(sqb[:, (fc % 2) * TPC:(fc % 2 + 1) * TPC],
                                           os_[:, fc * TPC:(fc + 1) * TPC], AF.Square))
                s.wait_ge(c_pe.h, e_smm[-1])
                c_ac.emit(s.activation(rstd[:, :], psS[:, :], AF.Sqrt,
                                       scale=k.cst[:, 0:1], bias=k.cst[:, 4:5]))

            @b.tensor
            def _(t):
                for fc in range(FC):
                    t.wait_ge(c_ac.h, e_sq[fc])
                    st, sp = (fc == 0), (fc == FC - 1)
                    for nh in range(2):
                        sl = slice(nh * 512, (nh + 1) * 512)
                        _g = t.matmul(psS[:, sl], k.ones_bf[:, :],
                                 sqb[:, (fc % 2) * TPC:(fc % 2 + 1) * TPC][:, sl],
                                 start=st, stop=sp)
                    c_pe.emit(_g)
                t.wait_ge(c_dv.h, e_xg[-1])
                n = 0
                for oc in range(FC):
                    t.wait_ge(qgp.h, e_wo[oc])
                    for th in range(2):
                        if n >= 2:
                            t.wait_ge(c_dv.h, e_add[n - 2])
                        for kc in range(FC):
                            _g = t.matmul(psA[n % 2][:, :],
                                     wo_b[:, (oc % 2) * D + kc * 128:(oc % 2) * D + (kc + 1) * 128],
                                     os_[:, kc * TPC + th * 512: kc * TPC + (th + 1) * 512],
                                     start=(kc == 0), stop=(kc == FC - 1))
                        c_pe.emit(_g)
                        n += 1

            @b.vector
            def _(v):
                v.wait_ge(c_ac.h, e_rst)
                k.vst(v, v.reciprocal(rstd[:, :], rstd[:, :]))
                for fc in range(FC):
                    v.wait_ge(qsy.h, e_ou[FC + fc])
                    v.tensor_mul(tmp[:, :], os_[:, fc * TPC:(fc + 1) * TPC], rstd[:, :])
                    c_dv.emit(v.tensor_mul(os_[:, fc * TPC:(fc + 1) * TPC], tmp[:, :],
                                           us_[:, fc * TPC:(fc + 1) * TPC]))
                v.wait_ge(qsy.h, e_restore)
                n = 0
                for oc in range(FC):
                    for th in range(2):
                        v.wait_ge(c_pe.h, e_mm[n])
                        sl = slice(th * 512, (th + 1) * 512)
                        c_dv.emit(v.tensor_tensor(k.rch(oc, sl), psA[n % 2][:, :],
                                                  k.rch(oc, sl), ALU.add))
                        n += 1

            @b.gpsimd
            def _(g):
                for oc in range(FC):
                    if oc >= 2:
                        g.wait_ge(c_pe.h, e_mm[2 * (oc - 1) - 1])
                    qgp.emit(g.dma_start(out=wo_b[:, (oc % 2) * D:(oc % 2 + 1) * D],
                                         in_=k.in_wout[oc, :, :]), 16)


def emit_ffn1(k: K):
    nc = k.nc
    qsy, qgp = k.ch("qsy"), k.ch("qgp")
    c_pe, c_ac, c_dv = k.ch("pe"), k.ch("ac"), k.ch("dv")
    with ExitStack() as ph:
        sbf = lambda n, s, d: ph.enter_context(nc.sbuf_tensor(n, s, d))
        qg_b = sbf("f1_qg", [128, 2 * 2048], BF16)
        qu_b = sbf("f1_qu", [128, 2 * 2048], BF16)
        qd_b = sbf("f1_qd", [128, 2 * HC1 * 128], BF16)
        h_sb = sbf("f1_h", [128, HC1 * 512], BF16)
        sg_s = sbf("f1_sg", [128, HC1], F32)
        su_s = sbf("f1_su", [128, HC1], F32)
        sd_s = sbf("f1_sd", [128, FC], F32)
        xc = sbf("f1_xc", [128, 512], F32)
        hc = sbf("f1_hc", [128, 512], F32)
        gl = sbf("f1_gl", [128, 2 * 512], F32)
        ul = sbf("f1_ul", [128, 2 * 512], F32)
        gsb = sbf("f1_gs", [128, 2 * 512], BF16)
        dl = sbf("f1_dl", [128, 2 * 512], F32)
        psX = ph.enter_context(nc.psum_tensor("f1_psX", [128, 512], F32))
        psG = [ph.enter_context(nc.psum_tensor(f"f1_psG{i}", [128, 512], F32)) for i in range(2)]
        psU = [ph.enter_context(nc.psum_tensor(f"f1_psU{i}", [128, 512], F32)) for i in range(2)]
        psH = ph.enter_context(nc.psum_tensor("f1_psH", [128, 512], F32))
        psD = [ph.enter_context(nc.psum_tensor(f"f1_psD{i}", [128, 512], F32)) for i in range(2)]

        e_s = qsy.reserve(3, 16)
        e_gu, e_qd = {}, {}
        e_xmm, e_gmm, e_umm, e_hmm, e_dmm = {}, {}, {}, {}, {}
        e_xc, e_gli, e_uli, e_hh, e_hc, e_dli = {}, {}, {}, {}, {}, {}
        e_gs = {}
        for th in range(2):
            for mc in range(HC1):
                e_gu[(th, mc)] = qgp.reserve(2, 16)
            for oc in range(FC):
                e_qd[(th, oc)] = qgp.reserve(1, 16)[0]
            e_xmm[th] = c_pe.reserve(1)[0]
            for mc in range(HC1):
                e_gmm[(th, mc)] = c_pe.reserve(1)[0]
                e_umm[(th, mc)] = c_pe.reserve(1)[0]
            e_hmm[th] = c_pe.reserve(1)[0]
            for oc in range(FC):
                e_dmm[(th, oc)] = c_pe.reserve(1)[0]
            e_xc[th] = c_dv.reserve(1)[0]
            for mc in range(HC1):
                e_gli[(th, mc)] = c_dv.reserve(1)[0]
                e_uli[(th, mc)] = c_dv.reserve(1)[0]
                e_hh[(th, mc)] = c_dv.reserve(1)[0]
            e_hc[th] = c_dv.reserve(1)[0]
            for oc in range(FC):
                e_dli[(th, oc)] = c_dv.reserve(2)
            for mc in range(HC1):
                e_gs[(th, mc)] = c_ac.reserve(1)[0]

        with nc.Block() as b:
            @b.sync
            def _(sy):
                qsy.emit(sy.dma_start(out=sg_s[:, :], in_=k.in_sg.ap()), 16)
                qsy.emit(sy.dma_start(out=su_s[:, :], in_=k.in_su.ap()), 16)
                qsy.emit(sy.dma_start(out=sd_s[:, :], in_=k.in_sd.ap()), 16)

            @b.gpsimd
            def _(g):
                for th in range(2):
                    for mc in range(HC1):
                        gidx = th * HC1 + mc
                        if gidx >= 2:
                            pth, pmc = (gidx - 2) // HC1, (gidx - 2) % HC1
                            g.wait_ge(c_pe.h, e_umm[(pth, pmc)])
                        qgp.emit(g.dma_start(out=qg_b[:, (mc % 2) * 2048:(mc % 2 + 1) * 2048],
                                             in_=k.in_qg[mc, :, :]), 16)
                        qgp.emit(g.dma_start(out=qu_b[:, (mc % 2) * 2048:(mc % 2 + 1) * 2048],
                                             in_=k.in_qu[mc, :, :]), 16)
                    for oc in range(FC):
                        gidx = th * FC + oc
                        if gidx >= 2:
                            pth, poc = (gidx - 2) // FC, (gidx - 2) % FC
                            g.wait_ge(c_pe.h, e_dmm[(pth, poc)])
                        qgp.emit(g.dma_start(out=qd_b[:, (oc % 2) * HC1 * 128:(oc % 2 + 1) * HC1 * 128],
                                             in_=k.in_qd[oc, :, :]), 16)

            @b.tensor
            def _(t):
                for th in range(2):
                    tsl = slice(th * 512, (th + 1) * 512)
                    if th >= 1:
                        t.wait_ge(c_dv.h, e_xc[th - 1])
                    for fc in range(FC):
                        _g = t.matmul(psX[:, :], k.ones_bf[:, :],
                                 k.xln[:, fc * TPC + th * 512: fc * TPC + (th + 1) * 512],
                                 start=(fc == 0), stop=(fc == FC - 1))
                    c_pe.emit(_g)
                    for mc in range(HC1):
                        t.wait_ge(qgp.h, e_gu[(th, mc)][1])
                        gidx = th * HC1 + mc
                        if gidx >= 2:
                            pth, pmc = (gidx - 2) // HC1, (gidx - 2) % HC1
                            t.wait_ge(c_dv.h, e_gli[(pth, pmc)])
                        for kc in range(FC):
                            _g = t.matmul(psG[mc % 2][:, :],
                                     qg_b[:, (mc % 2) * 2048 + kc * 128:(mc % 2) * 2048 + (kc + 1) * 128],
                                     k.xln[:, kc * TPC + th * 512: kc * TPC + (th + 1) * 512],
                                     start=(kc == 0), stop=(kc == FC - 1))
                        c_pe.emit(_g)
                        if gidx >= 2:
                            pth, pmc = (gidx - 2) // HC1, (gidx - 2) % HC1
                            t.wait_ge(c_dv.h, e_uli[(pth, pmc)])
                        for kc in range(FC):
                            _g = t.matmul(psU[mc % 2][:, :],
                                     qu_b[:, (mc % 2) * 2048 + kc * 128:(mc % 2) * 2048 + (kc + 1) * 128],
                                     k.xln[:, kc * TPC + th * 512: kc * TPC + (th + 1) * 512],
                                     start=(kc == 0), stop=(kc == FC - 1))
                        c_pe.emit(_g)
                    t.wait_ge(c_dv.h, e_hh[(th, HC1 - 1)])
                    if th >= 1:
                        t.wait_ge(c_dv.h, e_hc[th - 1])
                    for mc in range(HC1):
                        _g = t.matmul(psH[:, :], k.ones_bf[:, :],
                                 h_sb[:, mc * 512:(mc + 1) * 512],
                                 start=(mc == 0), stop=(mc == HC1 - 1))
                    c_pe.emit(_g)
                    for oc in range(FC):
                        t.wait_ge(qgp.h, e_qd[(th, oc)])
                        gidx = th * FC + oc
                        if gidx >= 2:
                            pth, poc = (gidx - 2) // FC, (gidx - 2) % FC
                            t.wait_ge(c_dv.h, e_dli[(pth, poc)][0])
                        for kc in range(HC1):
                            _g = t.matmul(psD[oc % 2][:, :],
                                     qd_b[:, (oc % 2) * HC1 * 128 + kc * 128:
                                          (oc % 2) * HC1 * 128 + (kc + 1) * 128],
                                     h_sb[:, kc * 512:(kc + 1) * 512],
                                     start=(kc == 0), stop=(kc == HC1 - 1))
                        c_pe.emit(_g)

            @b.scalar
            def _(s):
                s.wait_ge(qsy.h, e_s[2])
                for th in range(2):
                    for mc in range(HC1):
                        s.wait_ge(c_dv.h, e_gli[(th, mc)])
                        gidx = th * HC1 + mc
                        if gidx >= 2:
                            pth, pmc = (gidx - 2) // HC1, (gidx - 2) % HC1
                            s.wait_ge(c_dv.h, e_hh[(pth, pmc)])
                        c_ac.emit(s.activation(gsb[:, (mc % 2) * 512:(mc % 2 + 1) * 512],
                                               gl[:, (mc % 2) * 512:(mc % 2 + 1) * 512],
                                               AF.Silu, scale=sg_s[:, mc:mc + 1]))

            @b.vector
            def _(v):
                for th in range(2):
                    v.wait_ge(c_pe.h, e_xmm[th])
                    c_dv.emit(v.tensor_scalar(xc[:, :], psX[:, :], -128.0, None, ALU.mult))
                    for mc in range(HC1):
                        msl = slice((mc % 2) * 512, (mc % 2 + 1) * 512)
                        gidx = th * HC1 + mc
                        if gidx >= 2:
                            pth, pmc = (gidx - 2) // HC1, (gidx - 2) % HC1
                            v.wait_ge(c_ac.h, e_gs[(pth, pmc)])
                        v.wait_ge(c_pe.h, e_gmm[(th, mc)])
                        c_dv.emit(v.tensor_tensor(gl[:, msl], psG[mc % 2][:, :], xc[:, :], ALU.add))
                        v.wait_ge(c_pe.h, e_umm[(th, mc)])
                        c_dv.emit(v.tensor_tensor(ul[:, msl], psU[mc % 2][:, :], xc[:, :], ALU.add))
                        v.wait_ge(c_ac.h, e_gs[(th, mc)])
                        c_dv.emit(v.scalar_tensor_tensor(h_sb[:, mc * 512:(mc + 1) * 512],
                                                         ul[:, msl], su_s[:, mc:mc + 1],
                                                         gsb[:, msl], ALU.mult, ALU.mult))
                    v.wait_ge(c_pe.h, e_hmm[th])
                    c_dv.emit(v.tensor_scalar(hc[:, :], psH[:, :], -128.0, None, ALU.mult))
                    for oc in range(FC):
                        osl = slice((oc % 2) * 512, (oc % 2 + 1) * 512)
                        v.wait_ge(c_pe.h, e_dmm[(th, oc)])
                        c_dv.emit(v.tensor_tensor(dl[:, osl], psD[oc % 2][:, :], hc[:, :], ALU.add))
                        c_dv.emit(v.scalar_tensor_tensor(
                            k.rch(oc, slice(th * 512, (th + 1) * 512)),
                            dl[:, osl], sd_s[:, oc:oc + 1],
                            k.rch(oc, slice(th * 512, (th + 1) * 512)),
                            ALU.mult, ALU.add))


def emit_out(k: K):
    nc = k.nc
    qsy = k.ch("qsy")
    c_pe, c_dv = k.ch("pe"), k.ch("dv")
    with ExitStack() as ph:
        orow = ph.enter_context(nc.sbuf_tensor("orow", [128, 2 * D], F32))
        psT = [ph.enter_context(nc.psum_tensor(f"o_psT{i}", [128, 512], F32)) for i in range(4)]
        NTOT = TCH * FC
        e_tr = c_pe.reserve(NTOT)
        e_cp = c_dv.reserve(NTOT)
        e_st = qsy.reserve(TCH, 16)
        with nc.Block() as b:
            @b.tensor
            def _(t):
                n = 0
                for tc in range(TCH):
                    for fc in range(FC):
                        if n >= 4:
                            t.wait_ge(c_dv.h, e_cp[n - 4])
                        _g = t.transpose(psT[n % 4][:, 0:128],
                                              k.rch(fc, slice(tc * 128, (tc + 1) * 128)),
                                              k.ident_f[:, :])
                        c_pe.emit(_g)
                        n += 1

            @b.vector
            def _(v):
                n = 0
                for tc in range(TCH):
                    if tc >= 2:
                        v.wait_ge(qsy.h, e_st[tc - 2])
                    for fc in range(FC):
                        v.wait_ge(c_pe.h, e_tr[n])
                        c_dv.emit(v.tensor_copy(
                            orow[:, (tc % 2) * D + fc * 128:(tc % 2) * D + (fc + 1) * 128],
                            psT[n % 4][:, 0:128]))
                        n += 1

            @b.sync
            def _(sy):
                for tc in range(TCH):
                    sy.wait_ge(c_dv.h, e_cp[(tc + 1) * FC - 1])
                    qsy.emit(sy.dma_start(out=k.out[tc * 128:(tc + 1) * 128, :],
                                          in_=orow[:, (tc % 2) * D:(tc % 2 + 1) * D]), 16)
                sy.wait_ge(qsy.h, e_st[-1])



def build_program(stop_after="out", debug=False):
    k = K(debug=debug)
    nc = k.nc
    dump = None

    def stop(name, dump_src=None, cast=False):
        nonlocal dump
        if debug and stop_after == name:
            k.barrier()
            if dump_src is not None:
                emit_dump(k, dump_src, cast=cast)
            return True
        return False

    emit_init(k)
    k.barrier()
    emit_collective(k, "AllGather", k.agw_in.ap(), k.agw_out.ap(), "agw")
    k.barrier()

    stk_a = ExitStack()
    k.res = stk_a.enter_context(nc.sbuf_tensor("res_a", [128, FC * TPC], F32))
    k.xln = stk_a.enter_context(nc.sbuf_tensor("xln_a", [128, FC * TPC], BF16))
    emit_load_x(k)
    k.barrier()
    if stop("load", lambda fc: k.rch(fc)):
        k.check_channels()
        return k
    emit_ln(k, "ln0", ag=None if stop_after in ("ln0noag", "ln0nodump") else (k.ag0_in, "ag0", k.ag0_out))
    k.barrier()
    if stop_after == "ln0noag":
        if debug:
            emit_dump(k, lambda fc: k.xch(fc), cast=True)
        k.check_channels()
        return k
    if stop_after == "ln0nodump":
        if debug:
            emit_dump(k, lambda fc: k.rch(fc), cast=False)
        k.check_channels()
        return k
    emit_collective(k, "AllGather", k.ag0_in.ap(), k.ag0_out.ap(), "ag0")
    k.barrier()
    if stop("ln0", lambda fc: k.xch(fc), cast=True):
        k.check_channels()
        return k
    emit_park_res(k)
    k.barrier()
    stk_a.close()
    k.res = k.xln = None

    emit_attn0(k)
    k.barrier()
    emit_collective(k, "AllToAll", k.a2y_in.ap(), k.a2y_out.ap(), "a2y")
    k.barrier()

    stk_b = ExitStack()
    k.res = stk_b.enter_context(nc.sbuf_tensor("res_b", [128, FC * TPC], F32))
    k.xln = stk_b.enter_context(nc.sbuf_tensor("xln_b", [128, FC * TPC], BF16))
    wt32 = stk_b.enter_context(nc.sbuf_tensor("wt32", [NE, TPC], F32))
    emit_attn0_out(k)
    k.barrier()
    if stop("res1", lambda fc: k.rch(fc)):
        k.check_channels()
        return k
    emit_ln(k, "ln1")
    k.barrier()
    emit_router(k, wt32)
    k.barrier()
    emit_moe(k, wt32)
    k.barrier()
    if stop("moe", lambda fc: k.rch(fc)):
        k.check_channels()
        return k
    emit_ln(k, "ln2", ag=(k.ag2_in, "ag2", k.ag2_out))
    k.barrier()
    emit_collective(k, "AllGather", k.ag2_in.ap(), k.ag2_out.ap(), "ag2")
    k.barrier()
    emit_park_res(k)
    k.barrier()
    stk_b.close()
    k.res = k.xln = None

    if debug and stop_after == "a1tb":
        k.dump_a1tb = True
        emit_attn1(k)
        return k
    if debug and stop_after == "a1qk2":
        k.dump_qk_pre = True
        emit_attn1(k)
        return k
    if debug and stop_after == "a1qk":
        k.dump_a1qk = True
        emit_attn1(k)
        k.check_channels()
        return k
    emit_attn1(k)
    k.barrier()
    if debug and stop_after == "a1pre":
        with k.nc.Block() as _b:
            @_b.gpsimd
            def _(g):
                qgp = k.ch("qgp")
                ev = qgp.reserve(4, 16)
                for j in range(4):
                    qgp.emit(g.dma_start(
                        out=k.dbg[j * 512:(j + 1) * 512, :],
                        in_=k.a2ou_in.ap()[j * 512:(j + 1) * 512, :]), 16)
                g.wait_ge(qgp.h, ev[-1])
        k.check_channels()
        return k
    emit_collective(k, "AllToAll", k.a2ou_in.ap(), k.a2ou_out.ap(), "a2ou")
    k.barrier()
    if debug and stop_after == "a1in":
        with k.nc.Block() as _b:
            @_b.gpsimd
            def _(g):
                qgp = k.ch("qgp")
                ev = qgp.reserve(4, 16)
                for j in range(4):
                    qgp.emit(g.dma_start(
                        out=k.dbg[j * 512:(j + 1) * 512, :],
                        in_=k.a2ou_in.ap()[j * 512:(j + 1) * 512, :]), 16)
                g.wait_ge(qgp.h, ev[-1])
        k.check_channels()
        return k
    if debug and stop_after in ("a1o", "a1u"):
        off = 0 if stop_after == "a1o" else 256
        with k.nc.Block() as _b:
            @_b.gpsimd
            def _(g):
                qgp = k.ch("qgp")
                ev = qgp.reserve(FC, 16)
                for fc in range(FC):
                    qgp.emit(g.dma_start(
                        out=k.dbg[fc * 128:(fc + 1) * 128, :],
                        in_=k.a2ou_out.ap()[(fc // 2) * 512 + off + (fc % 2) * 128:
                                            (fc // 2) * 512 + off + (fc % 2) * 128 + 128, :]), 16)
                g.wait_ge(qgp.h, ev[-1])
        k.check_channels()
        return k

    stk_c = ExitStack()
    k.res = stk_c.enter_context(nc.sbuf_tensor("res_c", [128, FC * TPC], F32))
    k.xln = stk_c.enter_context(nc.sbuf_tensor("xln_c", [128, FC * TPC], BF16))
    emit_attn1_out(k)
    k.barrier()
    if stop("res3", lambda fc: k.rch(fc)):
        k.check_channels()
        return k
    emit_ln(k, "ln3")
    k.barrier()
    emit_ffn1(k)
    k.barrier()
    if stop("ffn1", lambda fc: k.rch(fc)):
        k.check_channels()
        return k
    emit_out(k)
    stk_c.close()
    k.check_channels()
    return k


def kernel(**kw):
    in_maps = make_host_inputs(kw)
    k = build_program(stop_after="out", debug=False)
    res = run_bass_kernel_spmd(k.nc, in_maps, core_ids=list(range(NCORE)))
    out = np.concatenate([res.results[c]["out"] for c in range(NCORE)], axis=0)
    return out.reshape(1, TOK, D).astype(np.float32)


# version 03 (final)
# speedup vs baseline: 1.0137x; 1.0028x over previous
import os
import numpy as np
from contextlib import ExitStack

os.environ.setdefault("NEURON_SCRATCHPAD_PAGE_SIZE", "512")

import concourse.bass as bass
import concourse.mybir as mybir
from concourse.bass_utils import run_bass_kernel_spmd
from concourse.aot_env import allow_ici_all_to_all

F32 = mybir.dt.float32
BF16 = mybir.dt.bfloat16
U8 = mybir.dt.uint8
AF = mybir.ActivationFunctionType
ALU = mybir.AluOpType
AX = mybir.AxisListType

NCORE = 8
TOK = 8192
D = 2048
TPC = 1024
FC = 16
TCH = 8

HD0 = 256
NE, EPC, DI, MCI = 32, 4, 768, 6
H1, D1, BLK, NB = 16, 128, 256, 32
HID1, HC1 = 5632, 44

LAMBDA_INIT = 0.8
RMS_EPS = 1.1920929e-07
L1_EPS = 1e-6
LN_EPS = 1e-5
INV_SQRT_HALF0 = float(1.0 / np.sqrt(128.0))

GW_TILE = 128 * 2048
DW_TILE = 128 * 768
EXP_ELEMS = MCI * GW_TILE
GW_BASE = 0
UW_BASE = EPC * EXP_ELEMS
DW_BASE = 2 * EPC * EXP_ELEMS
RANK_ELEMS = 3 * EPC * EXP_ELEMS
AGW_ROWS = RANK_ELEMS // 1024



def _pmaj(a):
    K_, M = a.shape
    return np.ascontiguousarray(a.reshape(K_ // 128, 128, M).transpose(1, 0, 2).reshape(128, -1))


def _colblocks(a, bs=128):
    K_, M = a.shape
    kc = K_ // 128
    nb = M // bs
    return np.ascontiguousarray(a.reshape(kc, 128, nb, bs).transpose(2, 1, 0, 3).reshape(nb, 128, kc * bs))


def _u8view(w_int32):
    w = np.ascontiguousarray(np.asarray(w_int32))
    assert w.dtype == np.int32
    return np.ascontiguousarray(w.view(np.uint8).reshape(*w.shape, 4)[..., 0])


def make_host_inputs(kw):
    f = np.float32
    x = np.ascontiguousarray(np.asarray(kw["L0_attn_x"], f).reshape(TOK, D))
    wq = np.asarray(kw["L0_attn_wq"], f)
    wk = np.asarray(kw["L0_attn_wk"], f)
    wv = np.asarray(kw["L0_attn_wv"], f)
    woT_cb = _colblocks(np.ascontiguousarray(np.asarray(kw["L0_attn_wo"], f).T))
    lam = np.concatenate([np.asarray(kw[n], f).reshape(2, 128).T for n in
                          ("L0_attn_lq1", "L0_attn_lk1", "L0_attn_lq2", "L0_attn_lk2")], axis=1)
    lam = np.ascontiguousarray(lam)
    router_pm = _pmaj(np.ascontiguousarray(np.asarray(kw["L0_ffn_router_w"], f).T))

    gw = np.asarray(kw["L0_ffn_gate_w"], f)
    uw = np.asarray(kw["L0_ffn_up_w"], f)
    dw = np.asarray(kw["L0_ffn_down_w"], f)

    wqkvu = np.asarray(kw["L1_attn_W_qkvu"], f)
    slopes = np.asarray(kw["L1_attn_slope_rate"], f).reshape(H1)
    woutT_cb = _colblocks(np.ascontiguousarray(np.asarray(kw["L1_attn_W_out"], f).T))

    qg_cb = _colblocks(np.ascontiguousarray(_u8view(kw["L1_ffn_wq_gate"]).T))
    qu_cb = _colblocks(np.ascontiguousarray(_u8view(kw["L1_ffn_wq_up"]).T))
    qd_cb = _colblocks(np.ascontiguousarray(_u8view(kw["L1_ffn_wq_down"]).T))
    sg = np.ascontiguousarray(np.asarray(kw["L1_ffn_s_gate"], f).reshape(HC1, 128).T)
    su = np.ascontiguousarray(np.asarray(kw["L1_ffn_s_up"], f).reshape(HC1, 128).T)
    sd = np.ascontiguousarray(np.asarray(kw["L1_ffn_s_down"], f).reshape(FC, 128).T)

    ident = np.eye(128, dtype=f)
    masks = np.zeros((128, 4 * 512), f)
    for j in range(4):
        si = np.arange(128)[:, None]
        m = np.arange(512)[None, :]
        masks[:, j * 512:(j + 1) * 512] = (si + 128 * j <= m).astype(f)
    arr_b = np.broadcast_to(np.arange(1, BLK + 1, dtype=f)[None, :], (128, BLK)).copy()
    arr2t = np.stack([255.0 - (np.arange(128) + 128 * j) for j in range(2)], axis=1).astype(f)
    idx2 = np.zeros((128, 2 * BLK), f)
    dmask = np.zeros((128, 2 * BLK), f)
    for j in range(2):
        si = np.arange(128)[:, None] + 128 * j
        m = np.arange(BLK)[None, :]
        v = (m - si).astype(f)
        dmask[:, j * BLK:(j + 1) * BLK] = (v >= 0).astype(f)
        v[v < 0] = 0.0
        idx2[:, j * BLK:(j + 1) * BLK] = v
    c256 = np.full((128, 1), 256.0, f)

    in_maps = []
    for c in range(NCORE):
        mm = {}
        mm["x_slice"] = np.ascontiguousarray(x[c * TPC:(c + 1) * TPC, :])
        mm["l0_wq"] = _pmaj(np.ascontiguousarray(wq[c * HD0:(c + 1) * HD0, :].T))
        mm["l0_wk"] = _pmaj(np.ascontiguousarray(wk[c * HD0:(c + 1) * HD0, :].T))
        mm["l0_wv"] = _pmaj(np.ascontiguousarray(wv[c * HD0:(c + 1) * HD0, :].T))
        mm["l0_wo"] = woT_cb
        mm["l0_lam"] = lam
        mm["l0_router"] = router_pm
        parts = []
        for arr3 in (gw, uw, dw):
            for e in range(EPC * c, EPC * (c + 1)):
                parts.append(_colblocks(np.ascontiguousarray(arr3[e].T)).reshape(-1))
        mm["moe_w"] = np.ascontiguousarray(np.concatenate(parts).reshape(AGW_ROWS, 1024))
        rows = [wqkvu[g * D + HD0 * c: g * D + HD0 * (c + 1), :] for g in range(4)]
        mm["l1_wqkvu"] = _pmaj(np.ascontiguousarray(np.concatenate(rows, axis=0).T))
        mm["l1_slopes"] = np.ascontiguousarray(
            np.broadcast_to(slopes[2 * c:2 * c + 2][None, :], (128, 2)).astype(f))
        mm["l1_wout"] = woutT_cb
        mm["l1_qg"] = qg_cb
        mm["l1_qu"] = qu_cb
        mm["l1_qd"] = qd_cb
        mm["l1_sg"] = sg
        mm["l1_su"] = su
        mm["l1_sd"] = sd
        mm["ident"] = ident
        mm["l0_masks"] = masks
        mm["l1_arr"] = arr_b
        mm["l1_arr2t"] = arr2t
        mm["l1_idx"] = idx2
        mm["l1_dmask"] = dmask
        mm["l1_c256"] = c256
        in_maps.append(mm)
    return in_maps



class SemC:
    def __init__(self, es, nc, name):
        self.h = es.enter_context(nc.semaphore(name))
        self.v = 0
        self.emitted = 0
        self.name = name

    def reserve(self, n=1, step=1):
        vals = [self.v + step * (i + 1) for i in range(n)]
        self.v += step * n
        return vals

    def emit(self, inst, step=1):
        self.emitted += step
        inst.then_inc(self.h, step)
        return inst


class K:
    def __init__(self, debug=False):
        nc = bass.Bass("TRN2", target_bir_lowering=False, debug=False)
        self.nc = nc
        self.debug = debug
        es = ExitStack()
        self.es = es
        dt = nc.dram_tensor
        self.in_x = dt("x_slice", [TPC, D], F32, kind="ExternalInput")
        self.in_wq = dt("l0_wq", [128, FC * HD0], F32, kind="ExternalInput")
        self.in_wk = dt("l0_wk", [128, FC * HD0], F32, kind="ExternalInput")
        self.in_wv = dt("l0_wv", [128, FC * HD0], F32, kind="ExternalInput")
        self.in_wo = dt("l0_wo", [FC, 128, D], F32, kind="ExternalInput")
        self.in_lam = dt("l0_lam", [128, 8], F32, kind="ExternalInput")
        self.in_router = dt("l0_router", [128, FC * NE], F32, kind="ExternalInput")
        self.in_moew = dt("moe_w", [AGW_ROWS, 1024], F32, kind="ExternalInput")
        self.in_l1w = dt("l1_wqkvu", [128, FC * 1024], F32, kind="ExternalInput")
        self.in_slopes = dt("l1_slopes", [128, 2], F32, kind="ExternalInput")
        self.in_wout = dt("l1_wout", [FC, 128, D], F32, kind="ExternalInput")
        self.in_qg = dt("l1_qg", [HC1, 128, FC * 128], U8, kind="ExternalInput")
        self.in_qu = dt("l1_qu", [HC1, 128, FC * 128], U8, kind="ExternalInput")
        self.in_qd = dt("l1_qd", [FC, 128, HC1 * 128], U8, kind="ExternalInput")
        self.in_sg = dt("l1_sg", [128, HC1], F32, kind="ExternalInput")
        self.in_su = dt("l1_su", [128, HC1], F32, kind="ExternalInput")
        self.in_sd = dt("l1_sd", [128, FC], F32, kind="ExternalInput")
        self.in_ident = dt("ident", [128, 128], F32, kind="ExternalInput")
        self.in_masks = dt("l0_masks", [128, 4 * 512], F32, kind="ExternalInput")
        self.in_arr = dt("l1_arr", [128, BLK], F32, kind="ExternalInput")
        self.in_arr2t = dt("l1_arr2t", [128, 2], F32, kind="ExternalInput")
        self.in_idx = dt("l1_idx", [128, 2 * BLK], F32, kind="ExternalInput")
        self.in_dmask = dt("l1_dmask", [128, 2 * BLK], F32, kind="ExternalInput")
        self.in_c256 = dt("l1_c256", [128, 1], F32, kind="ExternalInput")

        self.out = dt("out", [TPC, D], F32, kind="ExternalOutput")
        if debug:
            self.dbg = dt("dbg", [D, TPC], F32, kind="ExternalOutput")

        self.agw_in = dt("agw_in", [AGW_ROWS, 1024], BF16)
        self.agw_out = dt("agw_out", [NCORE * AGW_ROWS, 1024], BF16)
        self.ag0_in = dt("ag0_in", [D, TPC], BF16)
        self.ag0_out = dt("ag0_out", [NCORE * D, TPC], BF16)
        self.a2y_in = dt("a2y_in", [NCORE * HD0, TPC], BF16)
        self.a2y_out = dt("a2y_out", [NCORE * HD0, TPC], BF16)
        self.ag2_in = dt("ag2_in", [D, TPC], BF16)
        self.ag2_out = dt("ag2_out", [NCORE * D, TPC], BF16)
        self.a2ou_in = dt("a2ou_in", [NCORE * 512, TPC], BF16)
        self.a2ou_out = dt("a2ou_out", [NCORE * 512, TPC], BF16)
        self.res_park = dt("res_park", [D, TPC], F32)

        self._ch = {}
        self.bar = {e: SemC(es, nc, f"bar_{e}") for e in
                    ("sync", "tensor", "scalar", "vector", "gpsimd")}
        self.cc = {n: SemC(es, nc, f"cc_{n}") for n in
                   ("agw", "ag0", "a2y", "ag2", "a2ou")}

        sb = lambda name, shape, dtype: es.enter_context(nc.sbuf_tensor(name, shape, dtype))
        self.ones_bf = sb("ones_bf", [128, 128], BF16)
        self.ones_f = sb("ones_f", [128, 128], F32)
        self.ident_f = sb("ident_f", [128, 128], F32)
        self.ident_b = sb("ident_b", [128, 128], BF16)
        self.masks_b = sb("masks_b", [128, 4 * 512], BF16)
        self.neglam = sb("neglam", [128, 1], F32)
        self.scrcol = sb("scrcol", [128, 8], F32)
        self.lamt = sb("lamt", [128, 8], F32)
        self.cst = sb("cst", [128, 6], F32)

        self.res = None
        self.xln = None

    def ch(self, name):
        if name not in self._ch:
            self._ch[name] = SemC(self.es, self.nc, f"ch_{name}")
        return self._ch[name]

    def rch(self, fc, sl=slice(None)):
        return self.res[:, fc * TPC:(fc + 1) * TPC][:, sl]

    def xch(self, fc, sl=slice(None)):
        return self.xln[:, fc * TPC:(fc + 1) * TPC][:, sl]

    def vst(self, v, inst):
        c = self.ch("dvx")
        c.v += 1
        c.emitted += 1
        inst.then_inc(c.h, 1)
        v.wait_ge(c.h, c.v)

    def barrier(self):
        nc = self.nc
        tot = {e: self.bar[e].reserve(1)[0] for e in self.bar}
        qd = [(self._ch[n].h, self._ch[n].v) for n in ("qsy", "qsc", "qgp") if n in self._ch]

        def body(engname):
            def f(e):
                for h, v in qd:
                    e.wait_ge(h, v)
                if engname in ("vector", "scalar", "gpsimd"):
                    self.bar[engname].emit(e.drain())
                else:
                    self.bar[engname].emit(e.nop())
                for en, t in tot.items():
                    if en != engname:
                        e.wait_ge(self.bar[en].h, t)
            return f
        with nc.Block() as b:
            b.sync(body("sync"))
            b.tensor(body("tensor"))
            b.scalar(body("scalar"))
            b.vector(body("vector"))
            b.gpsimd(body("gpsimd"))

    def check_channels(self):
        for c in list(self._ch.values()) + list(self.bar.values()) + list(self.cc.values()):
            if c.v != c.emitted:
                raise RuntimeError(f"channel {c.name}: reserved {c.v} != emitted {c.emitted}")



def emit_init(k: K):
    nc = k.nc
    qgp, qsy = k.ch("qgp"), k.ch("qsy")
    c_dv, c_pe, c_ac = k.ch("dv"), k.ch("pe"), k.ch("ac")
    with ExitStack() as ph:
        psl = ph.enter_context(nc.psum_tensor("ps_init", [128, 512], F32))
        e_w = qgp.reserve(3, 16)
        e_sy = qsy.reserve(2, 16)
        e_prod = c_dv.reserve(1)[0]
        e_mm = c_pe.reserve(1)[0]
        e_sum = c_dv.reserve(1)[0]
        e_exp = c_ac.reserve(1)[0]
        e_neg = c_dv.reserve(1)[0]
        with nc.Block() as b:
            @b.gpsimd
            def _(g):
                qgp.emit(g.dma_start(out=k.agw_in.ap(), in_=k.in_moew.ap()), 16)
                qgp.emit(g.dma_start(out=k.ident_b[:, :], in_=k.in_ident.ap()), 16)
                qgp.emit(g.dma_start(out=k.masks_b[:, :], in_=k.in_masks.ap()), 16)

            @b.sync
            def _(sy):
                qsy.emit(sy.dma_start(out=k.ident_f[:, :], in_=k.in_ident.ap()), 16)
                qsy.emit(sy.dma_start(out=k.lamt[:, :], in_=k.in_lam.ap()), 16)

            @b.vector
            def _(v):
                v.memset(k.ones_bf[:, :], 1.0)
                v.memset(k.ones_f[:, :], 1.0)
                v.memset(k.cst[:, 0:1], 1.0 / D)
                v.memset(k.cst[:, 1:2], 1.0 / 128.0)
                v.memset(k.cst[:, 2:3], LN_EPS)
                v.memset(k.cst[:, 3:4], RMS_EPS)
                v.memset(k.cst[:, 4:5], L1_EPS)
                v.memset(k.cst[:, 5:6], INV_SQRT_HALF0)
                v.wait_ge(qsy.h, e_sy[1])
                v.tensor_mul(k.scrcol[:, 0:1], k.lamt[:, 0:1], k.lamt[:, 2:3])
                v.tensor_mul(k.scrcol[:, 1:2], k.lamt[:, 1:2], k.lamt[:, 3:4])
                v.tensor_mul(k.scrcol[:, 2:3], k.lamt[:, 4:5], k.lamt[:, 6:7])
                c_dv.emit(v.tensor_mul(k.scrcol[:, 3:4], k.lamt[:, 5:6], k.lamt[:, 7:8]))

            @b.tensor
            def _(t):
                t.wait_ge(c_dv.h, e_prod)
                c_pe.emit(t.matmul(psl[:, 0:4], k.ones_f[:, :], k.scrcol[:, 0:4],
                                   start=True, stop=True))

            @b.vector
            def _(v):
                v.wait_ge(c_pe.h, e_mm)
                k.vst(v, v.tensor_copy(k.scrcol[:, 0:4], psl[:, 0:4]))
                v.tensor_add(k.scrcol[:, 4:5], k.scrcol[:, 0:1], k.scrcol[:, 1:2])
                c_dv.emit(v.tensor_add(k.scrcol[:, 5:6], k.scrcol[:, 2:3], k.scrcol[:, 3:4]))

            @b.scalar
            def _(s):
                s.wait_ge(c_dv.h, e_sum)
                s.activation(k.scrcol[:, 6:7], k.scrcol[:, 4:5], AF.Exp)
                c_ac.emit(s.activation(k.scrcol[:, 7:8], k.scrcol[:, 5:6], AF.Exp))

            @b.vector
            def _(v):
                v.wait_ge(c_ac.h, e_exp)
                k.vst(v, v.tensor_scalar(k.scrcol[:, 6:7], k.scrcol[:, 6:7], -1.0, None, ALU.mult))
                k.vst(v, v.tensor_add(k.neglam[:, :], k.scrcol[:, 7:8], k.scrcol[:, 6:7]))
                c_dv.emit(v.tensor_scalar(k.neglam[:, :], k.neglam[:, :], -LAMBDA_INIT, None, ALU.add))
        k.e_init_dv = e_neg


def emit_load_x(k: K):
    nc = k.nc
    qsy, c_pe, c_dv = k.ch("qsy"), k.ch("pe"), k.ch("dv")
    with ExitStack() as ph:
        xin = ph.enter_context(nc.sbuf_tensor("xin", [128, 2 * D], F32))
        psT = [ph.enter_context(nc.psum_tensor(f"psT{i}", [128, 512], F32)) for i in range(4)]
        NTOT = TCH * FC
        e_ld = qsy.reserve(TCH, 16)
        e_tr = c_pe.reserve(NTOT)
        e_cp = c_dv.reserve(NTOT)
        e_init = getattr(k, "e_init_dv", None)
        with nc.Block() as b:
            @b.sync
            def _(sy):
                for tc in range(TCH):
                    if tc >= 2:
                        sy.wait_ge(c_pe.h, e_tr[(tc - 1) * FC - 1])
                    qsy.emit(sy.dma_start(out=xin[:, (tc % 2) * D:(tc % 2 + 1) * D],
                                          in_=k.in_x[tc * 128:(tc + 1) * 128, :]), 16)

            @b.tensor
            def _(t):
                if e_init is not None:
                    t.wait_ge(c_dv.h, e_init)
                n = 0
                for tc in range(TCH):
                    t.wait_ge(qsy.h, e_ld[tc])
                    for fc in range(FC):
                        if n >= 4:
                            t.wait_ge(c_dv.h, e_cp[n - 4])
                        c_pe.emit(t.transpose(
                            psT[n % 4][:, 0:128],
                            xin[:, (tc % 2) * D + fc * 128:(tc % 2) * D + (fc + 1) * 128],
                            k.ident_f[:, :]))
                        n += 1

            @b.vector
            def _(v):
                n = 0
                for tc in range(TCH):
                    for fc in range(FC):
                        v.wait_ge(c_pe.h, e_tr[n])
                        c_dv.emit(v.tensor_copy(k.rch(fc, slice(tc * 128, (tc + 1) * 128)),
                                                psT[n % 4][:, 0:128]))
                        n += 1


def emit_ln(k: K, tag, ag=None):
    nc = k.nc
    qsy = k.ch("qsy")
    c_pe, c_ac, c_dv = k.ch("pe"), k.ch("ac"), k.ch("dv")
    with ExitStack() as ph:
        sq = ph.enter_context(nc.sbuf_tensor(f"ln_sq_{tag}", [128, 2 * TPC], BF16))
        mean = ph.enter_context(nc.sbuf_tensor(f"ln_mean_{tag}", [128, TPC], F32))
        ex2 = ph.enter_context(nc.sbuf_tensor(f"ln_ex2_{tag}", [128, TPC], F32))
        tsc = ph.enter_context(nc.sbuf_tensor(f"ln_tsc_{tag}", [128, TPC], F32))
        rstd = ph.enter_context(nc.sbuf_tensor(f"ln_rstd_{tag}", [128, TPC], F32))
        mrn = ph.enter_context(nc.sbuf_tensor(f"ln_mrn_{tag}", [128, TPC], F32))
        ps_s = ph.enter_context(nc.psum_tensor(f"ln_pss_{tag}", [128, TPC], F32))
        ps_q = ph.enter_context(nc.psum_tensor(f"ln_psq_{tag}", [128, TPC], F32))

        e_sq = c_ac.reserve(FC)
        e_stats = c_ac.reserve(1)[0]
        e_sqrt = c_ac.reserve(1)[0]
        e_xb = c_dv.reserve(FC)
        e_var = c_dv.reserve(1)[0]
        e_norm = c_dv.reserve(FC)
        e_mm = c_pe.reserve(FC)
        if ag is not None:
            e_ag = qsy.reserve(FC, 16)

        with nc.Block() as b:
            @b.scalar
            def _(s):
                for fc in range(FC):
                    if fc >= 2:
                        s.wait_ge(c_pe.h, e_mm[fc - 2])
                    c_ac.emit(s.activation(sq[:, (fc % 2) * TPC:(fc % 2 + 1) * TPC],
                                           k.rch(fc), AF.Square))
                s.wait_ge(c_pe.h, e_mm[-1])
                s.activation(mean[:, :], ps_s[:, :], AF.Copy, scale=k.cst[:, 0:1])
                c_ac.emit(s.activation(ex2[:, :], ps_q[:, :], AF.Copy, scale=k.cst[:, 0:1]))
                s.wait_ge(c_dv.h, e_var)
                c_ac.emit(s.activation(tsc[:, :], tsc[:, :], AF.Sqrt, bias=k.cst[:, 2:3]))

            @b.vector
            def _(v):
                for fc in range(FC):
                    c_dv.emit(v.tensor_copy(k.xch(fc), k.rch(fc)))
                v.wait_ge(c_ac.h, e_stats)
                v.tensor_mul(tsc[:, :], mean[:, :], mean[:, :])
                c_dv.emit(v.scalar_tensor_tensor(tsc[:, :], tsc[:, :], -1.0, ex2[:, :],
                                                 ALU.mult, ALU.add))
                v.wait_ge(c_ac.h, e_sqrt)
                k.vst(v, v.reciprocal(rstd[:, :], tsc[:, :]))
                v.scalar_tensor_tensor(mrn[:, :], mean[:, :], -1.0, rstd[:, :],
                                       ALU.mult, ALU.mult)
                for fc in range(FC):
                    v.tensor_mul(tsc[:, :], k.rch(fc), rstd[:, :])
                    c_dv.emit(v.tensor_add(k.xch(fc), tsc[:, :], mrn[:, :]))

            @b.tensor
            def _(t):
                for fc in range(FC):
                    t.wait_ge(c_ac.h, e_sq[fc])
                    t.wait_ge(c_dv.h, e_xb[fc])
                    st, sp = (fc == 0), (fc == FC - 1)
                    for nh in range(2):
                        sl = slice(nh * 512, (nh + 1) * 512)
                        t.matmul(ps_s[:, sl], k.ones_bf[:, :], k.xch(fc, sl), start=st, stop=sp)
                        _g = t.matmul(ps_q[:, sl], k.ones_bf[:, :],
                                 sq[:, (fc % 2) * TPC:(fc % 2 + 1) * TPC][:, sl], start=st, stop=sp)
                    c_pe.emit(_g)

            if ag is not None:
                ag_in, ccname, ag_outt = ag

                @b.sync
                def _(sy):
                    for fc in range(FC):
                        sy.wait_ge(c_dv.h, e_norm[fc])
                        qsy.emit(sy.dma_start(out=ag_in.ap()[fc * 128:(fc + 1) * 128, :],
                                              in_=k.xch(fc)), 16)


def emit_park_res(k: K):
    nc = k.nc
    qsy = k.ch("qsy")
    ev = qsy.reserve(FC, 16)
    with nc.Block() as b:
        @b.sync
        def _(sy):
            for fc in range(FC):
                qsy.emit(sy.dma_start(out=k.res_park.ap()[fc * 128:(fc + 1) * 128, :],
                                      in_=k.rch(fc)), 16)
            sy.wait_ge(qsy.h, ev[-1])


def emit_restore_res(k: K):
    nc = k.nc
    qsy = k.ch("qsy")
    ev = qsy.reserve(FC, 16)
    with nc.Block() as b:
        @b.sync
        def _(sy):
            for fc in range(FC):
                qsy.emit(sy.dma_start(out=k.rch(fc),
                                      in_=k.res_park.ap()[fc * 128:(fc + 1) * 128, :]), 16)
    return ev[-1]


def emit_dump(k: K, getter, cast=False):
    nc = k.nc
    with nc.Block() as b:
        if cast:
            qgp = k.ch("qgp")
            ev = qgp.reserve(FC, 16)

            @b.gpsimd
            def _(g):
                for fc in range(FC):
                    qgp.emit(g.dma_start(out=k.dbg[fc * 128:(fc + 1) * 128, :],
                                         in_=getter(fc)), 16)
                g.wait_ge(qgp.h, ev[-1])
        else:
            qsy = k.ch("qsy")
            ev = qsy.reserve(FC, 16)

            @b.sync
            def _(sy):
                for fc in range(FC):
                    qsy.emit(sy.dma_start(out=k.dbg[fc * 128:(fc + 1) * 128, :],
                                          in_=getter(fc)), 16)
                sy.wait_ge(qsy.h, ev[-1])


def emit_collective(k: K, kind, in_ap, out_ap, ccname):
    nc = k.nc
    tgt = k.cc[ccname].reserve(1)[0]
    with nc.Block() as b:
        @b.gpsimd
        def _(g):
            with allow_ici_all_to_all():
                k.cc[ccname].emit(g.collective_compute(
                    kind, ALU.bypass,
                    replica_groups=[list(range(NCORE))],
                    ins=[in_ap], outs=[out_ap]))
            g.wait_ge(k.cc[ccname].h, tgt)



def emit_attn0(k: K):
    nc = k.nc
    qsy, qgp = k.ch("qsy"), k.ch("qgp")
    c_pe, c_ac, c_dv = k.ch("pe"), k.ch("ac"), k.ch("dv")
    NTT = 16
    with ExitStack() as ph:
        sbf = lambda n, s, d: ph.enter_context(nc.sbuf_tensor(n, s, d))
        wq_b = sbf("a0_wq", [128, FC * HD0], BF16)
        wk_b = sbf("a0_wk", [128, FC * HD0], BF16)
        wv_b = sbf("a0_wv", [128, FC * HD0], BF16)
        xt = sbf("a0_xt", [128, 24 * 512], BF16)
        q1n = sbf("a0_q1n", [128, TOK], BF16)
        q2n = sbf("a0_q2n", [128, TOK], BF16)
        k1n = sbf("a0_k1n", [128, TOK], BF16)
        k2n = sbf("a0_k2n", [128, TOK], BF16)
        v_tm = sbf("a0_vtm", [128, 64 * HD0], BF16)
        sqs = sbf("a0_sqs", [128, 2 * 512], BF16)
        rst = sbf("a0_rst", [128, 2 * 512], F32)
        pexp = sbf("a0_pexp", [128, 4 * 512], BF16)
        y1b = sbf("a0_y1", [128, 2 * 1024], F32)
        y2b = sbf("a0_y2", [128, 2 * 1024], F32)
        rden = sbf("a0_rden", [128, 2 * 512], F32)
        ytmp = sbf("a0_ytmp", [128, 2 * 1024], BF16)
        ps = [ph.enter_context(nc.psum_tensor(f"a0_ps{i}", [128, 512], F32)) for i in range(8)]

        e_w = qgp.reserve(3, 16)
        e_xt = qsy.reserve(NTT * FC, 16)
        e_proj, e_p2, e_stat = [], [], []
        for nt in range(NTT):
            e_proj.append(c_pe.reserve(FC))
            e_p2.append(c_pe.reserve(FC))
            e_stat.append(c_pe.reserve(4))
        e_sqr = [c_ac.reserve(8) for _ in range(NTT)]
        e_evac = [c_dv.reserve(12) for _ in range(NTT)]

        with nc.Block() as b:
            @b.gpsimd
            def _(g):
                qgp.emit(g.dma_start(out=wq_b[:, :], in_=k.in_wq.ap()), 16)
                qgp.emit(g.dma_start(out=wk_b[:, :], in_=k.in_wk.ap()), 16)
                qgp.emit(g.dma_start(out=wv_b[:, :], in_=k.in_wv.ap()), 16)

            @b.sync
            def _(sy):
                sy.wait_ge(k.cc["ag0"].h, 1)
                n = 0
                for nt in range(NTT):
                    r, co = nt // 2, (nt % 2) * 512
                    for fc in range(FC):
                        if n >= 24:
                            pt = n - 24
                            sy.wait_ge(c_pe.h, e_p2[pt // FC][pt % FC])
                        qsy.emit(sy.dma_start(
                            out=xt[:, (n % 24) * 512:(n % 24 + 1) * 512],
                            in_=k.ag0_out.ap()[r * D + fc * 128: r * D + (fc + 1) * 128,
                                               co:co + 512]), 16)
                        n += 1

            @b.tensor
            def _(t):
                t.wait_ge(qgp.h, e_w[2])
                for nt in range(NTT):
                    if nt >= 1:
                        t.wait_ge(c_dv.h, e_evac[nt - 1][-1])
                    for fc in range(FC):
                        n = nt * FC + fc
                        t.wait_ge(qsy.h, e_xt[n])
                        xs = xt[:, (n % 24) * 512:(n % 24 + 1) * 512]
                        st, sp = (fc == 0), (fc == FC - 1)
                        t.matmul(ps[0][:, :], wq_b[:, fc * HD0:fc * HD0 + 128], xs, start=st, stop=sp)
                        t.matmul(ps[1][:, :], wq_b[:, fc * HD0 + 128:fc * HD0 + 256], xs, start=st, stop=sp)
                        t.matmul(ps[2][:, :], wk_b[:, fc * HD0:fc * HD0 + 128], xs, start=st, stop=sp)
                        t.matmul(ps[3][:, :], wk_b[:, fc * HD0 + 128:fc * HD0 + 256], xs, start=st, stop=sp)
                        for tc in range(2):
                            _g = t.matmul(ps[4 + tc][:, 0:256],
                                          xs[:, tc * 128:(tc + 1) * 128],
                                          wv_b[:, fc * HD0:(fc + 1) * HD0], start=st, stop=sp)
                        c_pe.emit(_g)
                    t.wait_ge(c_dv.h, e_evac[nt][1])
                    for fc in range(FC):
                        n = nt * FC + fc
                        xs = xt[:, (n % 24) * 512:(n % 24 + 1) * 512]
                        st, sp = (fc == 0), (fc == FC - 1)
                        for tc in range(2, 4):
                            _g = t.matmul(ps[4 + tc - 2][:, 0:256],
                                          xs[:, tc * 128:(tc + 1) * 128],
                                          wv_b[:, fc * HD0:(fc + 1) * HD0], start=st, stop=sp)
                        c_pe.emit(_g)
                    for part in range(4):
                        t.wait_ge(c_ac.h, e_sqr[nt][2 * part])
                        _g = t.matmul(ps[6 + part % 2][:, :], k.ones_bf[:, :],
                                           sqs[:, (part % 2) * 512:(part % 2 + 1) * 512],
                                           start=True, stop=True)
                        c_pe.emit(_g)

            @b.scalar
            def _(s):
                for nt in range(NTT):
                    s.wait_ge(c_pe.h, e_proj[nt][FC - 1])
                    for part in range(4):
                        if part >= 2:
                            s.wait_ge(c_pe.h, e_stat[nt][part - 2])
                            s.wait_ge(c_dv.h, e_evac[nt][2 + 2 * (part - 2) + 1])
                        c_ac.emit(s.activation(sqs[:, (part % 2) * 512:(part % 2 + 1) * 512],
                                               ps[part][:, :], AF.Square))
                        s.wait_ge(c_pe.h, e_stat[nt][part])
                        c_ac.emit(s.activation(rst[:, (part % 2) * 512:(part % 2 + 1) * 512],
                                               ps[6 + part % 2][:, :], AF.Sqrt,
                                               scale=k.cst[:, 1:2], bias=k.cst[:, 3:4]))

            @b.vector
            def _(v):
                dsts = [q1n, q2n, k1n, k2n]
                for nt in range(NTT):
                    for tc in range(2):
                        v.wait_ge(c_pe.h, e_proj[nt][FC - 1])
                        c_dv.emit(v.tensor_copy(
                            v_tm[:, (nt * 4 + tc) * 256:(nt * 4 + tc + 1) * 256],
                            ps[4 + tc][:, 0:256]))
                    for part in range(4):
                        v.wait_ge(c_ac.h, e_sqr[nt][2 * part + 1])
                        sl = slice((part % 2) * 512, (part % 2 + 1) * 512)
                        c_dv.emit(v.reciprocal(rst[:, sl], rst[:, sl]))
                        v.wait_ge(c_dv.h, e_evac[nt][2 + 2 * part])
                        c_dv.emit(v.tensor_mul(dsts[part][:, nt * 512:(nt + 1) * 512],
                                               ps[part][:, :], rst[:, sl]))
                    for tc in range(2, 4):
                        v.wait_ge(c_pe.h, e_p2[nt][FC - 1])
                        c_dv.emit(v.tensor_copy(
                            v_tm[:, (nt * 4 + tc) * 256:(nt * 4 + tc + 1) * 256],
                            ps[4 + tc - 2][:, 0:256]))

        plan = []
        for mt in range(NTT):
            for a in range(2):
                plan.append((mt, a, 4 * (mt + 1)))

        e_sc, e_av, e_exp, e_msk, e_fin, e_cmb, e_ydma = {}, {}, {}, {}, {}, {}, {}
        for (mt, a, smax) in plan:
            scl_, avl = [None] * smax, [None] * smax
            scl_[0] = c_pe.reserve(1)[0]
            for sc in range(1, smax):
                scl_[sc] = c_pe.reserve(1)[0]
                avl[sc - 1] = c_pe.reserve(1)[0]
            avl[smax - 1] = c_pe.reserve(1)[0]
            e_sc[(mt, a)] = scl_
            e_av[(mt, a)] = avl
            e_exp[(mt, a)] = c_ac.reserve(smax)
            e_msk[(mt, a)] = c_dv.reserve(4)
            e_fin[(mt, a)] = c_dv.reserve(3)
            if a == 1:
                e_cmb[mt] = c_dv.reserve(2)
        for mt in range(NTT):
            e_ydma[mt] = qsy.reserve(2, 16)

        with nc.Block() as b:
            @b.tensor
            def _(t):
                t.wait_ge(c_dv.h, e_evac[NTT - 1][-1])
                for (mt, a, smax) in plan:
                    qn = q1n if a == 0 else q2n
                    kn = k1n if a == 0 else k2n
                    if (mt, a) != (0, 0):
                        pm, pa = (mt, 0) if a == 1 else (mt - 1, 1)
                        t.wait_ge(c_dv.h, e_fin[(pm, pa)][2])

                    def av(sc):
                        j = sc - 4 * mt
                        if j >= 0:
                            t.wait_ge(c_dv.h, e_msk[(mt, a)][j])
                        else:
                            t.wait_ge(c_ac.h, e_exp[(mt, a)][sc])
                        pslot = pexp[:, (sc % 4) * 512:(sc % 4 + 1) * 512]
                        st, sp = (sc == 0), (sc == smax - 1)
                        t.matmul(ps[2][:, :], v_tm[:, sc * HD0:sc * HD0 + 128], pslot,
                                 start=st, stop=sp)
                        t.matmul(ps[3][:, :], v_tm[:, sc * HD0 + 128:sc * HD0 + 256], pslot,
                                 start=st, stop=sp)
                        _g = t.matmul(ps[4][:, :], k.ones_bf[:, :], pslot, start=st, stop=sp)
                        c_pe.emit(_g)

                    for sc in range(smax):
                        if sc >= 2:
                            t.wait_ge(c_ac.h, e_exp[(mt, a)][sc - 2])
                        _g = t.matmul(ps[sc % 2][:, :], kn[:, sc * 128:(sc + 1) * 128],
                                           qn[:, mt * 512:(mt + 1) * 512], start=True, stop=True)
                        c_pe.emit(_g)
                        if sc >= 1:
                            av(sc - 1)
                    av(smax - 1)

            @b.scalar
            def _(s):
                for (mt, a, smax) in plan:
                    for sc in range(smax):
                        s.wait_ge(c_pe.h, e_sc[(mt, a)][sc])
                        if sc >= 4:
                            s.wait_ge(c_pe.h, e_av[(mt, a)][sc - 4])
                        c_ac.emit(s.activation(pexp[:, (sc % 4) * 512:(sc % 4 + 1) * 512],
                                               ps[sc % 2][:, :], AF.Exp, scale=k.cst[:, 5:6]))

            @b.vector
            def _(v):
                for (mt, a, smax) in plan:
                    for j in range(4):
                        sc = 4 * mt + j
                        v.wait_ge(c_ac.h, e_exp[(mt, a)][sc])
                        sl = slice((sc % 4) * 512, (sc % 4 + 1) * 512)
                        c_dv.emit(v.tensor_mul(pexp[:, sl], pexp[:, sl],
                                               k.masks_b[:, j * 512:(j + 1) * 512]))
                    v.wait_ge(c_pe.h, e_av[(mt, a)][smax - 1])
                    ybuf = y1b if a == 0 else y2b
                    rsl = slice(a * 512, (a + 1) * 512)
                    c_dv.emit(v.reciprocal(rden[:, rsl], ps[4][:, :]))
                    mo = (mt % 2) * 1024
                    c_dv.emit(v.tensor_mul(ybuf[:, mo:mo + 512], ps[2][:, :], rden[:, rsl]))
                    c_dv.emit(v.tensor_mul(ybuf[:, mo + 512:mo + 1024], ps[3][:, :], rden[:, rsl]))
                    if a == 1:
                        if mt >= 2:
                            v.wait_ge(qsy.h, e_ydma[mt - 2][1])
                        for e in range(2):
                            c_dv.emit(v.scalar_tensor_tensor(
                                ytmp[:, mo + e * 512:mo + (e + 1) * 512],
                                y2b[:, mo + e * 512:mo + (e + 1) * 512],
                                k.neglam[:, :],
                                y1b[:, mo + e * 512:mo + (e + 1) * 512],
                                ALU.mult, ALU.add))

            @b.sync
            def _(sy):
                for mt in range(NTT):
                    sy.wait_ge(c_dv.h, e_cmb[mt][1])
                    mo = (mt % 2) * 1024
                    r, co = mt // 2, (mt % 2) * 512
                    for e in range(2):
                        qsy.emit(sy.dma_start(
                            out=k.a2y_in.ap()[r * HD0 + e * 128: r * HD0 + (e + 1) * 128,
                                              co:co + 512],
                            in_=ytmp[:, mo + e * 512:mo + (e + 1) * 512]), 16)



def emit_attn0_out(k: K):
    nc = k.nc
    qsy, qgp = k.ch("qsy"), k.ch("qgp")
    c_pe, c_dv = k.ch("pe"), k.ch("dv")
    e_restore = emit_restore_res(k)
    with ExitStack() as ph:
        ys = ph.enter_context(nc.sbuf_tensor("a0o_ys", [128, FC * TPC], BF16))
        wo_b = ph.enter_context(nc.sbuf_tensor("a0o_wo", [128, 2 * D], BF16))
        psA = [ph.enter_context(nc.psum_tensor(f"a0o_ps{i}", [128, 512], F32)) for i in range(2)]
        e_ys = qsy.reserve(FC, 16)
        e_wo = qgp.reserve(FC, 16)
        e_mm = c_pe.reserve(FC * 2)
        e_add = c_dv.reserve(FC * 2)
        with nc.Block() as b:
            @b.sync
            def _(sy):
                sy.wait_ge(k.cc["a2y"].h, 1)
                for fc in range(FC):
                    qsy.emit(sy.dma_start(out=ys[:, fc * TPC:(fc + 1) * TPC],
                                          in_=k.a2y_out.ap()[fc * 128:(fc + 1) * 128, :]), 16)

            @b.gpsimd
            def _(g):
                for oc in range(FC):
                    if oc >= 2:
                        g.wait_ge(c_pe.h, e_mm[2 * (oc - 1) - 1])
                    qgp.emit(g.dma_start(out=wo_b[:, (oc % 2) * D:(oc % 2 + 1) * D],
                                         in_=k.in_wo[oc, :, :]), 16)

            @b.tensor
            def _(t):
                t.wait_ge(qsy.h, e_ys[-1])
                n = 0
                for oc in range(FC):
                    t.wait_ge(qgp.h, e_wo[oc])
                    for th in range(2):
                        if n >= 2:
                            t.wait_ge(c_dv.h, e_add[n - 2])
                        for kc in range(FC):
                            _g = t.matmul(psA[n % 2][:, :],
                                     wo_b[:, (oc % 2) * D + kc * 128:(oc % 2) * D + (kc + 1) * 128],
                                     ys[:, kc * TPC + th * 512: kc * TPC + (th + 1) * 512],
                                     start=(kc == 0), stop=(kc == FC - 1))
                        c_pe.emit(_g)
                        n += 1

            @b.vector
            def _(v):
                v.wait_ge(qsy.h, e_restore)
                n = 0
                for oc in range(FC):
                    for th in range(2):
                        v.wait_ge(c_pe.h, e_mm[n])
                        sl = slice(th * 512, (th + 1) * 512)
                        c_dv.emit(v.scalar_tensor_tensor(
                            k.rch(oc, sl), psA[n % 2][:, :], 1.0 - LAMBDA_INIT,
                            k.rch(oc, sl), ALU.mult, ALU.add))
                        n += 1


def emit_router(k: K, wt32):
    nc = k.nc
    qgp = k.ch("qgp")
    c_pe, c_ac, c_dv = k.ch("pe"), k.ch("ac"), k.ch("dv")
    with ExitStack() as ph:
        rt_b = ph.enter_context(nc.sbuf_tensor("rt_b", [128, FC * NE], BF16))
        lgt = ph.enter_context(nc.sbuf_tensor("rt_lgt", [128, NE], F32))
        prb = ph.enter_context(nc.sbuf_tensor("rt_prb", [128, NE], F32))
        tpr = ph.enter_context(nc.sbuf_tensor("rt_tpr", [128, NE], F32))
        wtt = ph.enter_context(nc.sbuf_tensor("rt_wtt", [128, NE], F32))
        msk = ph.enter_context(nc.sbuf_tensor("rt_msk", [128, NE], F32))
        mcol = ph.enter_context(nc.sbuf_tensor("rt_mcol", [128, 1], F32))
        psL = ph.enter_context(nc.psum_tensor("rt_psL", [128, 512], F32))
        psT = ph.enter_context(nc.psum_tensor("rt_psT", [128, 512], F32))
        e_w = qgp.reserve(1, 16)
        e_lg, e_tr = [], []
        for tc in range(TCH):
            e_lg.append(c_pe.reserve(1)[0])
            e_tr.append(c_pe.reserve(1)[0])
        e_lgfree, e_shift, e_wt, e_cp = [], [], [], []
        for tc in range(TCH):
            e_lgfree.append(c_dv.reserve(1)[0])
            e_shift.append(c_dv.reserve(1)[0])
            e_wt.append(c_dv.reserve(1)[0])
            e_cp.append(c_dv.reserve(1)[0])
        e_ex = [c_ac.reserve(1)[0] for _ in range(TCH)]
        with nc.Block() as b:
            @b.gpsimd
            def _(g):
                qgp.emit(g.dma_start(out=rt_b[:, :], in_=k.in_router.ap()), 16)

            @b.tensor
            def _(t):
                t.wait_ge(qgp.h, e_w[0])
                for tc in range(TCH):
                    if tc >= 1:
                        t.wait_ge(c_dv.h, e_lgfree[tc - 1])
                        t.wait_ge(c_dv.h, e_cp[tc - 1])
                    for fc in range(FC):
                        _g = t.matmul(psL[:, 0:NE],
                                 k.xln[:, fc * TPC + tc * 128: fc * TPC + (tc + 1) * 128],
                                 rt_b[:, fc * NE:(fc + 1) * NE],
                                 start=(fc == 0), stop=(fc == FC - 1))
                    c_pe.emit(_g)
                    t.wait_ge(c_dv.h, e_wt[tc])
                    _g = t.transpose(psT[0:NE, 0:128], wtt[:, :], k.ident_f[:, :])
                    c_pe.emit(_g)

            @b.scalar
            def _(s):
                for tc in range(TCH):
                    s.wait_ge(c_dv.h, e_shift[tc])
                    c_ac.emit(s.activation(prb[:, :], lgt[:, :], AF.Exp))

            @b.vector
            def _(v):
                for tc in range(TCH):
                    v.wait_ge(c_pe.h, e_lg[tc])
                    c_dv.emit(v.tensor_copy(lgt[:, :], psL[:, 0:NE]))
                    k.vst(v, v.reduce_max(mcol[:, :], lgt[:, :], AX.X, negate=True))
                    c_dv.emit(v.tensor_single_scalar(lgt[:, :], lgt[:, :], mcol[:, :], ALU.add))
                    v.wait_ge(c_ac.h, e_ex[tc])
                    k.vst(v, v.reduce_sum(mcol[:, :], prb[:, :], AX.X))
                    k.vst(v, v.reciprocal(mcol[:, :], mcol[:, :]))
                    k.vst(v, v.tensor_single_scalar(prb[:, :], prb[:, :], mcol[:, :], ALU.mult))
                    k.vst(v, v.tensor_copy(tpr[:, :], prb[:, :]))
                    for _ in range(3):
                        k.vst(v, v.reduce_max(mcol[:, :], tpr[:, :], AX.X))
                        k.vst(v, v.tensor_single_scalar(msk[:, :], tpr[:, :], mcol[:, :], ALU.is_ge))
                        k.vst(v, v.scalar_tensor_tensor(tpr[:, :], msk[:, :], -1e30, tpr[:, :],
                                                        ALU.mult, ALU.add))
                    k.vst(v, v.reduce_max(mcol[:, :], tpr[:, :], AX.X))
                    k.vst(v, v.tensor_single_scalar(msk[:, :], prb[:, :], mcol[:, :], ALU.is_ge))
                    k.vst(v, v.tensor_mul(wtt[:, :], prb[:, :], msk[:, :]))
                    k.vst(v, v.reduce_sum(mcol[:, :], wtt[:, :], AX.X))
                    k.vst(v, v.reciprocal(mcol[:, :], mcol[:, :]))
                    c_dv.emit(v.tensor_single_scalar(wtt[:, :], wtt[:, :], mcol[:, :], ALU.mult))
                    v.wait_ge(c_pe.h, e_tr[tc])
                    c_dv.emit(v.tensor_copy(wt32[0:NE, tc * 128:(tc + 1) * 128],
                                            psT[0:NE, 0:128]))


def emit_moe(k: K, wt32):
    nc = k.nc
    qsy = k.ch("qsy")
    c_pe, c_ac, c_dv = k.ch("pe"), k.ch("ac"), k.ch("dv")
    agw = k.agw_out.ap().rearrange("r c -> (r c)")
    with ExitStack() as ph:
        sbf = lambda n, s, d: ph.enter_context(nc.sbuf_tensor(n, s, d))
        gw_t = sbf("moe_gw", [128, 2 * 2048], BF16)
        uw_t = sbf("moe_uw", [128, 2 * 2048], BF16)
        dw_t = sbf("moe_dw", [128, 2 * 768], BF16)
        wrow = sbf("moe_wrow", [1, 2 * TPC], F32)
        h_sb = sbf("moe_h", [128, MCI * TPC], BF16)
        gs = sbf("moe_gs", [128, TPC], BF16)
        tts = sbf("moe_tts", [128, TPC], F32)
        wtb = sbf("moe_wtb", [128, TPC], F32)
        psG = ph.enter_context(nc.psum_tensor("moe_psG", [128, TPC], F32))
        psU = ph.enter_context(nc.psum_tensor("moe_psU", [128, TPC], F32))
        psW = ph.enter_context(nc.psum_tensor("moe_psW", [128, TPC], F32))
        psD = [ph.enter_context(nc.psum_tensor(f"moe_psD{i}", [128, 512], F32)) for i in range(2)]

        def gw_off(e, mc):
            return (e // EPC) * RANK_ELEMS + GW_BASE + (e % EPC) * EXP_ELEMS + mc * GW_TILE

        def uw_off(e, mc):
            return (e // EPC) * RANK_ELEMS + UW_BASE + (e % EPC) * EXP_ELEMS + mc * GW_TILE

        def dw_off(e, oc):
            return (e // EPC) * RANK_ELEMS + DW_BASE + (e % EPC) * EXP_ELEMS + oc * DW_TILE

        qgp = k.ch("qgp")
        e_gu, e_dw = [], []
        e_wtb, e_gmm, e_umm, e_dmm = [], [], [], []
        e_sil = []
        e_wtbc, e_h, e_dadd = [], [], []
        e_wrow = [qgp.reserve(1, 16)[0] for _ in range(NE)]
        for e in range(NE):
            e_gu.append([qsy.reserve(2, 16) for _ in range(MCI)])
            e_dw.append([qsy.reserve(1, 16)[0] for _ in range(FC)])
            e_wtb.append(c_pe.reserve(1)[0])
            gm, um = [], []
            for mc in range(MCI):
                gm.append(c_pe.reserve(2))
                um.append(c_pe.reserve(2))
            e_gmm.append(gm)
            e_umm.append(um)
            e_dmm.append([c_pe.reserve(2) for _ in range(FC)])
            e_sil.append([c_ac.reserve(2) for _ in range(MCI)])
            e_wtbc.append(c_dv.reserve(1)[0])
            e_h.append([c_dv.reserve(2) for _ in range(MCI)])
            e_dadd.append([c_dv.reserve(2) for _ in range(FC)])

        with nc.Block() as b:
            @b.gpsimd
            def _(g):
                for e in range(NE):
                    if e >= 2:
                        g.wait_ge(c_pe.h, e_wtb[e - 2])
                    qgp.emit(g.dma_start(out=wrow[0:1, (e % 2) * TPC:(e % 2 + 1) * TPC],
                                         in_=wt32[e:e + 1, :]), 16)

            @b.sync
            def _(sy):
                sy.wait_ge(k.cc["agw"].h, 1)
                for e in range(NE):
                    for mc in range(MCI):
                        g = e * MCI + mc
                        if g >= 2:
                            pe_, pmc = (g - 2) // MCI, (g - 2) % MCI
                            sy.wait_ge(c_pe.h, e_umm[pe_][pmc][1])
                        o = gw_off(e, mc)
                        qsy.emit(sy.dma_start(
                            out=gw_t[:, (g % 2) * 2048:(g % 2 + 1) * 2048],
                            in_=agw[o:o + GW_TILE].rearrange("(p f) -> p f", p=128)), 16)
                        o = uw_off(e, mc)
                        qsy.emit(sy.dma_start(
                            out=uw_t[:, (g % 2) * 2048:(g % 2 + 1) * 2048],
                            in_=agw[o:o + GW_TILE].rearrange("(p f) -> p f", p=128)), 16)
                    for oc in range(FC):
                        g = e * FC + oc
                        if g >= 2:
                            pe_, poc = (g - 2) // FC, (g - 2) % FC
                            sy.wait_ge(c_pe.h, e_dmm[pe_][poc][1])
                        o = dw_off(e, oc)
                        qsy.emit(sy.dma_start(
                            out=dw_t[:, (g % 2) * 768:(g % 2 + 1) * 768],
                            in_=agw[o:o + DW_TILE].rearrange("(p f) -> p f", p=128)), 16)

            @b.scalar
            def _(s):
                for e in range(NE):
                    for mc in range(MCI):
                        for th in range(2):
                            s.wait_ge(c_pe.h, e_gmm[e][mc][th])
                            if mc >= 1:
                                s.wait_ge(c_dv.h, e_h[e][mc - 1][th])
                            elif e >= 1:
                                s.wait_ge(c_dv.h, e_h[e - 1][MCI - 1][th])
                            c_ac.emit(s.activation(gs[:, th * 512:(th + 1) * 512],
                                                   psG[:, th * 512:(th + 1) * 512], AF.Silu))

            @b.tensor
            def _(t):
                for e in range(NE):
                    if e >= 1:
                        t.wait_ge(c_dv.h, e_wtbc[e - 1])
                    t.wait_ge(qgp.h, e_wrow[e])
                    wsl = wrow[0:1, (e % 2) * TPC:(e % 2 + 1) * TPC]
                    t.matmul(psW[:, 0:512], k.ones_f[0:1, :], wsl[0:1, 0:512],
                             start=True, stop=True)
                    _g = t.matmul(psW[:, 512:1024], k.ones_f[0:1, :], wsl[0:1, 512:1024],
                                       start=True, stop=True)
                    c_pe.emit(_g)
                    for mc in range(MCI):
                        g = e * MCI + mc
                        t.wait_ge(qsy.h, e_gu[e][mc][1])
                        if g >= 1:
                            pe_, pmc = (g - 1) // MCI, (g - 1) % MCI
                            t.wait_ge(c_ac.h, e_sil[pe_][pmc][1])
                            t.wait_ge(c_dv.h, e_h[pe_][pmc][1])
                        for th in range(2):
                            for kc in range(FC):
                                _g = t.matmul(psG[:, th * 512:(th + 1) * 512],
                                         gw_t[:, (g % 2) * 2048 + kc * 128:(g % 2) * 2048 + (kc + 1) * 128],
                                         k.xln[:, kc * TPC + th * 512: kc * TPC + (th + 1) * 512],
                                         start=(kc == 0), stop=(kc == FC - 1))
                            c_pe.emit(_g)
                        for th in range(2):
                            for kc in range(FC):
                                _g = t.matmul(psU[:, th * 512:(th + 1) * 512],
                                         uw_t[:, (g % 2) * 2048 + kc * 128:(g % 2) * 2048 + (kc + 1) * 128],
                                         k.xln[:, kc * TPC + th * 512: kc * TPC + (th + 1) * 512],
                                         start=(kc == 0), stop=(kc == FC - 1))
                            c_pe.emit(_g)
                    t.wait_ge(c_dv.h, e_h[e][MCI - 1][1])
                    for oc in range(FC):
                        g = e * FC + oc
                        t.wait_ge(qsy.h, e_dw[e][oc])
                        for th in range(2):
                            if g >= 1:
                                pe_, poc = (g - 1) // FC, (g - 1) % FC
                                t.wait_ge(c_dv.h, e_dadd[pe_][poc][th])
                            for kc in range(MCI):
                                _g = t.matmul(psD[th][:, :],
                                         dw_t[:, (g % 2) * 768 + kc * 128:(g % 2) * 768 + (kc + 1) * 128],
                                         h_sb[:, kc * TPC + th * 512: kc * TPC + (th + 1) * 512],
                                         start=(kc == 0), stop=(kc == MCI - 1))
                            c_pe.emit(_g)

            @b.vector
            def _(v):
                for e in range(NE):
                    v.wait_ge(c_pe.h, e_wtb[e])
                    c_dv.emit(v.tensor_copy(wtb[:, :], psW[:, :]))
                    for mc in range(MCI):
                        for th in range(2):
                            sl = slice(th * 512, (th + 1) * 512)
                            v.wait_ge(c_pe.h, e_umm[e][mc][th])
                            v.tensor_mul(tts[:, sl], psU[:, sl], wtb[:, sl])
                            v.wait_ge(c_ac.h, e_sil[e][mc][th])
                            c_dv.emit(v.tensor_mul(
                                h_sb[:, mc * TPC + th * 512: mc * TPC + (th + 1) * 512],
                                tts[:, sl], gs[:, sl]))
                    for oc in range(FC):
                        for th in range(2):
                            v.wait_ge(c_pe.h, e_dmm[e][oc][th])
                            sl = slice(th * 512, (th + 1) * 512)
                            c_dv.emit(v.tensor_tensor(k.rch(oc, sl), psD[th][:, :],
                                                      k.rch(oc, sl), ALU.add))


def emit_attn1(k: K):
    nc = k.nc
    qsy, qgp, qsc = k.ch("qsy"), k.ch("qgp"), k.ch("qsc")
    c_pe, c_ac, c_dv = k.ch("pe"), k.ch("ac"), k.ch("dv")
    NTT = 16
    with ExitStack() as ph:
        sbf = lambda n, s, d: ph.enter_context(nc.sbuf_tensor(n, s, d))
        w_b = sbf("a1_w", [128, FC * 1024], BF16)
        xt = sbf("a1_xt", [128, 24 * 512], BF16)
        qsb = [sbf(f"a1_q{h}", [128, TOK], BF16) for h in range(2)]
        ksb = [sbf(f"a1_k{h}", [128, TOK], BF16) for h in range(2)]
        v_tm = sbf("a1_vtm", [128, 64 * 256], BF16)
        ubt = sbf("a1_ubt", [128, 2 * 1024], BF16)
        slp = sbf("a1_slp", [128, 2], F32)
        nsc = sbf("a1_nsc", [128, 2], F32)
        arrs = sbf("a1_arrs", [128, BLK], F32)
        arr2 = sbf("a1_arr2", [128, 2], F32)
        idxs = sbf("a1_idxs", [128, 2 * BLK], F32)
        dmsk = sbf("a1_dmsk", [128, 2 * BLK], F32)
        cc256 = sbf("a1_c256", [128, 1], F32)
        qdec = [sbf(f"a1_qdec{h}", [128, BLK], F32) for h in range(2)]
        kdec = [sbf(f"a1_kdec{h}", [128, 2], F32) for h in range(2)]
        ddt = [sbf(f"a1_ddt{h}", [128, 2 * BLK], F32) for h in range(2)]
        bdc = [sbf(f"a1_bdc{h}", [128, 1], F32) for h in range(2)]
        kv = [sbf(f"a1_kv{i}", [128, 128], F32) for i in range(2)]
        kv_b = sbf("a1_kvb", [128, 128], BF16)
        qd_t = sbf("a1_qdt", [128, BLK], BF16)
        tnd = sbf("a1_tnd", [128, BLK], F32)
        dumm = sbf("a1_dumm", [1, 8], F32)
        pdd = sbf("a1_pdd", [128, 2 * BLK], BF16)
        ktm = sbf("a1_ktm", [128, 2 * 128], BF16)
        ot = sbf("a1_ot", [128, 4 * BLK], BF16)

        e_w = qgp.reserve(4, 16)
        e_tab = qsy.reserve(6, 16)
        e_xt = qsy.reserve(NTT * FC + 4, 16)
        e_ns = c_dv.reserve(1)[0]
        e_ddm = c_dv.reserve(2)
        e_tbl = c_ac.reserve(8)
        e_proj, e_p2 = [], []
        for nt in range(NTT):
            e_proj.append(c_pe.reserve(FC))
            e_p2.append(c_pe.reserve(FC))
        e_sil = [c_ac.reserve(4) for _ in range(NTT)]
        e_evac = [c_dv.reserve(6) for _ in range(NTT)]
        e_udma = [qsc.reserve(2, 16) for _ in range(NTT)]

        with ExitStack() as pp:
            ps = [pp.enter_context(nc.psum_tensor(f"a1p_ps{i}", [128, 512], F32)) for i in range(8)]
            with nc.Block() as b:
                @b.gpsimd
                def _(g):
                    for i in range(4):
                        qgp.emit(g.dma_start(out=w_b[:, i * 4096:(i + 1) * 4096],
                                             in_=k.in_l1w[:, i * 4096:(i + 1) * 4096]), 16)

                @b.sync
                def _(sy):
                    qsy.emit(sy.dma_start(out=slp[:, :], in_=k.in_slopes.ap()), 16)
                    qsy.emit(sy.dma_start(out=arrs[:, :], in_=k.in_arr.ap()), 16)
                    qsy.emit(sy.dma_start(out=arr2[:, :], in_=k.in_arr2t.ap()), 16)
                    qsy.emit(sy.dma_start(out=idxs[:, :], in_=k.in_idx.ap()), 16)
                    qsy.emit(sy.dma_start(out=cc256[:, :], in_=k.in_c256.ap()), 16)
                    qsy.emit(sy.dma_start(out=dmsk[:, :], in_=k.in_dmask.ap()), 16)
                    sy.wait_ge(k.cc["ag2"].h, 1)
                    n = 0
                    for nt in range(NTT):
                        r, co = nt // 2, (nt % 2) * 512
                        for fc in range(FC):
                            if n >= 24:
                                pt = n - 24
                                sy.wait_ge(c_pe.h, e_p2[pt // FC][pt % FC])
                            qsy.emit(sy.dma_start(
                                out=xt[:, (n % 24) * 512:(n % 24 + 1) * 512],
                                in_=k.ag2_out.ap()[r * D + fc * 128: r * D + (fc + 1) * 128,
                                                   co:co + 512]), 16)
                            n += 1
                    for _ in range(4):
                        qsy.emit(sy.dma_start(
                            out=slp[:, 0:1],
                            in_=k.in_slopes.ap()[:, 0:1],
                            allow_slow_non_contiguous=True), 16)

                @b.vector
                def _(v):
                    v.wait_ge(qsy.h, e_tab[0])
                    c_dv.emit(v.tensor_scalar(nsc[:, :], slp[:, :], -1.0, None, ALU.mult))
                    for h in range(2):
                        v.wait_ge(c_ac.h, e_tbl[4 * h + 2])
                        c_dv.emit(v.tensor_mul(ddt[h][:, :], ddt[h][:, :], dmsk[:, :]))
                    for nt in range(NTT):
                        for h in range(2):
                            v.wait_ge(c_pe.h, e_proj[nt][FC - 1])
                            if nt >= 2:
                                v.wait_ge(qsc.h, e_udma[nt - 2][1])
                            c_dv.emit(v.tensor_copy(
                                ubt[:, (nt % 2) * 1024 + h * 512:(nt % 2) * 1024 + (h + 1) * 512],
                                ps[4 + h][:, :]))
                        for tc in range(2):
                            c_dv.emit(v.tensor_copy(
                                v_tm[:, (nt * 4 + tc) * 256:(nt * 4 + tc + 1) * 256],
                                ps[6 + tc][:, 0:256]))
                        for tc in range(2, 4):
                            v.wait_ge(c_pe.h, e_p2[nt][FC - 1])
                            c_dv.emit(v.tensor_copy(
                                v_tm[:, (nt * 4 + tc) * 256:(nt * 4 + tc + 1) * 256],
                                ps[6 + tc - 2][:, 0:256]))

                @b.scalar
                def _(s):
                    s.wait_ge(qsy.h, e_tab[5])
                    s.wait_ge(c_dv.h, e_ns)
                    for h in range(2):
                        c_ac.emit(s.activation(qdec[h][:, :], arrs[:, :], AF.Exp,
                                               scale=nsc[:, h:h + 1]))
                        c_ac.emit(s.activation(kdec[h][:, :], arr2[:, :], AF.Exp,
                                               scale=nsc[:, h:h + 1]))
                        c_ac.emit(s.activation(ddt[h][:, :], idxs[:, :], AF.Exp,
                                               scale=nsc[:, h:h + 1]))
                        c_ac.emit(s.activation(bdc[h][:, :], cc256[:, :], AF.Exp,
                                               scale=nsc[:, h:h + 1]))
                    for nt in range(NTT):
                        s.wait_ge(c_pe.h, e_proj[nt][FC - 1])
                        for i, (buf, bank) in enumerate([(qsb[0], 0), (qsb[1], 1),
                                                         (ksb[0], 2), (ksb[1], 3)]):
                            c_ac.emit(s.activation(buf[:, nt * 512:(nt + 1) * 512],
                                                   ps[bank][:, :], AF.Silu))
                        for h in range(2):
                            s.wait_ge(c_dv.h, e_evac[nt][h])
                            qsc.emit(s.dma_start(
                                out=k.a2ou_in.ap()[(nt // 2) * 512 + 256 + h * 128:
                                                   (nt // 2) * 512 + 256 + (h + 1) * 128,
                                                   (nt % 2) * 512:(nt % 2 + 1) * 512],
                                in_=ubt[:, (nt % 2) * 1024 + h * 512:(nt % 2) * 1024 + (h + 1) * 512]), 16)

                @b.tensor
                def _(t):
                    t.wait_ge(qgp.h, e_w[3])
                    for nt in range(NTT):
                        if nt >= 1:
                            t.wait_ge(c_ac.h, e_sil[nt - 1][3])
                            t.wait_ge(c_dv.h, e_evac[nt - 1][-1])
                        for fc in range(FC):
                            n = nt * FC + fc
                            t.wait_ge(qsy.h, e_xt[n + 4])
                            xs = xt[:, (n % 24) * 512:(n % 24 + 1) * 512]
                            st, sp = (fc == 0), (fc == FC - 1)
                            base = fc * 1024
                            t.matmul(ps[0][:, :], w_b[:, base:base + 128], xs, start=st, stop=sp)
                            t.matmul(ps[1][:, :], w_b[:, base + 128:base + 256], xs, start=st, stop=sp)
                            t.matmul(ps[2][:, :], w_b[:, base + 256:base + 384], xs, start=st, stop=sp)
                            t.matmul(ps[3][:, :], w_b[:, base + 384:base + 512], xs, start=st, stop=sp)
                            t.matmul(ps[4][:, :], w_b[:, base + 768:base + 896], xs, start=st, stop=sp)
                            t.matmul(ps[5][:, :], w_b[:, base + 896:base + 1024], xs, start=st, stop=sp)
                            for tc in range(2):
                                _g = t.matmul(ps[6 + tc][:, 0:256],
                                              xs[:, tc * 128:(tc + 1) * 128],
                                              w_b[:, base + 512:base + 768], start=st, stop=sp)
                            c_pe.emit(_g)
                        t.wait_ge(c_dv.h, e_evac[nt][3])
                        for fc in range(FC):
                            n = nt * FC + fc
                            xs = xt[:, (n % 24) * 512:(n % 24 + 1) * 512]
                            st, sp = (fc == 0), (fc == FC - 1)
                            base = fc * 1024
                            for tc in range(2, 4):
                                _g = t.matmul(ps[6 + tc - 2][:, 0:256],
                                              xs[:, tc * 128:(tc + 1) * 128],
                                              w_b[:, base + 512:base + 768], start=st, stop=sp)
                            c_pe.emit(_g)


        if getattr(k, "dump_a1tb", False):
            qgp2 = k.ch("qgp")
            ev = qgp2.reserve(12, 16)
            with nc.Block() as b2:
                @b2.gpsimd
                def _(g):
                    g.wait_ge(k.ch("ac").h, e_tbl[-1])
                    g.wait_ge(k.ch("dv").h, e_evac[NTT - 1][-1])
                    g.wait_ge(k.ch("ac").h, e_sil[NTT - 1][3])
                    dd = k.dbg
                    qgp2.emit(g.dma_start(out=dd[0:128, 0:BLK], in_=qdec[0][:, :]), 16)
                    qgp2.emit(g.dma_start(out=dd[128:256, 0:BLK], in_=qdec[1][:, :]), 16)
                    qgp2.emit(g.dma_start(out=dd[256:384, 0:2 * BLK], in_=ddt[0][:, :]), 16)
                    qgp2.emit(g.dma_start(out=dd[384:512, 0:2 * BLK], in_=ddt[1][:, :]), 16)
                    qgp2.emit(g.dma_start(out=dd[512:640, 0:2], in_=kdec[0][:, :],
                                           allow_slow_non_contiguous=True), 16)
                    qgp2.emit(g.dma_start(out=dd[640:768, 0:2], in_=kdec[1][:, :],
                                          allow_slow_non_contiguous=True), 16)
                    qgp2.emit(g.dma_start(out=dd[768:896, 0:1], in_=bdc[0][:, :],
                                          allow_slow_non_contiguous=True), 16)
                    qgp2.emit(g.dma_start(out=dd[896:1024, 0:2], in_=nsc[:, :],
                                          allow_slow_non_contiguous=True), 16)
                    qgp2.emit(g.dma_start(out=dd[1024:1152, 0:512], in_=qsb[1][:, 0:512]), 16)
                    qgp2.emit(g.dma_start(out=dd[1152:1280, 0:512], in_=ksb[1][:, 0:512]), 16)
                    qgp2.emit(g.dma_start(out=dd[1280:1408, 0:1024], in_=v_tm[:, 0:1024]), 16)
                    qgp2.emit(g.dma_start(out=dd[1408:1536, 0:1024], in_=ubt[:, 0:1024]), 16)
                    g.wait_ge(qgp2.h, ev[-1])
            return
        e_scan_pe = {}
        e_scan_dv = {}
        e_odma = {}
        for h in range(2):
            for bb in range(NB):
                e_scan_pe[(h, bb)] = c_pe.reserve(3)
                e_scan_dv[(h, bb)] = c_dv.reserve(5)
        e_scan_tail = c_dv.reserve(1)[0]

        if getattr(k, "dump_qk_pre", False):
            k.barrier()
            with nc.Block() as b:
                @b.gpsimd
                def _(g):
                    qgp2 = k.ch("qgp")
                    ev = qgp2.reserve(6, 16)
                    qgp2.emit(g.dma_start(out=k.dbg[0:128, :], in_=qsb[0][:, 0:1024]), 16)
                    qgp2.emit(g.dma_start(out=k.dbg[128:256, :], in_=qsb[1][:, 0:1024]), 16)
                    qgp2.emit(g.dma_start(out=k.dbg[256:384, :], in_=ksb[0][:, 0:1024]), 16)
                    qgp2.emit(g.dma_start(out=k.dbg[384:512, :], in_=ksb[1][:, 0:1024]), 16)
                    qgp2.emit(g.dma_start(out=k.dbg[512:640, :], in_=v_tm[:, 0:1024]), 16)
                    qgp2.emit(g.dma_start(out=k.dbg[640:768, :], in_=qsb[0][:, 7168:8192]), 16)
                    g.wait_ge(qgp2.h, ev[-1])
            return

        with ExitStack() as pp:
            psO = pp.enter_context(nc.psum_tensor("a1s_O", [128, 512], F32))
            psS = [pp.enter_context(nc.psum_tensor(f"a1s_S{j}", [128, 512], F32)) for j in range(2)]
            psKV = pp.enter_context(nc.psum_tensor("a1s_KV", [128, 512], F32))
            psB = pp.enter_context(nc.psum_tensor("a1s_B", [128, 1024], BF16))
            psF = pp.enter_context(nc.psum_tensor("a1s_F", [128, 512], F32))
            psN = pp.enter_context(nc.psum_tensor("a1s_N", [128, 512], F32))

            first = True
            for h in range(2):
                for bb in range(NB):
                    E = e_scan_dv[(h, bb)]
                    EP = e_scan_pe[(h, bb)]
                    g_ = h * NB + bb
                    e_odma[(h, bb)] = qsy.reserve(1, 16)[0]
                    with nc.Block() as b:
                        @b.vector
                        def _(v, h=h, bb=bb, E=E, EP=EP, g_=g_, first=first):
                            cur, nxt = kv[bb % 2], kv[(bb + 1) % 2]
                            if first:
                                v.wait_ge(c_ac.h, e_sil[NTT - 1][3])
                                v.wait_ge(c_dv.h, e_evac[NTT - 1][-1])
                                v.wait_ge(c_ac.h, e_tbl[-1])
                            if bb == 0:
                                k.vst(v, v.memset(kv[0][:, :], 0.0))
                            v.tensor_mul(qd_t[:, :], qsb[h][:, bb * BLK:(bb + 1) * BLK],
                                         qdec[h][:, :])
                            k.vst(v, v.tensor_copy(kv_b[:, :], cur[:, :]))
                            c_dv.emit(v.tensor_copy(dumm[0:1, 0:4], dumm[0:1, 4:8]))
                            v.wait_ge(c_pe.h, EP[0])
                            v.tensor_mul(pdd[:, 0:BLK], psS[0][:, 0:BLK], ddt[h][:, 0:BLK])
                            v.tensor_mul(pdd[:, BLK:2 * BLK], psS[1][:, 0:BLK],
                                         ddt[h][:, BLK:2 * BLK])
                            import os as _os
                            if _os.environ.get("SCAN_ND", "1") == "1":
                                k.vst(v, v.tensor_copy(tnd[:, :], psN[:, 0:BLK]))
                            else:
                                k.vst(v, v.tensor_scalar(tnd[:, :], k.ones_bf[:, 0:128] if False else psN[:, 0:BLK], 0.0, None, ALU.mult))
                            c_dv.emit(v.tensor_copy(dumm[0:1, 0:4], dumm[0:1, 4:8]))
                            v.wait_ge(c_pe.h, EP[1])
                            v.tensor_single_scalar(ktm[:, 0:128], psB[:, 0:128],
                                                   kdec[h][:, 0:1], ALU.mult)
                            k.vst(v, v.tensor_single_scalar(ktm[:, 128:256], psB[:, 128:256],
                                                            kdec[h][:, 1:2], ALU.mult))
                            c_dv.emit(v.tensor_copy(dumm[0:1, 0:4], dumm[0:1, 4:8]))
                            v.wait_ge(c_pe.h, EP[2])
                            c_dv.emit(v.scalar_tensor_tensor(nxt[:, :], cur[:, :],
                                                             bdc[h][:, :], psKV[:, 0:128],
                                                             ALU.mult, ALU.add))
                            if g_ >= 4:
                                pv = g_ - 4
                                v.wait_ge(qsy.h, e_odma[(pv // NB, pv % NB)])
                            k.vst(v, v.scalar_tensor_tensor(
                                ot[:, (g_ % 4) * BLK:(g_ % 4 + 1) * BLK],
                                psO[:, 0:BLK], 1.0, tnd[:, :], ALU.mult, ALU.add))
                            c_dv.emit(v.tensor_copy(dumm[0:1, 0:4], dumm[0:1, 4:8]))

                        @b.tensor
                        def _(t, h=h, bb=bb, E=E, EP=EP, first=first):
                            if first:
                                t.wait_ge(c_ac.h, e_sil[NTT - 1][3])
                                t.wait_ge(c_dv.h, e_evac[NTT - 1][-1])

                            def flush(ev):
                                mm = t.matmul(psF[0:16, 0:128], k.ones_bf[:, 0:16],
                                              k.ones_bf[:, :], start=True, stop=True)
                                if ev:
                                    c_pe.emit(mm)
                            t.wait_ge(c_dv.h, E[0])
                            t.matmul(psS[0][:, 0:BLK], ksb[h][:, (2 * bb) * 128:(2 * bb + 1) * 128],
                                     qsb[h][:, bb * BLK:(bb + 1) * BLK], start=True, stop=True)
                            t.matmul(psS[1][:, 0:BLK],
                                     ksb[h][:, (2 * bb + 1) * 128:(2 * bb + 2) * 128],
                                     qsb[h][:, bb * BLK:(bb + 1) * BLK],
                                     start=True, stop=True)
                            import os as _os
                            if _os.environ.get("SCAN_ND", "1") == "1":
                                t.matmul(psN[:, 0:BLK], kv_b[:, :], qd_t[:, :],
                                         start=True, stop=True)
                            else:
                                t.matmul(psN[:, 0:BLK], k.ones_bf[:, :], qd_t[:, :],
                                         start=True, stop=True)
                            flush(True)
                            t.wait_ge(c_dv.h, E[1])
                            t.matmul(psO[:, 0:BLK], v_tm[:, (2 * bb) * 256 + h * 128:
                                                         (2 * bb) * 256 + (h + 1) * 128],
                                     pdd[:, 0:BLK], start=True, stop=False)
                            t.matmul(psO[:, 0:BLK],
                                     v_tm[:, (2 * bb + 1) * 256 + h * 128:
                                          (2 * bb + 1) * 256 + (h + 1) * 128],
                                     pdd[:, BLK:2 * BLK], start=False, stop=True)
                            t.transpose(psB[:, 0:128], ksb[h][:, (2 * bb) * 128:(2 * bb + 1) * 128],
                                        k.ident_b[:, :])
                            t.transpose(psB[:, 128:256],
                                        ksb[h][:, (2 * bb + 1) * 128:(2 * bb + 2) * 128],
                                        k.ident_b[:, :])
                            flush(True)
                            t.wait_ge(c_dv.h, E[2])
                            t.matmul(psKV[:, 0:128], ktm[:, 0:128],
                                     v_tm[:, (2 * bb) * 256 + h * 128:(2 * bb) * 256 + (h + 1) * 128],
                                     start=True, stop=False)
                            t.matmul(psKV[:, 0:128], ktm[:, 128:256],
                                     v_tm[:, (2 * bb + 1) * 256 + h * 128:
                                          (2 * bb + 1) * 256 + (h + 1) * 128],
                                     start=False, stop=True)
                            flush(True)

                        @b.sync
                        def _(sy, h=h, bb=bb, E=E, g_=g_):
                            sy.wait_ge(c_dv.h, E[4])
                            qsy.emit(sy.dma_start(
                                out=k.a2ou_in.ap()[(bb // 4) * 512 + h * 128:
                                                   (bb // 4) * 512 + (h + 1) * 128,
                                                   (bb % 4) * 256:(bb % 4 + 1) * 256],
                                in_=ot[:, (g_ % 4) * BLK:(g_ % 4 + 1) * BLK]), 16)
                    first = False
            with nc.Block() as b:
                @b.vector
                def _(v):
                    c_dv.emit(v.tensor_copy(dumm[0:1, 0:4], dumm[0:1, 4:8]))
            if getattr(k, "dump_a1qk", False):
                k.barrier()
                with nc.Block() as b:
                    @b.gpsimd
                    def _(g):
                        qgp2 = k.ch("qgp")
                        ev = qgp2.reserve(6, 16)
                        qgp2.emit(g.dma_start(out=k.dbg[0:128, :], in_=qsb[0][:, 0:1024]), 16)
                        qgp2.emit(g.dma_start(out=k.dbg[128:256, :], in_=qsb[1][:, 0:1024]), 16)
                        qgp2.emit(g.dma_start(out=k.dbg[256:384, :], in_=ksb[0][:, 0:1024]), 16)
                        qgp2.emit(g.dma_start(out=k.dbg[384:512, :], in_=ksb[1][:, 0:1024]), 16)
                        qgp2.emit(g.dma_start(out=k.dbg[512:640, :], in_=v_tm[:, 0:1024]), 16)
                        qgp2.emit(g.dma_start(out=k.dbg[640:768, 0:512], in_=pdd[:, 0:512]), 16)
                        g.wait_ge(qgp2.h, ev[-1])


def emit_attn1_out(k: K):
    nc = k.nc
    qsy, qgp = k.ch("qsy"), k.ch("qgp")
    c_pe, c_ac, c_dv = k.ch("pe"), k.ch("ac"), k.ch("dv")
    e_restore = emit_restore_res(k)
    with ExitStack() as ph:
        os_ = ph.enter_context(nc.sbuf_tensor("a1o_os", [128, FC * TPC], BF16))
        us_ = ph.enter_context(nc.sbuf_tensor("a1o_us", [128, FC * TPC], BF16))
        sqb = ph.enter_context(nc.sbuf_tensor("a1o_sqb", [128, 2 * TPC], BF16))
        rstd = ph.enter_context(nc.sbuf_tensor("a1o_rstd", [128, TPC], F32))
        tmp = ph.enter_context(nc.sbuf_tensor("a1o_tmp", [128, TPC], F32))
        wo_b = ph.enter_context(nc.sbuf_tensor("a1o_wo", [128, 2 * D], BF16))
        psS = ph.enter_context(nc.psum_tensor("a1o_psS", [128, TPC], F32))
        psA = [ph.enter_context(nc.psum_tensor(f"a1o_ps{i}", [128, 512], F32)) for i in range(2)]

        e_ou = qsy.reserve(2 * FC, 16)
        e_sq = c_ac.reserve(FC)
        e_rst = c_ac.reserve(1)[0]
        e_smm = c_pe.reserve(FC)
        e_xg = c_dv.reserve(FC)
        e_wo = qgp.reserve(FC, 16)
        e_mm = c_pe.reserve(FC * 2)
        e_add = c_dv.reserve(FC * 2)
        with nc.Block() as b:
            @b.sync
            def _(sy):
                sy.wait_ge(k.cc["a2ou"].h, 1)
                for fc in range(FC):
                    qsy.emit(sy.dma_start(
                        out=os_[:, fc * TPC:(fc + 1) * TPC],
                        in_=k.a2ou_out.ap()[(fc // 2) * 512 + (fc % 2) * 128:
                                            (fc // 2) * 512 + (fc % 2) * 128 + 128, :]), 16)
                for fc in range(FC):
                    qsy.emit(sy.dma_start(
                        out=us_[:, fc * TPC:(fc + 1) * TPC],
                        in_=k.a2ou_out.ap()[(fc // 2) * 512 + 256 + (fc % 2) * 128:
                                            (fc // 2) * 512 + 256 + (fc % 2) * 128 + 128, :]), 16)

            @b.scalar
            def _(s):
                for fc in range(FC):
                    s.wait_ge(qsy.h, e_ou[fc])
                    if fc >= 2:
                        s.wait_ge(c_pe.h, e_smm[fc - 2])
                    c_ac.emit(s.activation(sqb[:, (fc % 2) * TPC:(fc % 2 + 1) * TPC],
                                           os_[:, fc * TPC:(fc + 1) * TPC], AF.Square))
                s.wait_ge(c_pe.h, e_smm[-1])
                c_ac.emit(s.activation(rstd[:, :], psS[:, :], AF.Sqrt,
                                       scale=k.cst[:, 0:1], bias=k.cst[:, 4:5]))

            @b.tensor
            def _(t):
                for fc in range(FC):
                    t.wait_ge(c_ac.h, e_sq[fc])
                    st, sp = (fc == 0), (fc == FC - 1)
                    for nh in range(2):
                        sl = slice(nh * 512, (nh + 1) * 512)
                        _g = t.matmul(psS[:, sl], k.ones_bf[:, :],
                                 sqb[:, (fc % 2) * TPC:(fc % 2 + 1) * TPC][:, sl],
                                 start=st, stop=sp)
                    c_pe.emit(_g)
                t.wait_ge(c_dv.h, e_xg[-1])
                n = 0
                for oc in range(FC):
                    t.wait_ge(qgp.h, e_wo[oc])
                    for th in range(2):
                        if n >= 2:
                            t.wait_ge(c_dv.h, e_add[n - 2])
                        for kc in range(FC):
                            _g = t.matmul(psA[n % 2][:, :],
                                     wo_b[:, (oc % 2) * D + kc * 128:(oc % 2) * D + (kc + 1) * 128],
                                     os_[:, kc * TPC + th * 512: kc * TPC + (th + 1) * 512],
                                     start=(kc == 0), stop=(kc == FC - 1))
                        c_pe.emit(_g)
                        n += 1

            @b.vector
            def _(v):
                v.wait_ge(c_ac.h, e_rst)
                k.vst(v, v.reciprocal(rstd[:, :], rstd[:, :]))
                for fc in range(FC):
                    v.wait_ge(qsy.h, e_ou[FC + fc])
                    v.tensor_mul(tmp[:, :], os_[:, fc * TPC:(fc + 1) * TPC], rstd[:, :])
                    c_dv.emit(v.tensor_mul(os_[:, fc * TPC:(fc + 1) * TPC], tmp[:, :],
                                           us_[:, fc * TPC:(fc + 1) * TPC]))
                v.wait_ge(qsy.h, e_restore)
                n = 0
                for oc in range(FC):
                    for th in range(2):
                        v.wait_ge(c_pe.h, e_mm[n])
                        sl = slice(th * 512, (th + 1) * 512)
                        c_dv.emit(v.tensor_tensor(k.rch(oc, sl), psA[n % 2][:, :],
                                                  k.rch(oc, sl), ALU.add))
                        n += 1

            @b.gpsimd
            def _(g):
                for oc in range(FC):
                    if oc >= 2:
                        g.wait_ge(c_pe.h, e_mm[2 * (oc - 1) - 1])
                    qgp.emit(g.dma_start(out=wo_b[:, (oc % 2) * D:(oc % 2 + 1) * D],
                                         in_=k.in_wout[oc, :, :]), 16)


def emit_ffn1(k: K):
    nc = k.nc
    qsy, qgp = k.ch("qsy"), k.ch("qgp")
    c_pe, c_ac, c_dv = k.ch("pe"), k.ch("ac"), k.ch("dv")
    with ExitStack() as ph:
        sbf = lambda n, s, d: ph.enter_context(nc.sbuf_tensor(n, s, d))
        qg_b = sbf("f1_qg", [128, 2 * 2048], BF16)
        qu_b = sbf("f1_qu", [128, 2 * 2048], BF16)
        qd_b = sbf("f1_qd", [128, 2 * HC1 * 128], BF16)
        h_sb = sbf("f1_h", [128, HC1 * 512], BF16)
        sg_s = sbf("f1_sg", [128, HC1], F32)
        su_s = sbf("f1_su", [128, HC1], F32)
        sd_s = sbf("f1_sd", [128, FC], F32)
        xc = sbf("f1_xc", [128, 512], F32)
        hc = sbf("f1_hc", [128, 512], F32)
        gl = sbf("f1_gl", [128, 2 * 512], F32)
        ul = sbf("f1_ul", [128, 2 * 512], F32)
        gsb = sbf("f1_gs", [128, 2 * 512], BF16)
        dl = sbf("f1_dl", [128, 2 * 512], F32)
        psX = ph.enter_context(nc.psum_tensor("f1_psX", [128, 512], F32))
        psG = [ph.enter_context(nc.psum_tensor(f"f1_psG{i}", [128, 512], F32)) for i in range(2)]
        psU = [ph.enter_context(nc.psum_tensor(f"f1_psU{i}", [128, 512], F32)) for i in range(2)]
        psH = ph.enter_context(nc.psum_tensor("f1_psH", [128, 512], F32))
        psD = [ph.enter_context(nc.psum_tensor(f"f1_psD{i}", [128, 512], F32)) for i in range(2)]

        e_s = qsy.reserve(3, 16)
        e_gu, e_qd = {}, {}
        e_xmm, e_gmm, e_umm, e_hmm, e_dmm = {}, {}, {}, {}, {}
        e_xc, e_gli, e_uli, e_hh, e_hc, e_dli = {}, {}, {}, {}, {}, {}
        e_gs = {}
        for th in range(2):
            for mc in range(HC1):
                e_gu[(th, mc)] = qgp.reserve(2, 16)
            for oc in range(FC):
                e_qd[(th, oc)] = qgp.reserve(1, 16)[0]
            e_xmm[th] = c_pe.reserve(1)[0]
            for mc in range(HC1):
                e_gmm[(th, mc)] = c_pe.reserve(1)[0]
                e_umm[(th, mc)] = c_pe.reserve(1)[0]
            e_hmm[th] = c_pe.reserve(1)[0]
            for oc in range(FC):
                e_dmm[(th, oc)] = c_pe.reserve(1)[0]
            e_xc[th] = c_dv.reserve(1)[0]
            for mc in range(HC1):
                e_gli[(th, mc)] = c_dv.reserve(1)[0]
                e_uli[(th, mc)] = c_dv.reserve(1)[0]
                e_hh[(th, mc)] = c_dv.reserve(1)[0]
            e_hc[th] = c_dv.reserve(1)[0]
            for oc in range(FC):
                e_dli[(th, oc)] = c_dv.reserve(2)
            for mc in range(HC1):
                e_gs[(th, mc)] = c_ac.reserve(1)[0]

        with nc.Block() as b:
            @b.sync
            def _(sy):
                qsy.emit(sy.dma_start(out=sg_s[:, :], in_=k.in_sg.ap()), 16)
                qsy.emit(sy.dma_start(out=su_s[:, :], in_=k.in_su.ap()), 16)
                qsy.emit(sy.dma_start(out=sd_s[:, :], in_=k.in_sd.ap()), 16)

            @b.gpsimd
            def _(g):
                for th in range(2):
                    for mc in range(HC1):
                        gidx = th * HC1 + mc
                        if gidx >= 2:
                            pth, pmc = (gidx - 2) // HC1, (gidx - 2) % HC1
                            g.wait_ge(c_pe.h, e_umm[(pth, pmc)])
                        qgp.emit(g.dma_start(out=qg_b[:, (mc % 2) * 2048:(mc % 2 + 1) * 2048],
                                             in_=k.in_qg[mc, :, :]), 16)
                        qgp.emit(g.dma_start(out=qu_b[:, (mc % 2) * 2048:(mc % 2 + 1) * 2048],
                                             in_=k.in_qu[mc, :, :]), 16)
                    for oc in range(FC):
                        gidx = th * FC + oc
                        if gidx >= 2:
                            pth, poc = (gidx - 2) // FC, (gidx - 2) % FC
                            g.wait_ge(c_pe.h, e_dmm[(pth, poc)])
                        qgp.emit(g.dma_start(out=qd_b[:, (oc % 2) * HC1 * 128:(oc % 2 + 1) * HC1 * 128],
                                             in_=k.in_qd[oc, :, :]), 16)

            @b.tensor
            def _(t):
                for th in range(2):
                    tsl = slice(th * 512, (th + 1) * 512)
                    if th >= 1:
                        t.wait_ge(c_dv.h, e_xc[th - 1])
                    for fc in range(FC):
                        _g = t.matmul(psX[:, :], k.ones_bf[:, :],
                                 k.xln[:, fc * TPC + th * 512: fc * TPC + (th + 1) * 512],
                                 start=(fc == 0), stop=(fc == FC - 1))
                    c_pe.emit(_g)
                    for mc in range(HC1):
                        t.wait_ge(qgp.h, e_gu[(th, mc)][1])
                        gidx = th * HC1 + mc
                        if gidx >= 2:
                            pth, pmc = (gidx - 2) // HC1, (gidx - 2) % HC1
                            t.wait_ge(c_dv.h, e_gli[(pth, pmc)])
                        for kc in range(FC):
                            _g = t.matmul(psG[mc % 2][:, :],
                                     qg_b[:, (mc % 2) * 2048 + kc * 128:(mc % 2) * 2048 + (kc + 1) * 128],
                                     k.xln[:, kc * TPC + th * 512: kc * TPC + (th + 1) * 512],
                                     start=(kc == 0), stop=(kc == FC - 1))
                        c_pe.emit(_g)
                        if gidx >= 2:
                            pth, pmc = (gidx - 2) // HC1, (gidx - 2) % HC1
                            t.wait_ge(c_dv.h, e_uli[(pth, pmc)])
                        for kc in range(FC):
                            _g = t.matmul(psU[mc % 2][:, :],
                                     qu_b[:, (mc % 2) * 2048 + kc * 128:(mc % 2) * 2048 + (kc + 1) * 128],
                                     k.xln[:, kc * TPC + th * 512: kc * TPC + (th + 1) * 512],
                                     start=(kc == 0), stop=(kc == FC - 1))
                        c_pe.emit(_g)
                    t.wait_ge(c_dv.h, e_hh[(th, HC1 - 1)])
                    if th >= 1:
                        t.wait_ge(c_dv.h, e_hc[th - 1])
                    for mc in range(HC1):
                        _g = t.matmul(psH[:, :], k.ones_bf[:, :],
                                 h_sb[:, mc * 512:(mc + 1) * 512],
                                 start=(mc == 0), stop=(mc == HC1 - 1))
                    c_pe.emit(_g)
                    for oc in range(FC):
                        t.wait_ge(qgp.h, e_qd[(th, oc)])
                        gidx = th * FC + oc
                        if gidx >= 2:
                            pth, poc = (gidx - 2) // FC, (gidx - 2) % FC
                            t.wait_ge(c_dv.h, e_dli[(pth, poc)][0])
                        for kc in range(HC1):
                            _g = t.matmul(psD[oc % 2][:, :],
                                     qd_b[:, (oc % 2) * HC1 * 128 + kc * 128:
                                          (oc % 2) * HC1 * 128 + (kc + 1) * 128],
                                     h_sb[:, kc * 512:(kc + 1) * 512],
                                     start=(kc == 0), stop=(kc == HC1 - 1))
                        c_pe.emit(_g)

            @b.scalar
            def _(s):
                s.wait_ge(qsy.h, e_s[2])
                for th in range(2):
                    for mc in range(HC1):
                        s.wait_ge(c_dv.h, e_gli[(th, mc)])
                        gidx = th * HC1 + mc
                        if gidx >= 2:
                            pth, pmc = (gidx - 2) // HC1, (gidx - 2) % HC1
                            s.wait_ge(c_dv.h, e_hh[(pth, pmc)])
                        c_ac.emit(s.activation(gsb[:, (mc % 2) * 512:(mc % 2 + 1) * 512],
                                               gl[:, (mc % 2) * 512:(mc % 2 + 1) * 512],
                                               AF.Silu, scale=sg_s[:, mc:mc + 1]))

            @b.vector
            def _(v):
                for th in range(2):
                    v.wait_ge(c_pe.h, e_xmm[th])
                    c_dv.emit(v.tensor_scalar(xc[:, :], psX[:, :], -128.0, None, ALU.mult))
                    for mc in range(HC1):
                        msl = slice((mc % 2) * 512, (mc % 2 + 1) * 512)
                        gidx = th * HC1 + mc
                        if gidx >= 2:
                            pth, pmc = (gidx - 2) // HC1, (gidx - 2) % HC1
                            v.wait_ge(c_ac.h, e_gs[(pth, pmc)])
                        v.wait_ge(c_pe.h, e_gmm[(th, mc)])
                        c_dv.emit(v.tensor_tensor(gl[:, msl], psG[mc % 2][:, :], xc[:, :], ALU.add))
                        v.wait_ge(c_pe.h, e_umm[(th, mc)])
                        c_dv.emit(v.tensor_tensor(ul[:, msl], psU[mc % 2][:, :], xc[:, :], ALU.add))
                        v.wait_ge(c_ac.h, e_gs[(th, mc)])
                        c_dv.emit(v.scalar_tensor_tensor(h_sb[:, mc * 512:(mc + 1) * 512],
                                                         ul[:, msl], su_s[:, mc:mc + 1],
                                                         gsb[:, msl], ALU.mult, ALU.mult))
                    v.wait_ge(c_pe.h, e_hmm[th])
                    c_dv.emit(v.tensor_scalar(hc[:, :], psH[:, :], -128.0, None, ALU.mult))
                    for oc in range(FC):
                        osl = slice((oc % 2) * 512, (oc % 2 + 1) * 512)
                        v.wait_ge(c_pe.h, e_dmm[(th, oc)])
                        c_dv.emit(v.tensor_tensor(dl[:, osl], psD[oc % 2][:, :], hc[:, :], ALU.add))
                        c_dv.emit(v.scalar_tensor_tensor(
                            k.rch(oc, slice(th * 512, (th + 1) * 512)),
                            dl[:, osl], sd_s[:, oc:oc + 1],
                            k.rch(oc, slice(th * 512, (th + 1) * 512)),
                            ALU.mult, ALU.add))


def emit_out(k: K):
    nc = k.nc
    qsy = k.ch("qsy")
    c_pe, c_dv = k.ch("pe"), k.ch("dv")
    with ExitStack() as ph:
        orow = ph.enter_context(nc.sbuf_tensor("orow", [128, 2 * D], F32))
        psT = [ph.enter_context(nc.psum_tensor(f"o_psT{i}", [128, 512], F32)) for i in range(4)]
        NTOT = TCH * FC
        e_tr = c_pe.reserve(NTOT)
        e_cp = c_dv.reserve(NTOT)
        e_st = qsy.reserve(TCH, 16)
        with nc.Block() as b:
            @b.tensor
            def _(t):
                n = 0
                for tc in range(TCH):
                    for fc in range(FC):
                        if n >= 4:
                            t.wait_ge(c_dv.h, e_cp[n - 4])
                        _g = t.transpose(psT[n % 4][:, 0:128],
                                              k.rch(fc, slice(tc * 128, (tc + 1) * 128)),
                                              k.ident_f[:, :])
                        c_pe.emit(_g)
                        n += 1

            @b.vector
            def _(v):
                n = 0
                for tc in range(TCH):
                    if tc >= 2:
                        v.wait_ge(qsy.h, e_st[tc - 2])
                    for fc in range(FC):
                        v.wait_ge(c_pe.h, e_tr[n])
                        c_dv.emit(v.tensor_copy(
                            orow[:, (tc % 2) * D + fc * 128:(tc % 2) * D + (fc + 1) * 128],
                            psT[n % 4][:, 0:128]))
                        n += 1

            @b.sync
            def _(sy):
                for tc in range(TCH):
                    sy.wait_ge(c_dv.h, e_cp[(tc + 1) * FC - 1])
                    qsy.emit(sy.dma_start(out=k.out[tc * 128:(tc + 1) * 128, :],
                                          in_=orow[:, (tc % 2) * D:(tc % 2 + 1) * D]), 16)
                sy.wait_ge(qsy.h, e_st[-1])



def build_program(stop_after="out", debug=False):
    k = K(debug=debug)
    nc = k.nc
    dump = None

    def stop(name, dump_src=None, cast=False):
        nonlocal dump
        if debug and stop_after == name:
            k.barrier()
            if dump_src is not None:
                emit_dump(k, dump_src, cast=cast)
            return True
        return False

    emit_init(k)

    stk_a = ExitStack()
    k.res = stk_a.enter_context(nc.sbuf_tensor("res_a", [128, FC * TPC], F32))
    k.xln = stk_a.enter_context(nc.sbuf_tensor("xln_a", [128, FC * TPC], BF16))
    emit_load_x(k)
    k.barrier()
    if stop("load", lambda fc: k.rch(fc)):
        k.check_channels()
        return k
    emit_ln(k, "ln0", ag=None if stop_after in ("ln0noag", "ln0nodump") else (k.ag0_in, "ag0", k.ag0_out))
    k.barrier()
    if stop_after == "ln0noag":
        if debug:
            emit_dump(k, lambda fc: k.xch(fc), cast=True)
        k.check_channels()
        return k
    if stop_after == "ln0nodump":
        if debug:
            emit_dump(k, lambda fc: k.rch(fc), cast=False)
        k.check_channels()
        return k
    emit_collective(k, "AllGather", k.ag0_in.ap(), k.ag0_out.ap(), "ag0")
    k.barrier()
    if stop("ln0", lambda fc: k.xch(fc), cast=True):
        k.check_channels()
        return k
    emit_park_res(k)
    k.barrier()
    stk_a.close()
    k.res = k.xln = None

    emit_attn0(k)
    k.barrier()
    emit_collective(k, "AllToAll", k.a2y_in.ap(), k.a2y_out.ap(), "a2y")
    k.barrier()
    emit_collective(k, "AllGather", k.agw_in.ap(), k.agw_out.ap(), "agw")
    k.barrier()

    stk_b = ExitStack()
    k.res = stk_b.enter_context(nc.sbuf_tensor("res_b", [128, FC * TPC], F32))
    k.xln = stk_b.enter_context(nc.sbuf_tensor("xln_b", [128, FC * TPC], BF16))
    wt32 = stk_b.enter_context(nc.sbuf_tensor("wt32", [NE, TPC], F32))
    emit_attn0_out(k)
    k.barrier()
    if stop("res1", lambda fc: k.rch(fc)):
        k.check_channels()
        return k
    emit_ln(k, "ln1")
    k.barrier()
    emit_router(k, wt32)
    k.barrier()
    emit_moe(k, wt32)
    k.barrier()
    if stop("moe", lambda fc: k.rch(fc)):
        k.check_channels()
        return k
    emit_ln(k, "ln2", ag=(k.ag2_in, "ag2", k.ag2_out))
    k.barrier()
    emit_collective(k, "AllGather", k.ag2_in.ap(), k.ag2_out.ap(), "ag2")
    k.barrier()
    emit_park_res(k)
    k.barrier()
    stk_b.close()
    k.res = k.xln = None

    if debug and stop_after == "a1tb":
        k.dump_a1tb = True
        emit_attn1(k)
        return k
    if debug and stop_after == "a1qk2":
        k.dump_qk_pre = True
        emit_attn1(k)
        return k
    if debug and stop_after == "a1qk":
        k.dump_a1qk = True
        emit_attn1(k)
        k.check_channels()
        return k
    emit_attn1(k)
    k.barrier()
    if debug and stop_after == "a1pre":
        with k.nc.Block() as _b:
            @_b.gpsimd
            def _(g):
                qgp = k.ch("qgp")
                ev = qgp.reserve(4, 16)
                for j in range(4):
                    qgp.emit(g.dma_start(
                        out=k.dbg[j * 512:(j + 1) * 512, :],
                        in_=k.a2ou_in.ap()[j * 512:(j + 1) * 512, :]), 16)
                g.wait_ge(qgp.h, ev[-1])
        k.check_channels()
        return k
    emit_collective(k, "AllToAll", k.a2ou_in.ap(), k.a2ou_out.ap(), "a2ou")
    k.barrier()
    if debug and stop_after == "a1in":
        with k.nc.Block() as _b:
            @_b.gpsimd
            def _(g):
                qgp = k.ch("qgp")
                ev = qgp.reserve(4, 16)
                for j in range(4):
                    qgp.emit(g.dma_start(
                        out=k.dbg[j * 512:(j + 1) * 512, :],
                        in_=k.a2ou_in.ap()[j * 512:(j + 1) * 512, :]), 16)
                g.wait_ge(qgp.h, ev[-1])
        k.check_channels()
        return k
    if debug and stop_after in ("a1o", "a1u"):
        off = 0 if stop_after == "a1o" else 256
        with k.nc.Block() as _b:
            @_b.gpsimd
            def _(g):
                qgp = k.ch("qgp")
                ev = qgp.reserve(FC, 16)
                for fc in range(FC):
                    qgp.emit(g.dma_start(
                        out=k.dbg[fc * 128:(fc + 1) * 128, :],
                        in_=k.a2ou_out.ap()[(fc // 2) * 512 + off + (fc % 2) * 128:
                                            (fc // 2) * 512 + off + (fc % 2) * 128 + 128, :]), 16)
                g.wait_ge(qgp.h, ev[-1])
        k.check_channels()
        return k

    stk_c = ExitStack()
    k.res = stk_c.enter_context(nc.sbuf_tensor("res_c", [128, FC * TPC], F32))
    k.xln = stk_c.enter_context(nc.sbuf_tensor("xln_c", [128, FC * TPC], BF16))
    emit_attn1_out(k)
    k.barrier()
    if stop("res3", lambda fc: k.rch(fc)):
        k.check_channels()
        return k
    emit_ln(k, "ln3")
    k.barrier()
    emit_ffn1(k)
    k.barrier()
    if stop("ffn1", lambda fc: k.rch(fc)):
        k.check_channels()
        return k
    emit_out(k)
    stk_c.close()
    k.check_channels()
    return k


def kernel(**kw):
    in_maps = make_host_inputs(kw)
    k = build_program(stop_after="out", debug=False)
    res = run_bass_kernel_spmd(k.nc, in_maps, core_ids=list(range(NCORE)))
    out = np.concatenate([res.results[c]["out"] for c in range(NCORE)], axis=0)
    return out.reshape(1, TOK, D).astype(np.float32)
